# version 21
# baseline (speedup 1.0000x reference)
import contextlib
import os

import numpy as np
import ml_dtypes
import concourse.bass as bass
import concourse.mybir as mybir
from concourse.bass_utils import run_bass_kernel_spmd

F32 = mybir.dt.float32
BF16 = mybir.dt.bfloat16
AF = mybir.ActivationFunctionType
ALU = mybir.AluOpType
AXX = mybir.AxisListType.X

B, S, D = 4, 8192, 1024
H, DH = 16, 64
TOK = 4096
LN_EPS = 1e-5
CUT = -8.73365498
C0_ROUTE = 0.20794415416798357
BF = ml_dtypes.bfloat16
ENGINES = ("sync", "gpsimd", "scalar", "vector", "tensor")


class Prog:

    def __init__(self, nc, name):
        self.nc = nc
        self.name = name
        self.steps = []
        self.last_w = {}
        self.readers = {}
        self.eng_seq = {}

    def step(self, eng, emit, reads=(), writes=(), inc=1):
        deps = set()
        for r in reads:
            if r in self.last_w:
                deps.add(self.last_w[r])
        for w in writes:
            if w in self.last_w:
                deps.add(self.last_w[w])
            for rd in self.readers.get(w, ()):
                deps.add(rd)
        i = len(self.steps)
        kind = "d" if inc == 16 else "c"
        key = (eng, kind)
        self.eng_seq[key] = self.eng_seq.get(key, 0) + inc
        self.steps.append(dict(i=i, eng=eng, emit=emit, deps=deps, key=key,
                               seq=self.eng_seq[key], inc=inc))
        for r in reads:
            self.readers.setdefault(r, []).append(i)
        for w in writes:
            self.last_w[w] = i
            self.readers[w] = []
        return i

    def emit(self, stack):
        used_k = sorted({s["key"] for s in self.steps})
        used_e = [e for e in ENGINES if any(s["eng"] == e for s in self.steps)]
        sems = {k: stack.enter_context(self.nc.semaphore(f"{self.name}_{k[0]}_{k[1]}"))
                for k in used_k}
        blk = stack.enter_context(self.nc.Block())
        steps_all = self.steps

        chain_sems = {e: stack.enter_context(self.nc.semaphore(f"{self.name}_{e}_ch"))
                      for e in used_e if e in ("scalar", "vector", "gpsimd")}

        class _Chained:

            _NOCHAIN = {"wait_ge", "dma_start", "indirect_dma_start",
                        "collective_compute", "sem_inc", "nop"}

            def __init__(self, eng, sem, ctr):
                self._eng = eng
                self._sem = sem
                self._ctr = ctr

            def __getattr__(self, nm):
                fn = getattr(self._eng, nm)
                if nm in self._NOCHAIN or not callable(fn):
                    return fn

                def call(*a, **kw):
                    instr = fn(*a, **kw)
                    try:
                        instr.then_inc(self._sem, 1)
                    except Exception:
                        return instr
                    self._ctr[0] += 1
                    self._eng.wait_ge(self._sem, self._ctr[0])
                    return instr
                return call

        def make_body(e):
            my = [s for s in steps_all if s["eng"] == e]

            def body(eng):
                hw = {k: 0 for k in used_k}
                ctr = [0]
                wrapped = _Chained(eng, chain_sems[e], ctr) if e in chain_sems else eng
                for s in my:
                    need = {}
                    for d in s["deps"]:
                        ds = steps_all[d]
                        if ds["eng"] == e:
                            continue
                        k = ds["key"]
                        need[k] = max(need.get(k, 0), ds["seq"])
                    for k, val in sorted(need.items()):
                        if val > hw[k]:
                            eng.wait_ge(sems[k], val)
                            hw[k] = val
                    if s["inc"] == 16:
                        s["emit"](eng).then_inc(sems[s["key"]], 16)
                        eng.wait_ge(sems[s["key"]], s["seq"])
                    elif e in chain_sems:
                        s["emit"](wrapped)
                        eng.sem_inc(sems[s["key"]], 1)
                    else:
                        s["emit"](eng).then_inc(sems[s["key"]], s["inc"])
            return body

        for e in used_e:
            getattr(blk, e)(make_body(e))


def _repl(e, dst, row):
    w = row.shape[0]
    return e.dma_start(out=dst[:, 0:w], in_=row.unsqueeze(0).broadcast_to([dst.shape[0], w]))


def _last(*instrs):
    return instrs[-1]


P1_EXTRA = None


def build_p1():
    nc = bass.Bass("TRN2", target_bir_lowering=False)
    io = {}
    for nm, shape in (("xo", [TOK, D]), ("xp", [TOK, D]), ("wqT", [D, D]), ("wkT", [D, D]),
                      ("wvT", [D, D]), ("woT", [D, D]), ("lng", [D]), ("lnb", [D]),
                      ("qg", [DH]), ("kg", [DH]), ("bo", [D]), ("rtok", [D]),
                      ("identf", [128, 128]), ("sel2c", [2, 128]), ("b2c", [128, 2])):
        io[nm] = nc.dram_tensor(nm, shape, F32, kind="ExternalInput").ap()
    io["x1o"] = nc.dram_tensor("x1o", [TOK, D], F32, kind="ExternalOutput").ap()
    io["so"] = nc.dram_tensor("so", [TOK], F32, kind="ExternalOutput").ap()
    emit_p1(nc, io)
    return nc


def emit_p1(nc, io):
    xo, xp, wqT, wkT, wvT, woT = io["xo"], io["xp"], io["wqT"], io["wkT"], io["wvT"], io["woT"]
    lng, lnb, qg, kg, bo, rtok = io["lng"], io["lnb"], io["qg"], io["kg"], io["bo"], io["rtok"]
    identf, sel2c, b2c = io["identf"], io["sel2c"], io["b2c"]
    x1o, so = io["x1o"], io["so"]
    attn_dr = nc.dram_tensor("attn_dr1", [8, 128, TOK], BF16).ap()
    xn_dr = nc.dram_tensor("xn_dr1", [8, 128, S], BF16).ap()
    rec_dr = nc.dram_tensor("rec_dr1", [2, 1, 1024], F32).ap()
    stack = contextlib.ExitStack()
    with stack:
        sbt = lambda *a: stack.enter_context(nc.sbuf_tensor(*a))
        gfull = sbt("gfull", [128, D], F32)
        bfull = sbt("bfull", [128, D], F32)
        bofull = sbt("bofull", [128, D], F32)
        rtfull = sbt("rtfull", [128, D], F32)
        idf = sbt("idf", [128, 128], F32)
        idb = sbt("idb", [128, 128], BF16)
        qg2 = sbt("qg2", [128, 1], F32)
        kg2 = sbt("kg2", [128, 1], F32)
        negM = sbt("negM", [128, 1], F32)
        onesf = sbt("onesf", [1, 128], F32)
        b2ones = sbt("b2ones", [128, 2], BF16)
        sel2 = sbt("sel2", [2, 128], F32)

        with contextlib.ExitStack() as st0:
            gk_row = st0.enter_context(nc.sbuf_tensor("gk_row", [1, 2 * DH], F32))
            mtmp = st0.enter_context(nc.sbuf_tensor("mtmp", [1, 2], F32))
            pbc = st0.enter_context(nc.psum_tensor("pbc", [128, 1], F32))
            p = Prog(nc, "cst")
            p.step("sync", lambda e: _repl(e, gfull, lng), writes=("g",), inc=16)
            p.step("sync", lambda e: _repl(e, bfull, lnb), writes=("b",), inc=16)
            p.step("sync", lambda e: _repl(e, bofull, bo), writes=("bo",), inc=16)
            p.step("sync", lambda e: _repl(e, rtfull, rtok), writes=("rt",), inc=16)
            p.step("sync", lambda e: e.dma_start(out=idf[:], in_=identf), writes=("idf",), inc=16)
            p.step("sync", lambda e: e.dma_start(out=qg2[0:64, 0], in_=qg), writes=("q1",), inc=16)
            p.step("sync", lambda e: e.dma_start(out=qg2[64:128, 0], in_=qg), writes=("q2",), inc=16)
            p.step("sync", lambda e: e.dma_start(out=kg2[0:64, 0], in_=kg), writes=("k1",), inc=16)
            p.step("sync", lambda e: e.dma_start(out=kg2[64:128, 0], in_=kg), writes=("k2",), inc=16)
            p.step("sync", lambda e: e.dma_start(out=gk_row[0:1, 0:DH], in_=qg.unsqueeze(0)), writes=("g1",), inc=16)
            p.step("sync", lambda e: e.dma_start(out=gk_row[0:1, DH:2 * DH], in_=kg.unsqueeze(0)), writes=("g2",), inc=16)
            p.step("vector", lambda v: v.tensor_copy(idb[:], idf[:]), reads=("idf",), writes=("idb",))
            p.step("vector", lambda v: v.memset(onesf[:], 1.0), writes=("ones",))
            p.step("sync", lambda e: e.dma_start(out=sel2[:], in_=sel2c), writes=("sel2",), inc=16)
            p.step("gpsimd", lambda e: e.dma_start(out=b2ones[:], in_=b2c), writes=("b2",), inc=16)
            p.step("vector", lambda v: _last(
                v.tensor_reduce(mtmp[0:1, 0:1], gk_row[0:1, 0:DH], AXX, ALU.max, apply_absolute_value=True),
                v.tensor_reduce(mtmp[0:1, 1:2], gk_row[0:1, DH:2 * DH], AXX, ALU.max, apply_absolute_value=True),
                v.tensor_tensor(mtmp[0:1, 0:1], mtmp[0:1, 0:1], mtmp[0:1, 1:2], ALU.mult),
                v.tensor_scalar(mtmp[0:1, 0:1], mtmp[0:1, 0:1], -8.0, None, ALU.mult)),
                reads=("g1", "g2"), writes=("mt",))
            p.step("tensor", lambda te: te.matmul(pbc[:], onesf[0:1, :], mtmp[0:1, 0:1], start=True, stop=True),
                   reads=("mt", "ones"), writes=("pbc",))
            p.step("vector", lambda v: v.tensor_copy(negM[:], pbc[:]), reads=("pbc",), writes=("nm",))
            p.emit(st0)


        with contextlib.ExitStack() as stA:
            xt = stA.enter_context(nc.sbuf_tensor("xt", [128, 2 * D], F32))
            trash = stA.enter_context(nc.sbuf_tensor("trashA", [128, D], F32))
            stt = stA.enter_context(nc.sbuf_tensor("stA", [128, 16], F32))
            xc = stA.enter_context(nc.sbuf_tensor("xcA", [128, 2 * D], BF16))
            xn_t = stA.enter_context(nc.sbuf_tensor("xn_t", [128, 2 * D], BF16))
            ptr = stA.enter_context(nc.psum_tensor("ptrA", [128, 4 * 1024], BF16))
            xn_s = stA.enter_context(nc.sbuf_tensor("xn_s", [128, 2 * D], BF16))
            p = Prog(nc, "pA")
            for i in range(64):
                src = xo[i * 128:(i + 1) * 128, :] if i < 32 else xp[(i - 32) * 128:(i - 31) * 128, :]
                pr = i % 2
                xb = ("xb", pr)
                xti = xt[:, pr * D:(pr + 1) * D]
                st8 = stt[:, pr * 8:pr * 8 + 8]
                xci = xc[:, pr * D:(pr + 1) * D]
                xnti = xn_t[:, pr * D:(pr + 1) * D]
                p.step("sync" if i % 2 == 0 else "gpsimd",
                       lambda e, src=src, xti=xti: e.dma_start(out=xti, in_=src),
                       writes=(xb,), inc=16)
                p.step("scalar", lambda s, xti=xti, st8=st8: _last(
                    s.activation(trash[:], xti, AF.Square, accum_out=st8[:, 1:2]),
                    s.activation(trash[:], xti, AF.Copy, accum_out=st8[:, 0:1])),
                    reads=(xb,), writes=(("stS", pr), "trash"))
                p.step("vector", lambda v, st8=st8: _last(
                    v.tensor_scalar(st8[:, 2:3], st8[:, 0:1], 1.0 / D, None, ALU.mult),
                    v.tensor_tensor(st8[:, 3:4], st8[:, 2:3], st8[:, 2:3], ALU.mult),
                    v.scalar_tensor_tensor(st8[:, 3:4], st8[:, 1:2], 1.0 / D, st8[:, 3:4], ALU.mult, ALU.subtract),
                    v.tensor_scalar(st8[:, 3:4], st8[:, 3:4], LN_EPS, None, ALU.add)),
                    reads=(("stS", pr),), writes=(("stV", pr),))
                p.step("scalar", lambda s, st8=st8: s.activation(st8[:, 6:7], st8[:, 3:4], AF.Sqrt),
                       reads=(("stV", pr),), writes=(("stQ", pr),))
                p.step("vector", lambda v, st8=st8: _last(
                    v.reciprocal(st8[:, 4:5], st8[:, 6:7]),
                    v.tensor_scalar(st8[:, 5:6], st8[:, 2:3], st8[:, 4:5], -1.0, ALU.mult, ALU.mult)),
                    reads=(("stQ", pr), ("stV", pr)), writes=(("stR", pr),))
                p.step("vector", lambda v, xti=xti, xci=xci, st8=st8: v.tensor_scalar(
                    xci, xti, st8[:, 4:5], st8[:, 5:6], ALU.mult, ALU.add),
                    reads=(("stR", pr), xb), writes=(("xc", pr),))
                p.step("vector", lambda v, xci=xci, xnti=xnti: _last(
                    v.tensor_tensor(xnti, xci, gfull[:], ALU.mult),
                    v.tensor_tensor(xnti, xnti, bfull[:], ALU.add)),
                    reads=(("xc", pr),), writes=(("xnt", pr),))
                for cp in range(4):
                    bk = (4 * i + cp) % 4
                    def trp(te, cp=cp, bk=bk, xnti=xnti):
                        te.transpose(ptr[:, bk * 1024:bk * 1024 + 128],
                                     xnti[:, 2 * cp * 128:(2 * cp + 1) * 128], idb[:])
                        return te.transpose(ptr[:, bk * 1024 + 128:bk * 1024 + 256],
                                            xnti[:, (2 * cp + 1) * 128:(2 * cp + 2) * 128], idb[:])
                    p.step("tensor", trp, reads=(("xnt", pr),), writes=(("ptr", bk),))
                    if cp % 2 == 0:
                        p.step("scalar", lambda s, cp=cp, i=i, bk=bk: s.copy(
                            xn_s[:, (i % 2) * D + 2 * cp * 128: (i % 2) * D + (2 * cp + 2) * 128],
                            ptr[:, bk * 1024:bk * 1024 + 256]),
                            reads=(("ptr", bk),), writes=(("xns", i % 2, cp),))
                    else:
                        p.step("vector", lambda v, cp=cp, i=i, bk=bk: v.tensor_copy(
                            xn_s[:, (i % 2) * D + 2 * cp * 128: (i % 2) * D + (2 * cp + 2) * 128],
                            ptr[:, bk * 1024:bk * 1024 + 256]),
                            reads=(("ptr", bk),), writes=(("xns", i % 2, cp),))
                p.step("gpsimd" if i % 2 == 0 else "sync", lambda e, i=i: e.dma_start(
                    out=xn_dr[:, :, i * 128:(i + 1) * 128].rearrange("c p t -> p c t"),
                    in_=xn_s[:, (i % 2) * D:(i % 2 + 1) * D].rearrange("p (c t) -> p c t", c=8)),
                    reads=tuple(("xns", i % 2, cp) for cp in range(4)), writes=(("xnd", i),), inc=16)
            p.emit(stA)

        with contextlib.ExitStack() as stB:
            w3 = stB.enter_context(nc.sbuf_tensor("w3", [128, 2 * 3 * 1024], BF16))
            xnb = stB.enter_context(nc.sbuf_tensor("xnb", [128, 2 * 8 * 512], BF16))
            qn_hp = stB.enter_context(nc.sbuf_tensor("qn_hp", [128, TOK], BF16))
            kn_hp = stB.enter_context(nc.sbuf_tensor("kn_hp", [128, S], BF16))
            vx_hp = stB.enter_context(nc.sbuf_tensor("vx_hp", [128, 64 * 130], BF16))
            vtmp = stB.enter_context(nc.sbuf_tensor("vtmp", [128, 512], BF16))
            sq2 = stB.enter_context(nc.sbuf_tensor("sq2", [128, 512], BF16))
            nrm = stB.enter_context(nc.sbuf_tensor("nrm", [2, 2 * 512], F32))
            p_sb = stB.enter_context(nc.sbuf_tensor("p_sb", [128, 4 * 1024], BF16))
            recb = stB.enter_context(nc.sbuf_tensor("recb", [64, 4 * 512], F32))
            recd = stB.enter_context(nc.sbuf_tensor("recd", [1, 2 * 1024], F32))
            attn_hp = stB.enter_context(nc.sbuf_tensor("attn_hp", [128, TOK], BF16))
            pool = stB.enter_context(nc.psum_tensor("poolB", [128, 4096], F32))
            bank = lambda k: pool[:, k * 512:(k + 1) * 512]
            p = Prog(nc, "pBC")

            p.step("gpsimd", lambda g: _last(
                g.memset(vx_hp[:].rearrange("p (c t) -> p c t", c=64)[:, :, 64:65], 1.0),
                g.memset(vx_hp[:].rearrange("p (c t) -> p c t", c=64)[:, :, 129:130], 1.0)),
                writes=("vxones",))

            for hp in range(8):
                sl = hp % 2
                woff = sl * 3 * 1024
                for wi, wsrc in enumerate((wqT, wkT, wvT)):
                    dst = w3[:, woff + wi * 1024: woff + (wi + 1) * 1024]
                    src = wsrc.rearrange("(c p) j -> p c j", p=128)[:, :, hp * 128:(hp + 1) * 128]
                    p.step("gpsimd", lambda g, dst=dst, src=src: g.dma_start(
                        out=dst.rearrange("p (c j) -> p c j", c=8), in_=src),
                        writes=(("w3", sl, wi),), inc=16)

                ops = []
                for tt in range(16):
                    if tt < 8:
                        ops.append(("q", tt))
                    ops.append(("k", tt))
                    ops.append(("v", tt))
                xdma_done = set()
                stage2 = {}
                stage2b = {}

                def emit_op(n, kind, tt, hp=hp, sl=sl, woff=woff):
                    xsl = tt % 2
                    xv = xnb[:, xsl * 4096:(xsl + 1) * 4096]
                    if hp > 0:
                        xdma_done.update((0, 1))
                    if tt not in xdma_done:
                        xdma_done.add(tt)
                        p.step("sync" if tt % 2 == 0 else "gpsimd",
                               lambda e, tt=tt, xv=xv: e.dma_start(
                                   out=xv.rearrange("p (c t) -> p c t", c=8),
                                   in_=xn_dr[:, :, tt * 512:(tt + 1) * 512].rearrange("c p t -> p c t")),
                               writes=(("xnb", xsl),), inc=16)
                    ppb = n % 3
                    wi = {"q": 0, "k": 1, "v": 2}[kind]
                    wsl = w3[:, woff + wi * 1024: woff + (wi + 1) * 1024]

                    def mm_proj(te, wsl=wsl, xv=xv, ppb=ppb):
                        for kc in range(8):
                            mm = te.matmul(bank(ppb), wsl[:, kc * 128:(kc + 1) * 128],
                                           xv[:, kc * 512:(kc + 1) * 512],
                                           start=(kc == 0), stop=(kc == 7))
                        return mm
                    p.step("tensor", mm_proj, reads=(("w3", sl, wi), ("xnb", xsl)),
                           writes=(("bank", ppb),))

                    if kind == "v":
                        def st2v(n=n, tt=tt, ppb=ppb):
                            p.step("scalar", lambda s, ppb=ppb: s.copy(vtmp[:], bank(ppb)),
                                   reads=(("bank", ppb),), writes=("vtmp",))
                            for sub in range(4):
                                c = tt * 4 + sub
                                psl = c % 2

                                def tr2slot(psl):
                                    return pool[:, (6 + psl) * 512:(6 + psl) * 512 + 64].bitcast(BF16)
                                p.step("tensor", lambda te, sub=sub, psl=psl, tr2slot=tr2slot: te.transpose(
                                    tr2slot(psl), vtmp[:, sub * 128:(sub + 1) * 128], idb[:]),
                                    reads=("vtmp",), writes=(("bank", 6 + psl),))

                                def vevac(s_, c=c, psl=psl, tr2slot=tr2slot):
                                    dst = vx_hp[:, c * 130: c * 130 + 130].rearrange(
                                        "p (e j) -> p e j", e=2)[:, :, 0:64]
                                    srcv = tr2slot(psl).rearrange("p (e j) -> p e j", e=2)
                                    return s_.copy(dst, srcv)
                                p.step("scalar", vevac, reads=(("bank", 6 + psl),), writes=(("vx", c),))
                        stage2[n] = st2v
                    else:
                        gvec, dstbuf, scale_st, clampv, ptok = (
                            (qg2, qn_hp, 1.0, 8e-8, "prjq") if kind == "q"
                            else (kg2, kn_hp, 1.0 / 64.0, 1e-8, "prjk"))

                        def st2a(n=n, tt=tt, ppb=ppb, gvec=gvec, dstbuf=dstbuf,
                                 scale_st=scale_st, ptok=ptok):
                            nsl = nrm[:, (n % 2) * 512:(n % 2) * 512 + 512]
                            dsl = dstbuf[:, tt * 512:(tt + 1) * 512]
                            p.step("scalar", lambda s, ppb=ppb: s.activation(sq2[:], bank(ppb), AF.Square),
                                   reads=(("bank", ppb),), writes=("sq2",))
                            p.step("scalar", lambda s, dsl=dsl, gvec=gvec, ppb=ppb: s.activation(
                                dsl, bank(ppb), AF.Copy, scale=gvec[:, 0:1]),
                                reads=(("bank", ppb),), writes=(ptok + "g",))
                            p.step("tensor", lambda te: te.matmul(pool[0:2, 1536:2048], b2ones[:], sq2[:],
                                                                  start=True, stop=True),
                                   reads=("sq2",), writes=(("bank", 3),))
                            p.step("scalar", lambda s, nsl=nsl, scale_st=scale_st: s.activation(
                                nsl, pool[0:2, 1536:2048], AF.Sqrt, scale=scale_st),
                                reads=(("bank", 3),), writes=(("nrm", n % 2),))

                        def st2b(n=n, tt=tt, gvec=gvec, dstbuf=dstbuf, clampv=clampv, ptok=ptok):
                            nsl = nrm[:, (n % 2) * 512:(n % 2) * 512 + 512]
                            pfxb = 4 + n % 2
                            dsl = dstbuf[:, tt * 512:(tt + 1) * 512]
                            p.step("vector", lambda v, nsl=nsl, clampv=clampv: _last(
                                v.tensor_scalar(nsl, nsl, clampv, None, ALU.max),
                                v.reciprocal(nsl, nsl)),
                                reads=(("nrm", n % 2),), writes=(("nrm", n % 2),))
                            p.step("tensor", lambda te, nsl=nsl, pfxb=pfxb: te.matmul(
                                bank(pfxb), sel2[:], nsl, start=True, stop=True),
                                reads=(("nrm", n % 2),), writes=(("bank", pfxb),))
                            p.step("vector", lambda v, dsl=dsl, pfxb=pfxb: v.tensor_tensor(
                                dsl, dsl, bank(pfxb), ALU.mult),
                                reads=(("bank", pfxb), ptok + "g"), writes=((ptok, tt),))
                        stage2[n] = st2a
                        stage2b[n] = st2b

                for n, (kind, tt) in enumerate(ops):
                    emit_op(n, kind, tt)
                    if n >= 2:
                        stage2.pop(n - 2)()
                    if n >= 3 and (n - 3) in stage2b:
                        stage2b.pop(n - 3)()
                for n in (len(ops) - 2, len(ops) - 1):
                    stage2.pop(n)()
                    if n - 1 in stage2b:
                        stage2b.pop(n - 1)()
                for n in sorted(stage2b):
                    stage2b.pop(n)()

                def mm_sp(te, h, it, qt):
                    bk = h * 2 + it % 2
                    return te.matmul(bank(bk),
                                     kn_hp[h * 64:(h + 1) * 64, it * 128:(it + 1) * 128],
                                     qn_hp[h * 64:(h + 1) * 64, qt * 512:(qt + 1) * 512],
                                     start=True, stop=True, tile_position=(h * 64, 0))

                def emit_av(h, j, qt):
                    obk = 4 + 2 * (qt % 2) + h

                    def av2(te, h=h, j=j, obk=obk):
                        for dd in range(2):
                            it = 2 * j + dd
                            mm = te.matmul(
                                pool[0:65, obk * 512:(obk + 1) * 512],
                                vx_hp[:, it * 130 + h * 65: it * 130 + h * 65 + 65],
                                p_sb[:, (h * 2 + j % 2) * 1024 + dd * 512:
                                     (h * 2 + j % 2) * 1024 + dd * 512 + 512],
                                start=(it == 0), stop=(it == 63))
                        return mm
                    p.step("tensor", av2,
                           reads=(("pexp", h, j % 2), ("vx", 2 * j), ("vx", 2 * j + 1)),
                           writes=(("bank", obk),))

                def emit_epiA(qt, hp=hp):
                    ob0 = 4 + 2 * (qt % 2)
                    p.step("vector", lambda v, qt=qt, ob0=ob0: v.reciprocal(
                        recd[0:1, (qt % 2) * 1024:(qt % 2) * 1024 + 1024],
                        pool[64:65, ob0 * 512:(ob0 + 2) * 512]),
                        reads=(("bank", ob0), ("bank", ob0 + 1)), writes=(("recd", qt % 2),))
                    p.step("sync", lambda e, qt=qt: e.dma_start(
                        out=rec_dr[qt % 2], in_=recd[0:1, (qt % 2) * 1024:(qt % 2) * 1024 + 1024]),
                        reads=(("recd", qt % 2),), writes=(("recdd", qt % 2),), inc=16)
                    for h in range(2):
                        rb = recb[0:64, (2 * (qt % 2) + h) * 512:(2 * (qt % 2) + h) * 512 + 512]
                        p.step("gpsimd", lambda g, rb=rb, h=h, qt=qt: g.dma_start(
                            out=rb, in_=rec_dr[qt % 2, 0, h * 512:(h + 1) * 512]
                            .unsqueeze(0).broadcast_to([64, 512])),
                            reads=(("recdd", qt % 2),), writes=(("recb", h, qt % 2),), inc=16)

                def emit_epiB(qt, hp=hp):
                    ob0 = 4 + 2 * (qt % 2)
                    for h in range(2):
                        obk = ob0 + h
                        rb = recb[0:64, (2 * (qt % 2) + h) * 512:(2 * (qt % 2) + h) * 512 + 512]
                        p.step("vector", lambda v, h=h, qt=qt, rb=rb, obk=obk: v.tensor_tensor(
                            attn_hp[h * 64:(h + 1) * 64, qt * 512:(qt + 1) * 512],
                            pool[0:64, obk * 512:(obk + 1) * 512], rb, ALU.mult),
                            reads=(("recb", h, qt % 2), ("bank", obk)), writes=(("ahp", h, qt),))

                pend_epiA = None
                pend_epiB = None
                for qt in range(8):
                    for j in range(32):
                        for h in range(2):
                            p.step("tensor", lambda te, h=h, j=j, qt=qt, mm_sp=mm_sp: _last(
                                mm_sp(te, h, 2 * j, qt), mm_sp(te, h, 2 * j + 1, qt)),
                                reads=(("prjk", (2 * j) // 4), ("prjq", qt)),
                                writes=(("bank", h * 2), ("bank", h * 2 + 1)))
                        if j > 0:
                            for h in range(2):
                                emit_av(h, j - 1, qt)
                        for h in range(2):
                            p.step("scalar", lambda s, h=h, j=j: s.activation(
                                p_sb[:, (h * 2 + j % 2) * 1024:(h * 2 + j % 2) * 1024 + 1024],
                                pool[:, h * 1024:(h + 1) * 1024],
                                AF.Exp, bias=negM[:, 0:1], scale=1.0),
                                reads=(("bank", h * 2), ("bank", h * 2 + 1)),
                                writes=(("pexp", h, j % 2),))
                        if j == 1 and pend_epiA is not None:
                            pend_epiA()
                            pend_epiA = None
                        if j == 8 and pend_epiB is not None:
                            pend_epiB()
                            pend_epiB = None
                        if j == 16 and qt == 5 and hp < 7:
                            for ttp in (0, 1):
                                xvp = xnb[:, ttp * 4096:(ttp + 1) * 4096]
                                p.step("sync" if ttp % 2 == 0 else "gpsimd",
                                       lambda e, ttp=ttp, xvp=xvp: e.dma_start(
                                           out=xvp.rearrange("p (c t) -> p c t", c=8),
                                           in_=xn_dr[:, :, ttp * 512:(ttp + 1) * 512].rearrange("c p t -> p c t")),
                                       writes=(("xnb", ttp),), inc=16)
                    for h in range(2):
                        emit_av(h, 31, qt)
                    pend_epiA = lambda qt=qt, emit_epiA=emit_epiA: emit_epiA(qt)
                    pend_epiB = lambda qt=qt, emit_epiB=emit_epiB: emit_epiB(qt)
                pend_epiA()
                pend_epiB()
                p.step("sync", lambda e, hp=hp: e.dma_start(out=attn_dr[hp], in_=attn_hp[:]),
                       reads=tuple(("ahp", e2, qt) for e2 in range(2) for qt in range(8)),
                       writes=(("adr", hp),), inc=16)
            p.emit(stB)

        with contextlib.ExitStack() as stD:
            attn_all = stD.enter_context(nc.sbuf_tensor("attn_all", [128, 8 * TOK], BF16))
            wo_sb = stD.enter_context(nc.sbuf_tensor("wo_sb", [128, 8 * 1024], BF16))
            xt2 = stD.enter_context(nc.sbuf_tensor("xt2", [128, 2 * D], F32))
            y2 = stD.enter_context(nc.sbuf_tensor("y2", [128, D], F32))
            sqd = stD.enter_context(nc.sbuf_tensor("sqd", [128, D], F32))
            std_ = stD.enter_context(nc.sbuf_tensor("stD", [128, 8], F32))
            x1t = stD.enter_context(nc.sbuf_tensor("x1t", [128, D], F32))
            s_all = stD.enter_context(nc.sbuf_tensor("s_all", [128, 32], F32))
            s_tr = stD.enter_context(nc.sbuf_tensor("s_tr", [32, 128], F32))
            pod = stD.enter_context(nc.psum_tensor("pod", [128, 2 * 512], F32))
            pstr = stD.enter_context(nc.psum_tensor("pstr", [32, 128], F32))
            p = Prog(nc, "pD")
            p.step("gpsimd", lambda g: g.dma_start(out=wo_sb[:].rearrange("p (c j) -> p c j", c=8),
                                                   in_=woT.rearrange("(c p) j -> p c j", p=128)),
                   writes=("wo",), inc=16)
            p.step("sync", lambda e: e.dma_start(out=attn_all[:].rearrange("p (c t) -> p c t", c=8),
                                                 in_=attn_dr.rearrange("c p t -> p c t")),
                   writes=("aall",), inc=16)
            for i in range(32):
                xb = ("x2b", i % 2)
                xti = xt2[:, (i % 2) * D:(i % 2 + 1) * D]
                p.step("sync", lambda e, i=i, xti=xti: e.dma_start(out=xti, in_=xo[i * 128:(i + 1) * 128, :]),
                       writes=(xb,), inc=16)
                for jh in range(2):
                    def mm_od(te, i=i, jh=jh):
                        for kc in range(8):
                            mm = te.matmul(pod[:, jh * 512:(jh + 1) * 512],
                                           attn_all[:, kc * TOK + i * 128: kc * TOK + (i + 1) * 128],
                                           wo_sb[:, kc * 1024 + jh * 512: kc * 1024 + jh * 512 + 512],
                                           start=(kc == 0), stop=(kc == 7))
                        return mm
                    p.step("tensor", mm_od, reads=("aall", "wo"), writes=(("pod", jh),))
                    p.step("vector", lambda v, jh=jh, xti=xti: _last(
                        v.tensor_tensor(y2[:, jh * 512:(jh + 1) * 512], pod[:, jh * 512:(jh + 1) * 512],
                                        bofull[:, jh * 512:(jh + 1) * 512], ALU.add),
                        v.tensor_tensor(y2[:, jh * 512:(jh + 1) * 512], y2[:, jh * 512:(jh + 1) * 512],
                                        xti[:, jh * 512:(jh + 1) * 512], ALU.add)),
                        reads=(("pod", jh), xb), writes=(("y2", jh),))
                p.step("vector", lambda v: _last(
                    v.tensor_tensor(sqd[:], y2[:], y2[:], ALU.mult),
                    v.reduce_sum(std_[:, 1:2], sqd[:], AXX),
                    v.reduce_sum(std_[:, 0:1], y2[:], AXX),
                    v.tensor_scalar(std_[:, 2:3], std_[:, 0:1], 1.0 / D, None, ALU.mult),
                    v.tensor_tensor(std_[:, 3:4], std_[:, 2:3], std_[:, 2:3], ALU.mult),
                    v.scalar_tensor_tensor(std_[:, 3:4], std_[:, 1:2], 1.0 / D, std_[:, 3:4], ALU.mult, ALU.subtract),
                    v.tensor_scalar(std_[:, 3:4], std_[:, 3:4], LN_EPS, None, ALU.add)),
                    reads=(("y2", 0), ("y2", 1)), writes=("stD", "sqd"))
                p.step("scalar", lambda s: s.activation(std_[:, 6:7], std_[:, 3:4], AF.Sqrt),
                       reads=("stD",), writes=("stDb",))
                p.step("vector", lambda v: _last(
                    v.reciprocal(std_[:, 4:5], std_[:, 6:7]),
                    v.tensor_scalar(std_[:, 5:6], std_[:, 2:3], std_[:, 4:5], -1.0, ALU.mult, ALU.mult)),
                    reads=("stD", "stDb"), writes=("stD",))
                p.step("vector", lambda v: v.tensor_scalar(x1t[:], y2[:], std_[:, 4:5], std_[:, 5:6], ALU.mult, ALU.add),
                       reads=("stD", ("y2", 0), ("y2", 1)), writes=("x1t",))
                p.step("sync", lambda e, i=i: e.dma_start(out=x1o[i * 128:(i + 1) * 128, :], in_=x1t[:]),
                       reads=("x1t",), writes=(("x1d", i),), inc=16)
                p.step("vector", lambda v, i=i: _last(
                    v.tensor_tensor(sqd[:], x1t[:], rtfull[:], ALU.mult),
                    v.reduce_sum(s_all[:, i:i + 1], sqd[:], AXX)),
                    reads=("x1t",), writes=(("s", i), "sqd"))
            p.step("tensor", lambda te: te.transpose(pstr[:], s_all[:, 0:32], idf[:]),
                   reads=tuple(("s", i) for i in range(32)), writes=("pstr",))
            p.step("vector", lambda v: v.tensor_copy(s_tr[:], pstr[:]), reads=("pstr",), writes=("str",))
            p.step("sync", lambda e: e.dma_start(out=so.rearrange("(i p) -> i p", p=128), in_=s_tr[:]),
                   reads=("str",), writes=("sd",), inc=16)
            p.step("gpsimd", lambda g: g.engine_nop(),
                   reads=tuple(("x1d", i) for i in range(32)) + ("sd",))
            p.emit(stD)

    return nc



P2_W = [("w1l", [D, 512]), ("b1l", [512]), ("w2l", [512, D]), ("b2l", [D]), ("gl", [D]),
        ("w1h", [D, 4096]), ("b1h", [4096]), ("w2h", [4096, D]), ("b2h", [D]), ("gh", [D]),
        ("u64c", [64, 64]), ("l128c", [128, 128]), ("onesc", [128, 128]), ("oh", [1, 2])]


def build_p2():
    nc = bass.Bass("TRN2", target_bir_lowering=False)
    io = {}
    io["x1i"] = nc.dram_tensor("x1i", [TOK, D], F32, kind="ExternalInput").ap()
    io["sfull"] = nc.dram_tensor("sfull", [S], F32, kind="ExternalInput").ap()
    io["identf"] = nc.dram_tensor("identf", [128, 128], F32, kind="ExternalInput").ap()
    for nm, shape in P2_W:
        io[nm] = nc.dram_tensor(nm, shape, F32, kind="ExternalInput").ap()
    io["x2o"] = nc.dram_tensor("x2o", [TOK, D], F32, kind="ExternalOutput").ap()
    emit_p2(nc, io)
    return nc


def emit_p2(nc, io):
    x1i, sfull, oh = io["x1i"], io["sfull"], io["oh"]
    w1l, b1l, w2l, b2l, gl = io["w1l"], io["b1l"], io["w2l"], io["b2l"], io["gl"]
    w1h, b1h, w2h, b2h, gh = io["w1h"], io["b1h"], io["w2h"], io["b2h"], io["gh"]
    identf, u64c, l128c, onesc = io["identf"], io["u64c"], io["l128c"], io["onesc"]
    x2o = io["x2o"]
    mrow = nc.dram_tensor("mrow2", [TOK], F32).ap()

    stack = contextlib.ExitStack()
    with stack:
        sbt = lambda *a: stack.enter_context(nc.sbuf_tensor(*a))
        idf = sbt("idf2", [128, 128], F32)
        idb = sbt("idb2", [128, 128], BF16)
        onesf = sbt("onesf2", [128, 128], F32)
        u64b = sbt("u64b", [64, 64], BF16)
        l128b = sbt("l128b", [128, 128], BF16)
        pool = stack.enter_context(nc.psum_tensor("pool2", [128, 4096], F32))

        def bank(i, rows=128):
            return pool[0:rows, i * 512:(i + 1) * 512]

        with contextlib.ExitStack() as stR:
            s_sb = stR.enter_context(nc.sbuf_tensor("s_sb", [128, 64], F32))
            rr = stR.enter_context(nc.sbuf_tensor("rr", [128, 12], F32))
            e_sb = stR.enter_context(nc.sbuf_tensor("e_sb", [128, 64], F32))
            u_sb = stR.enter_context(nc.sbuf_tensor("u_sb", [128, 64], F32))
            trow = stR.enter_context(nc.sbuf_tensor("trow", [1, 128], F32))
            pmask = stR.enter_context(nc.sbuf_tensor("pmask", [128, 64], F32))
            qmaskb = stR.enter_context(nc.sbuf_tensor("qmaskb", [128, 64], BF16))
            qT = stR.enter_context(nc.sbuf_tensor("qT", [64, 128], BF16))
            pref = stR.enter_context(nc.sbuf_tensor("pref", [128, 64], F32))
            rb16 = stR.enter_context(nc.sbuf_tensor("rb16", [128, 1], BF16))
            mk_own = stR.enter_context(nc.sbuf_tensor("mk_own", [64, 64], F32))
            mk_shift = stR.enter_context(nc.sbuf_tensor("mk_shift", [64, 64], F32))
            o01 = stR.enter_context(nc.sbuf_tensor("o01", [64, 2], F32))
            ohs = stR.enter_context(nc.sbuf_tensor("ohs", [1, 2], F32))
            bis = stR.enter_context(nc.sbuf_tensor("bis", [128, 10], F32))
            p = Prog(nc, "pR")
            p.step("sync", lambda e: e.dma_start(out=s_sb[:], in_=sfull.rearrange("(p f) -> p f", p=128)),
                   writes=("s",), inc=16)
            p.step("sync", lambda e: e.dma_start(out=idf[:], in_=identf), writes=("idf",), inc=16)
            p.step("sync", lambda e: e.dma_start(out=onesf[:], in_=onesc), writes=("ones",), inc=16)
            p.step("gpsimd", lambda e: e.dma_start(out=u64b[:], in_=u64c), writes=("u64",), inc=16)
            p.step("gpsimd", lambda e: e.dma_start(out=l128b[:], in_=l128c), writes=("l128",), inc=16)
            p.step("sync", lambda e: e.dma_start(out=ohs[:], in_=oh), writes=("oh",), inc=16)
            p.step("vector", lambda v: v.tensor_copy(idb[:], idf[:]), reads=("idf",), writes=("idb",))
            p.step("vector", lambda v: v.reduce_max(rr[:, 8:9], s_sb[:], AXX),
                   reads=("s",), writes=("r8",))
            p.step("tensor", lambda te: te.transpose(pool[0:1, 0:128], rr[:, 8:9], idf[:]),
                   reads=("r8", "idf"), writes=(("rb", 0),))
            p.step("vector", lambda v: _last(
                v.tensor_copy(trow[0:1, :], pool[0:1, 0:128]),
                v.reduce_max(trow[0:1, 0:1], trow[0:1, :], AXX)), reads=(("rb", 0),), writes=("trow",))
            p.step("tensor", lambda te: te.matmul(pool[:, 128:129], onesf[0:1, :], trow[0:1, 0:1],
                                                  start=True, stop=True),
                   reads=("trow", "ones"), writes=(("rb", 0),))
            p.step("vector", lambda v: _last(
                v.tensor_copy(rr[:, 3:4], pool[:, 128:129]),
                v.memset(rr[:, 0:1], 0.0)), reads=(("rb", 0),), writes=("rr",))
            for it in range(50):
                p.step("vector", lambda v: _last(
                    v.tensor_tensor(rr[:, 1:2], rr[:, 3:4], rr[:, 0:1], ALU.min),
                    v.tensor_scalar(rr[:, 2:3], rr[:, 1:2], -10.0, None, ALU.mult),
                    v.tensor_scalar(u_sb[:], s_sb[:], rr[:, 0:1], None, ALU.min)),
                    reads=("rr", "s"), writes=("rr", "u"))
                p.step("scalar", lambda sc: sc.activation(e_sb[:], u_sb[:], AF.Exp,
                                                          bias=rr[:, 2:3], scale=10.0),
                       reads=("u", "rr"), writes=("e",))
                p.step("vector", lambda v: v.reduce_sum(rr[:, 9:10], e_sb[:], AXX),
                       reads=("e",), writes=("r9",))
                p.step("tensor", lambda te: te.matmul(pool[:, 256:257], onesf[:], rr[:, 9:10],
                                                      start=True, stop=True),
                       reads=("r9", "ones"), writes=(("rb", 0),))
                p.step("vector", lambda v: v.tensor_copy(rr[:, 4:5], pool[:, 256:257]),
                       reads=(("rb", 0),), writes=("r4",))
                p.step("scalar", lambda sc: sc.activation(rr[:, 5:6], rr[:, 4:5], AF.Ln),
                       reads=("r4",), writes=("r5",))
                p.step("vector", lambda v: _last(
                    v.tensor_scalar(rr[:, 11:12], rr[:, 5:6], 0.1, -C0_ROUTE, ALU.mult, ALU.add),
                    v.tensor_tensor(rr[:, 0:1], rr[:, 11:12], rr[:, 1:2], ALU.add)),
                    reads=("r5", "rr"), writes=("rr",))
            p.step("vector", lambda v: _last(
                v.tensor_scalar(rr[:, 10:11], rr[:, 0:1], CUT, None, ALU.add),
                v.tensor_scalar(pmask[:], s_sb[:], rr[:, 0:1], None, ALU.min),
                v.tensor_scalar(pmask[:], pmask[:], rr[:, 10:11], None, ALU.max),
                v.tensor_scalar(bis[:, 0:1], rr[:, 10:11], -1.0, None, ALU.add),
                v.tensor_copy(bis[:, 1:2], rr[:, 0:1])),
                reads=("rr", "s"), writes=("pm", "bis"))
            for it in range(45):
                p.step("vector", lambda v: _last(
                    v.tensor_tensor(bis[:, 2:3], bis[:, 0:1], bis[:, 1:2], ALU.add),
                    v.tensor_scalar(bis[:, 2:3], bis[:, 2:3], 0.5, None, ALU.mult),
                    v.tensor_scalar(e_sb[:], pmask[:], bis[:, 2:3], None, ALU.is_gt),
                    v.reduce_sum(rr[:, 8:9], e_sb[:], AXX)),
                    reads=("bis", "pm"), writes=("bis", "r89b"))
                p.step("tensor", lambda te: te.matmul(pool[:, 256:257], onesf[:], rr[:, 8:9], start=True, stop=True),
                       reads=("r89b", "ones"), writes=(("rb", 0),))
                p.step("vector", lambda v: _last(
                    v.tensor_copy(bis[:, 3:4], pool[:, 256:257]),
                    v.tensor_scalar(bis[:, 4:5], bis[:, 3:4], 1024.0, None, ALU.is_ge),
                    v.tensor_tensor(rr[:, 8:9], bis[:, 2:3], bis[:, 0:1], ALU.subtract),
                    v.tensor_tensor(rr[:, 8:9], rr[:, 8:9], bis[:, 4:5], ALU.mult),
                    v.tensor_tensor(bis[:, 0:1], bis[:, 0:1], rr[:, 8:9], ALU.add),
                    v.tensor_tensor(rr[:, 8:9], bis[:, 2:3], bis[:, 1:2], ALU.subtract),
                    v.tensor_scalar(rr[:, 9:10], bis[:, 4:5], -1.0, 1.0, ALU.mult, ALU.add),
                    v.tensor_tensor(rr[:, 8:9], rr[:, 8:9], rr[:, 9:10], ALU.mult),
                    v.tensor_tensor(bis[:, 1:2], bis[:, 1:2], rr[:, 8:9], ALU.add)),
                    reads=(("rb", 0), "bis"), writes=("bis",))
            p.step("vector", lambda v: _last(
                v.tensor_scalar(e_sb[:], pmask[:], bis[:, 1:2], None, ALU.is_gt),
                v.reduce_sum(rr[:, 8:9], e_sb[:], AXX)),
                reads=("bis", "pm"), writes=("eA", "r89c"))
            p.step("tensor", lambda te: te.matmul(pool[:, 256:257], onesf[:], rr[:, 8:9], start=True, stop=True),
                   reads=("r89c", "ones"), writes=(("rb", 0),))
            p.step("vector", lambda v: v.tensor_copy(bis[:, 8:9], pool[:, 256:257]),
                   reads=(("rb", 0),), writes=("cntA",))
            p.step("vector", lambda v: _last(
                v.tensor_scalar(u_sb[:], pmask[:], bis[:, 0:1], None, ALU.is_gt),
                v.tensor_tensor(u_sb[:], u_sb[:], e_sb[:], ALU.subtract),
                v.tensor_copy(qmaskb[:], u_sb[:])),
                reads=("bis", "pm", "eA"), writes=("qm",))
            p.step("tensor", lambda te: te.transpose(pool[0:64, 2048:2112].bitcast(BF16), qmaskb[:], idb[:]),
                   reads=("qm", "idb"), writes=(("rb", 4),))
            p.step("vector", lambda v: v.tensor_copy(qT[:], pool[0:64, 2048:2112].bitcast(BF16)),
                   reads=(("rb", 4),), writes=("qT",))
            p.step("tensor", lambda te: te.matmul(pool[:, 0:64], qT[:], u64b[:], start=True, stop=True),
                   reads=("qT", "u64"), writes=(("rb", 0),))
            p.step("vector", lambda v: _last(
                v.tensor_copy(pref[:], pool[:, 0:64]),
                v.tensor_copy(rb16[:], pool[:, 63:64])), reads=(("rb", 0),), writes=("pref", "rb16"))
            p.step("tensor", lambda te: te.matmul(pool[:, 512:513], l128b[:], rb16[:], start=True, stop=True),
                   reads=("l128", "rb16"), writes=(("rb", 1),))
            p.step("vector", lambda v: _last(
                v.tensor_copy(rr[:, 7:8], pool[:, 512:513]),
                v.tensor_scalar(rr[:, 6:7], bis[:, 8:9], -1.0, 1024.0, ALU.mult, ALU.add),
                v.tensor_scalar(pref[:], pref[:], rr[:, 7:8], None, ALU.add),
                v.tensor_scalar(pref[:], pref[:], rr[:, 6:7], None, ALU.is_le),
                v.tensor_tensor(pref[:], pref[:], qmaskb[:], ALU.mult),
                v.tensor_tensor(pmask[:], e_sb[:], pref[:], ALU.add)),
                reads=(("rb", 1), "qm", "bis", "eA", "pref"), writes=("pm",))
            p.step("tensor", lambda te: te.matmul(pool[0:64, 1536:1538], onesf[0:1, 0:64], ohs[0:1, 0:2],
                                                  start=True, stop=True),
                   reads=("oh", "ones"), writes=(("rb", 3),))
            p.step("gpsimd", lambda g: g.dma_start(out=mk_shift[:], in_=pmask[64:128, :]),
                   reads=("pm",), writes=("mks",), inc=16)
            p.step("vector", lambda v: _last(
                v.tensor_copy(o01[:], pool[0:64, 1536:1538]),
                v.tensor_scalar(mk_own[:], pmask[0:64, :], o01[:, 0:1], None, ALU.mult),
                v.scalar_tensor_tensor(mk_own[:], mk_shift[:], o01[:, 1:2], mk_own[:], ALU.mult, ALU.add)),
                reads=(("rb", 3), "pm", "mks"), writes=("mko",))
            p.step("sync", lambda e: e.dma_start(out=mrow.rearrange("(p f) -> p f", p=64), in_=mk_own[:]),
                   reads=("mko",), writes=("mrowd",), inc=16)
            p.emit(stR)

        with contextlib.ExitStack() as stF:
            w1l_sb = stF.enter_context(nc.sbuf_tensor("w1l_sb", [128, 8 * 512], BF16))
            w2l_sb = stF.enter_context(nc.sbuf_tensor("w2l_sb", [128, 4 * 1024], BF16))
            w1h_sb = stF.enter_context(nc.sbuf_tensor("w1h_sb", [128, 8 * 4096], BF16))
            w2h_sb = stF.enter_context(nc.sbuf_tensor("w2h_sb", [128, 32 * 1024], BF16))
            gcol = stF.enter_context(nc.sbuf_tensor("gcol", [128, 16], F32))
            b1c = stF.enter_context(nc.sbuf_tensor("b1c", [128, 36], F32))
            b2lf = stF.enter_context(nc.sbuf_tensor("b2lf", [128, D], BF16))
            b2hf = stF.enter_context(nc.sbuf_tensor("b2hf", [128, D], BF16))
            x1t = stF.enter_context(nc.sbuf_tensor("x1t2", [128, D], F32))
            sqf = stF.enter_context(nc.sbuf_tensor("sqf2", [128, D], F32))
            stt = stF.enter_context(nc.sbuf_tensor("st2", [128, 8], F32))
            mks = stF.enter_context(nc.sbuf_tensor("mks", [128, 6], F32))
            xmini = stF.enter_context(nc.sbuf_tensor("xmini", [128, 2 * 256], BF16))
            xhp = stF.enter_context(nc.sbuf_tensor("xhp", [128, 8 * 512], BF16))
            hl = stF.enter_context(nc.sbuf_tensor("hl", [128, 4 * 512], BF16))
            hh = stF.enter_context(nc.sbuf_tensor("hh", [128, 32 * 512], BF16))
            y2 = stF.enter_context(nc.sbuf_tensor("y22", [128, D], F32))
            p = Prog(nc, "pF")
            p.step("gpsimd", lambda g: g.dma_start(out=w1l_sb[:].rearrange("p (c j) -> p c j", c=8),
                                                   in_=w1l.rearrange("(c p) j -> p c j", p=128)),
                   writes=("w1l",), inc=16)
            p.step("gpsimd", lambda g: g.dma_start(out=w2l_sb[:].rearrange("p (c j) -> p c j", c=4),
                                                   in_=w2l.rearrange("(c p) j -> p c j", p=128)),
                   writes=("w2l",), inc=16)
            for hf in range(4):
                p.step("gpsimd", lambda g, hf=hf: g.dma_start(
                    out=w1h_sb[:].rearrange("p (c j) -> p c j", c=8)[:, :, hf * 1024:(hf + 1) * 1024],
                    in_=w1h.rearrange("(c p) j -> p c j", p=128)[:, :, hf * 1024:(hf + 1) * 1024]),
                    writes=(("w1h", hf),), inc=16)
                p.step("gpsimd", lambda g, hf=hf: g.dma_start(
                    out=w2h_sb[:, hf * 8192: (hf + 1) * 8192].rearrange("p (c j) -> p c j", c=8),
                    in_=w2h.rearrange("(c p) j -> p c j", p=128)[:, 8 * hf: 8 * (hf + 1), :]),
                    writes=(("w2h", hf),), inc=16)
            for c in range(8):
                p.step("sync", lambda e, c=c: e.dma_start(out=gcol[:, c], in_=gl[c * 128:(c + 1) * 128]),
                       writes=(("gc", c),), inc=16)
                p.step("sync", lambda e, c=c: e.dma_start(out=gcol[:, 8 + c], in_=gh[c * 128:(c + 1) * 128]),
                       writes=(("gc", 8 + c),), inc=16)
            for c in range(4):
                p.step("sync", lambda e, c=c: e.dma_start(out=b1c[:, c], in_=b1l[c * 128:(c + 1) * 128]),
                       writes=(("b1", c),), inc=16)
            for c in range(32):
                p.step("sync", lambda e, c=c: e.dma_start(out=b1c[:, 4 + c], in_=b1h[c * 128:(c + 1) * 128]),
                       writes=(("b1", 4 + c),), inc=16)
            p.step("gpsimd", lambda g: _repl(g, b2lf, b2l), writes=("b2l",), inc=16)
            p.step("gpsimd", lambda g: _repl(g, b2hf, b2h), writes=("b2h",), inc=16)
            for c in range(8):
                p.step("vector", lambda v, c=c: v.tensor_scalar(
                    w1l_sb[:, c * 512:(c + 1) * 512], w1l_sb[:, c * 512:(c + 1) * 512],
                    gcol[:, c:c + 1], None, ALU.mult),
                    reads=(("gc", c), "w1l"), writes=("w1l",))
                p.step("vector", lambda v, c=c: v.tensor_scalar(
                    w1h_sb[:, c * 4096:(c + 1) * 4096], w1h_sb[:, c * 4096:(c + 1) * 4096],
                    gcol[:, 8 + c:9 + c], None, ALU.mult),
                    reads=(("gc", 8 + c), ("w1h", 0), ("w1h", 1), ("w1h", 2), ("w1h", 3)),
                    writes=(("w1hg", c),))
            for blk in range(8):
                for it4 in range(4):
                    i = blk * 4 + it4
                    p.step("sync", lambda e, i=i: e.dma_start(out=x1t[:], in_=x1i[i * 128:(i + 1) * 128, :]),
                           writes=("x1b",), inc=16)
                    p.step("vector", lambda v: _last(
                        v.tensor_tensor(sqf[:], x1t[:], x1t[:], ALU.mult),
                        v.reduce_sum(stt[:, 0:1], sqf[:], AXX)),
                        reads=("x1b",), writes=("st", "sqf"))
                    p.step("scalar", lambda sc: sc.activation(stt[:, 1:2], stt[:, 0:1], AF.Sqrt),
                           reads=("st",), writes=("stb",))
                    p.step("vector", lambda v: _last(
                        v.tensor_scalar(stt[:, 2:3], stt[:, 1:2], 1e-12, None, ALU.max),
                        v.reciprocal(stt[:, 3:4], stt[:, 2:3]),
                        v.tensor_scalar(stt[:, 4:5], stt[:, 3:4], 32.0, None, ALU.mult)),
                        reads=("st", "stb"), writes=("st",))
                    for cp in range(4):
                        sl = (4 * i + cp) % 2
                        p.step("vector", lambda v, cp=cp, sl=sl: v.tensor_scalar(
                            xmini[:, sl * 256:(sl + 1) * 256], x1t[:, 2 * cp * 128:(2 * cp + 2) * 128],
                            stt[:, 4:5], None, ALU.mult),
                            reads=("st", "x1b"), writes=(("xm", sl),))

                        def tr2(te, sl=sl):
                            te.transpose(pool[:, (6 + sl) * 512: (6 + sl) * 512 + 64].bitcast(BF16),
                                         xmini[:, sl * 256:sl * 256 + 128], idb[:])
                            return te.transpose(pool[:, (6 + sl) * 512 + 64: (6 + sl) * 512 + 128].bitcast(BF16),
                                                xmini[:, sl * 256 + 128:sl * 256 + 256], idb[:])
                        p.step("tensor", tr2, reads=(("xm", sl), "idb"), writes=(("trb", sl),))
                        p.step("scalar", lambda s, cp=cp, it4=it4, sl=sl: s.copy(
                            xhp[:].rearrange("p (c t) -> p c t", c=8)[:, 2 * cp:2 * cp + 2,
                                                                     it4 * 128:(it4 + 1) * 128],
                            pool[:, (6 + sl) * 512: (6 + sl) * 512 + 128].bitcast(BF16)
                            .rearrange("p (c t) -> p c t", c=2)),
                            reads=(("trb", sl),), writes=(("xp", it4, cp),))
                xp_all = tuple(("xp", j, cp) for j in range(4) for cp in range(4))
                for jt in range(4):
                    def mm_upl(te, jt=jt):
                        for kc in range(8):
                            mm = te.matmul(bank(jt % 2), w1l_sb[:, kc * 512 + jt * 128: kc * 512 + jt * 128 + 128],
                                           xhp[:, kc * 512: (kc + 1) * 512],
                                           start=(kc == 0), stop=(kc == 7))
                        return mm
                    p.step("tensor", mm_upl, reads=("w1l",) + xp_all, writes=(("upb", jt % 2),))
                    p.step("scalar", lambda s, jt=jt: s.activation(
                        hl[:, jt * 512:(jt + 1) * 512], bank(jt % 2), AF.Gelu, bias=b1c[:, jt:jt + 1]),
                        reads=(("upb", jt % 2), ("b1", jt)), writes=(("hl", jt),))
                for jt in range(32):
                    def mm_uph(te, jt=jt):
                        for kc in range(8):
                            mm = te.matmul(bank(jt % 2), w1h_sb[:, kc * 4096 + jt * 128: kc * 4096 + jt * 128 + 128],
                                           xhp[:, kc * 512: (kc + 1) * 512],
                                           start=(kc == 0), stop=(kc == 7))
                        return mm
                    p.step("tensor", mm_uph, reads=tuple(("w1hg", c) for c in range(8)) + xp_all,
                           writes=(("upb", jt % 2),))
                    p.step("scalar", lambda s, jt=jt: s.activation(
                        hh[:, jt * 512:(jt + 1) * 512], bank(jt % 2), AF.Gelu, bias=b1c[:, 4 + jt:5 + jt]),
                        reads=(("upb", jt % 2), ("b1", 4 + jt)), writes=(("hh", jt),))
                for it4 in range(4):
                    i = blk * 4 + it4
                    for jh in range(2):
                        def mm_downl(te, it4=it4, jh=jh):
                            for kc in range(4):
                                mm = te.matmul(bank(2 + jh), hl[:, kc * 512 + it4 * 128: kc * 512 + it4 * 128 + 128],
                                               w2l_sb[:, kc * 1024 + jh * 512: kc * 1024 + jh * 512 + 512],
                                               start=(kc == 0), stop=(kc == 3))
                            return mm
                        p.step("tensor", mm_downl,
                               reads=tuple(("hl", j) for j in range(4)) + ("w2l",),
                               writes=(("pod", jh),))
                        def mm_downh(te, it4=it4, jh=jh):
                            for kc in range(32):
                                mm = te.matmul(bank(4 + jh), hh[:, kc * 512 + it4 * 128: kc * 512 + it4 * 128 + 128],
                                               w2h_sb[:, kc * 1024 + jh * 512: kc * 1024 + jh * 512 + 512],
                                               start=(kc == 0), stop=(kc == 31))
                            return mm
                        p.step("tensor", mm_downh,
                               reads=tuple(("hh", j) for j in range(32)) + tuple(("w2h", h) for h in range(4)),
                               writes=(("podh", jh),))
                    p.step("sync", lambda e, i=i: e.dma_start(out=x1t[:], in_=x1i[i * 128:(i + 1) * 128, :]),
                           writes=("x1b",), inc=16)
                    p.step("sync", lambda e, i=i, it4=it4: e.dma_start(out=mks[:, 4 + it4 % 2], in_=mrow[i * 128:(i + 1) * 128]),
                           writes=(("mk2", it4 % 2),), inc=16)
                    p.step("vector", lambda v, it4=it4: _last(
                        v.tensor_tensor(y2[:, 0:512], bank(2), x1t[:, 0:512], ALU.add),
                        v.tensor_tensor(y2[:, 512:1024], bank(3), x1t[:, 512:1024], ALU.add),
                        v.tensor_tensor(y2[:, 0:512], y2[:, 0:512], b2lf[:, 0:512], ALU.add),
                        v.tensor_tensor(y2[:, 512:1024], y2[:, 512:1024], b2lf[:, 512:1024], ALU.add),
                        v.scalar_tensor_tensor(y2[:, 0:512], bank(4), mks[:, 4 + it4 % 2:5 + it4 % 2], y2[:, 0:512], ALU.mult, ALU.add),
                        v.scalar_tensor_tensor(y2[:, 512:1024], bank(5), mks[:, 4 + it4 % 2:5 + it4 % 2], y2[:, 512:1024], ALU.mult, ALU.add),
                        v.scalar_tensor_tensor(y2[:, 0:512], b2hf[:, 0:512], mks[:, 4 + it4 % 2:5 + it4 % 2], y2[:, 0:512], ALU.mult, ALU.add),
                        v.scalar_tensor_tensor(y2[:, 512:1024], b2hf[:, 512:1024], mks[:, 4 + it4 % 2:5 + it4 % 2], y2[:, 512:1024], ALU.mult, ALU.add)),
                        reads=(("pod", 0), ("pod", 1), ("podh", 0), ("podh", 1), "x1b", ("mk2", it4 % 2), "b2l", "b2h"), writes=("y2",))
                    p.step("vector", lambda v: _last(
                        v.tensor_tensor(sqf[:], y2[:], y2[:], ALU.mult),
                        v.reduce_sum(stt[:, 1:2], sqf[:], AXX),
                        v.reduce_sum(stt[:, 0:1], y2[:], AXX),
                        v.tensor_scalar(stt[:, 2:3], stt[:, 0:1], 1.0 / D, None, ALU.mult),
                        v.tensor_tensor(stt[:, 3:4], stt[:, 2:3], stt[:, 2:3], ALU.mult),
                        v.scalar_tensor_tensor(stt[:, 3:4], stt[:, 1:2], 1.0 / D, stt[:, 3:4], ALU.mult, ALU.subtract),
                        v.tensor_scalar(stt[:, 3:4], stt[:, 3:4], LN_EPS, None, ALU.add)),
                        reads=("y2",), writes=("st", "sqf"))
                    p.step("scalar", lambda s: s.activation(stt[:, 6:7], stt[:, 3:4], AF.Sqrt),
                           reads=("st",), writes=("stb",))
                    p.step("vector", lambda v: _last(
                        v.reciprocal(stt[:, 4:5], stt[:, 6:7]),
                        v.tensor_scalar(stt[:, 5:6], stt[:, 2:3], stt[:, 4:5], -1.0, ALU.mult, ALU.mult),
                        v.tensor_scalar(sqf[:], y2[:], stt[:, 4:5], stt[:, 5:6], ALU.mult, ALU.add)),
                        reads=("st", "stb", "y2"), writes=("x2t", "sqf"))
                    p.step("sync", lambda e, i=i: e.dma_start(out=x2o[i * 128:(i + 1) * 128, :], in_=sqf[:]),
                           reads=("x2t",), writes=(("x2d", i),), inc=16)
            p.step("gpsimd", lambda g: g.nop(), reads=tuple(("x2d", i) for i in range(32)))
            p.emit(stF)


_CACHE = {}


def _ident():
    return np.eye(128, dtype=np.float32)


def _sel2c():
    m = np.zeros((2, 128), np.float32)
    m[0, 0:64] = 1.0
    m[1, 64:128] = 1.0
    return m


def _b2c():
    m = np.zeros((128, 2), np.float32)
    m[0:64, 0] = 1.0
    m[64:128, 1] = 1.0
    return m


def _p1_inmaps(inputs):
    x = np.asarray(inputs["L0_attn_x"])
    ident = _ident()
    maps = []
    for core in range(8):
        b, h = core // 2, core % 2
        xo = np.ascontiguousarray(x[b, h * TOK:(h + 1) * TOK, :])
        xp = np.ascontiguousarray(x[b, (1 - h) * TOK:(2 - h) * TOK, :])
        maps.append({
            "xo": xo, "xp": xp,
            "wqT": np.ascontiguousarray(np.asarray(inputs["L0_attn_w_q"]).T),
            "wkT": np.ascontiguousarray(np.asarray(inputs["L0_attn_w_k"]).T),
            "wvT": np.ascontiguousarray(np.asarray(inputs["L0_attn_w_v"]).T),
            "woT": np.ascontiguousarray(np.asarray(inputs["L0_attn_w_o"]).T),
            "lng": np.asarray(inputs["L0_attn_ln_gamma"]),
            "lnb": np.asarray(inputs["L0_attn_ln_beta"]),
            "qg": np.asarray(inputs["L0_attn_q_gamma"]),
            "kg": np.asarray(inputs["L0_attn_k_gamma"]),
            "bo": np.asarray(inputs["L0_attn_b_o"]),
            "rtok": np.asarray(inputs["L0_ffn_routing_token"]),
            "identf": ident, "sel2c": _sel2c(), "b2c": _b2c(),
        })
    return maps


def run_p1(inputs):
    if "p1" not in _CACHE:
        _CACHE["p1"] = build_p1()
    res = run_bass_kernel_spmd(_CACHE["p1"], _p1_inmaps(inputs), core_ids=list(range(8)))
    return res.results


def build_p3():
    nc = bass.Bass("TRN2", target_bir_lowering=False)
    io = {}
    io["x2i"] = nc.dram_tensor("x2i", [TOK, D], F32, kind="ExternalInput").ap()
    for nm in ("WqT", "WkT", "WvT"):
        io[nm] = nc.dram_tensor(nm, [D, D], F32, kind="ExternalInput").ap()
    for nm in ("bq", "bk", "bv"):
        io[nm] = nc.dram_tensor(nm, [D], F32, kind="ExternalInput").ap()
    io["identf"] = nc.dram_tensor("identf", [128, 128], F32, kind="ExternalInput").ap()
    io["qf_dr"] = nc.dram_tensor("qf_dr", [16, 64, TOK], BF16, kind="ExternalOutput").ap()
    io["kf_dr"] = nc.dram_tensor("kf_dr", [16, 64, TOK], BF16, kind="ExternalOutput").ap()
    io["kt_dr"] = nc.dram_tensor("kt_dr", [TOK, D], BF16, kind="ExternalOutput").ap()
    io["vt_dr"] = nc.dram_tensor("vt_dr", [32, 128, 1040], BF16, kind="ExternalOutput").ap()
    io["state_o"] = nc.dram_tensor("state_o", [64, 1040], F32, kind="ExternalOutput").ap()
    emit_p3(nc, io)
    return nc


def emit_p3(nc, io):
    x2i, WqT, WkT, WvT = io["x2i"], io["WqT"], io["WkT"], io["WvT"]
    bq, bk, bv, identf = io["bq"], io["bk"], io["bv"], io["identf"]
    qf_dr, kf_dr, kt_dr, vt_dr, state_o = io["qf_dr"], io["kf_dr"], io["kt_dr"], io["vt_dr"], io["state_o"]

    stack = contextlib.ExitStack()
    with stack:
        sbt = lambda *a: stack.enter_context(nc.sbuf_tensor(*a))
        idf = sbt("idf3", [128, 128], F32)
        idb = sbt("idb3", [128, 128], BF16)
        w3q = sbt("w3q", [128, 8 * 1024], BF16)
        w3k = sbt("w3k", [128, 8 * 1024], BF16)
        w3v = sbt("w3v", [128, 8 * 1024], BF16)
        bqf = sbt("bqf", [128, D], F32)
        bkf = sbt("bkf", [128, D], F32)
        bvf = sbt("bvf", [128, D], F32)
        xt = sbt("xt3", [128, 2 * D], F32)
        xb16 = sbt("xb163", [128, 2 * D], BF16)
        x2p = sbt("x2p", [128, 2 * 8 * 128], BF16)
        raw = sbt("raw3", [128, 2 * D], F32)
        t0f = sbt("t0f", [128, 2 * D], F32)
        t1f = sbt("t1f", [128, 2 * D], F32)
        phb = sbt("phb", [128, 4 * D], BF16)
        stg = sbt("stg3", [128, 4 * D], BF16)
        vxs = sbt("vxs", [128, 2 * 1040], BF16)
        sst = sbt("sst", [64, 1040], F32)
        pool = stack.enter_context(nc.psum_tensor("pool3", [128, 4096], F32))
        p = Prog(nc, "p3")
        p.step("sync", lambda e: e.dma_start(out=idf[:], in_=identf), writes=("idf",), inc=16)
        p.step("vector", lambda v: _last(
            v.tensor_copy(idb[:], idf[:]),
            v.memset(sst[:], 0.0),
            v.memset(vxs[:].rearrange("p (b c t) -> p b c t", b=2, c=16)[:, :, :, 64:65], 1.0)),
            reads=("idf",), writes=("idb", "vxones", "sst0"))
        for wsb, wdr, tok in ((w3q, WqT, "wq"), (w3k, WkT, "wk"), (w3v, WvT, "wv")):
            p.step("gpsimd", lambda g, wsb=wsb, wdr=wdr: g.dma_start(
                out=wsb[:].rearrange("p (c j) -> p c j", c=8),
                in_=wdr.rearrange("(c p) j -> p c j", p=128)),
                writes=(tok,), inc=16)
        for bf, bdr, tok in ((bqf, bq, "bqf"), (bkf, bk, "bkf"), (bvf, bv, "bvf")):
            p.step("sync", lambda e, bf=bf, bdr=bdr: _repl(e, bf, bdr), writes=(tok,), inc=16)
        for i in range(32):
            ip = i % 2
            xti = xt[:, ip * D:(ip + 1) * D]
            xbi = xb16[:, ip * D:(ip + 1) * D]
            x2pi = x2p[:, ip * 1024:(ip + 1) * 1024]
            p.step("sync", lambda e, i=i, xti=xti: e.dma_start(out=xti, in_=x2i[i * 128:(i + 1) * 128, :]),
                   writes=(("xt", ip),), inc=16)
            p.step("vector", lambda v, xti=xti, xbi=xbi: v.tensor_copy(xbi, xti),
                   reads=(("xt", ip),), writes=(("xb", ip),))
            for cp in range(4):
                sl = cp % 2

                def trx(te, cp=cp, sl=sl, xbi=xbi):
                    te.transpose(pool[:, sl * 512: sl * 512 + 64].bitcast(BF16),
                                 xbi[:, 2 * cp * 128:(2 * cp + 1) * 128], idb[:])
                    return te.transpose(pool[:, sl * 512 + 64: sl * 512 + 128].bitcast(BF16),
                                        xbi[:, (2 * cp + 1) * 128:(2 * cp + 2) * 128], idb[:])
                p.step("tensor", trx, reads=(("xb", ip), "idb"), writes=(("tr", sl),))
                p.step("scalar", lambda s, cp=cp, sl=sl, x2pi=x2pi: s.copy(
                    x2pi[:, 2 * cp * 128:(2 * cp + 2) * 128],
                    pool[:, sl * 512: sl * 512 + 128].bitcast(BF16)),
                    reads=(("tr", sl),), writes=(("xp", ip, cp),))
            for pidx, (pj, wsb, bf, wtok, btok) in enumerate((
                    ("q", w3q, bqf, "wq", "bqf"),
                    ("v", w3v, bvf, "wv", "bvf"),
                    ("k", w3k, bkf, "wk", "bkf"))):
                pp = (3 * i + pidx) % 2
                rawp = raw[:, pp * D:(pp + 1) * D]
                t0p = t0f[:, pp * D:(pp + 1) * D]
                t1p = t1f[:, pp * D:(pp + 1) * D]
                php = phb[:, (2 * ip + (0 if pj == "q" else 1)) * D:
                           (2 * ip + (0 if pj == "q" else 1)) * D + D]
                for jh in range(2):
                    def mm_p3(te, wsb=wsb, jh=jh, x2pi=x2pi):
                        for kc in range(8):
                            mm = te.matmul(pool[:, (2 + jh) * 512:(3 + jh) * 512],
                                           x2pi[:, kc * 128:(kc + 1) * 128],
                                           wsb[:, kc * 1024 + jh * 512: kc * 1024 + jh * 512 + 512],
                                           start=(kc == 0), stop=(kc == 7))
                        return mm
                    p.step("tensor", mm_p3, reads=(wtok,) + tuple(("xp", ip, cp) for cp in range(4)),
                           writes=(("pb", jh),))
                p.step("vector", lambda v, bf=bf, rawp=rawp: v.tensor_tensor(
                    rawp, pool[:, 1024:2048], bf[:], ALU.add),
                    reads=(("pb", 0), ("pb", 1), btok), writes=(("raw", pp),))
                if pj == "v":
                    p.step("vector", lambda v, i=i, rawp=rawp: v.tensor_copy(
                        vxs[:, (i % 2) * 1040:(i % 2) * 1040 + 1040].rearrange("p (c t) -> p c t", c=16)[:, :, 0:64],
                        rawp.rearrange("p (c t) -> p c t", c=16)),
                        reads=(("raw", pp), "vxones"), writes=(("vx", i % 2),))
                    p.step("sync", lambda e, i=i: e.dma_start(
                        out=vt_dr[i], in_=vxs[:, (i % 2) * 1040:(i % 2) * 1040 + 1040]),
                        reads=(("vx", i % 2),), writes=(("vtd", i),), inc=16)
                else:
                    p.step("vector", lambda v, rawp=rawp, t0p=t0p: v.tensor_scalar(
                        t0p, rawp, 0.0, None, ALU.min),
                        reads=(("raw", pp),), writes=(("t0", pp),))
                    p.step("scalar", lambda s, t0p=t0p, t1p=t1p: s.activation(t1p, t0p, AF.Exp),
                           reads=(("t0", pp),), writes=(("t1", pp),))
                    p.step("vector", lambda v, rawp=rawp, t1p=t1p, php=php: v.scalar_tensor_tensor(
                        php, rawp, 0.0, t1p, ALU.max, ALU.add),
                        reads=(("raw", pp), ("t1", pp)), writes=(("ph", pj, ip),))
                    if pj == "k":
                        p.step("sync", lambda e, i=i, php=php: e.dma_start(
                            out=kt_dr[i * 128:(i + 1) * 128, :], in_=php),
                            reads=(("ph", pj, ip),), writes=(("ktd", i),), inc=16)
                        def mm_st(te, i=i, php=php):
                            for h in range(16):
                                mm = te.matmul(pool[0:64, 2048 + h * 128: 2048 + h * 128 + 65],
                                               php[:, h * 64:(h + 1) * 64],
                                               vxs[:, (i % 2) * 1040 + h * 65:(i % 2) * 1040 + (h + 1) * 65],
                                               start=True, stop=True)
                            return mm
                        p.step("tensor", mm_st, reads=(("ph", pj, ip), ("vx", i % 2)), writes=("stacc",))
                        p.step("vector", lambda v: v.tensor_tensor(
                            sst[:], sst[:],
                            pool[0:64, 2048:4096].rearrange("p (c t) -> p c t", c=16)[:, :, 0:65], ALU.add),
                            reads=("stacc", "sst0"), writes=("sstv",))
                    soff = (2 * ip + (0 if pj == "q" else 1)) * D
                    for cp in range(4):
                        sl = cp % 2

                        def trp(te, cp=cp, sl=sl, php=php):
                            te.transpose(pool[:, sl * 512: sl * 512 + 64].bitcast(BF16),
                                         php[:, 2 * cp * 128:(2 * cp + 1) * 128], idb[:])
                            return te.transpose(pool[:, sl * 512 + 64: sl * 512 + 128].bitcast(BF16),
                                                php[:, (2 * cp + 1) * 128:(2 * cp + 2) * 128], idb[:])
                        p.step("tensor", trp, reads=(("ph", pj, ip), "idb"), writes=(("tr", sl),))
                        p.step("scalar", lambda s, cp=cp, sl=sl, soff=soff: s.copy(
                            stg[:, soff + 2 * cp * 128:soff + (2 * cp + 2) * 128],
                            pool[:, sl * 512: sl * 512 + 128].bitcast(BF16)),
                            reads=(("tr", sl),), writes=(("st", pj, ip, cp),))
                    dst = qf_dr if pj == "q" else kf_dr
                    p.step("sync", lambda e, i=i, dst=dst, soff=soff: e.dma_start(
                        out=dst[:, :, i * 128:(i + 1) * 128].rearrange("(c e) q t -> (e q) c t", e=2),
                        in_=stg[:, soff:soff + D].rearrange("p (c t) -> p c t", c=8)),
                        reads=tuple(("st", pj, ip, cp) for cp in range(4)), writes=(("spl", pj, i),), inc=16)
        p.step("sync", lambda e: e.dma_start(out=state_o, in_=sst[:]), reads=("sstv",), writes=("std",), inc=16)
        p.step("gpsimd", lambda g: g.nop(),
               reads=("std",) + tuple(("spl", "q", i) for i in range(32)) + tuple(("spl", "k", i) for i in range(32)) +
                     tuple(("vtd", i) for i in range(32)) + tuple(("ktd", i) for i in range(32)))
        p.emit(stack)


def build_p4():
    nc = bass.Bass("TRN2", target_bir_lowering=False)
    io = {}
    io["x2i"] = nc.dram_tensor("x2i", [TOK, D], F32, kind="ExternalInput").ap()
    io["qf_dr"] = nc.dram_tensor("qf_dr", [16, 64, TOK], BF16, kind="ExternalInput").ap()
    io["kf_dr"] = nc.dram_tensor("kf_dr", [16, 64, TOK], BF16, kind="ExternalInput").ap()
    io["kt_dr"] = nc.dram_tensor("kt_dr", [TOK, D], BF16, kind="ExternalInput").ap()
    io["vt_dr"] = nc.dram_tensor("vt_dr", [32, 128, 1040], BF16, kind="ExternalInput").ap()
    io["state_prev"] = nc.dram_tensor("state_prev", [64, 1040], F32, kind="ExternalInput").ap()
    io["WoT"] = nc.dram_tensor("WoT", [D, D], F32, kind="ExternalInput").ap()
    io["bo4"] = nc.dram_tensor("bo4", [D], F32, kind="ExternalInput").ap()
    io["identf"] = nc.dram_tensor("identf", [128, 128], F32, kind="ExternalInput").ap()
    io["trilc"] = nc.dram_tensor("trilc", [128, 128], F32, kind="ExternalInput").ap()
    io["x3o"] = nc.dram_tensor("x3o", [TOK, D], F32, kind="ExternalOutput").ap()
    emit_p4(nc, io)
    return nc


def emit_p4(nc, io, state_emit=None):
    x2i, qf_dr, kf_dr, kt_dr, vt_dr = io["x2i"], io["qf_dr"], io["kf_dr"], io["kt_dr"], io["vt_dr"]
    state_prev, WoT, bo, identf, trilc = io["state_prev"], io["WoT"], io["bo4"], io["identf"], io["trilc"]
    x3o = io["x3o"]

    stack = contextlib.ExitStack()
    with stack:
        sbt = lambda *a: stack.enter_context(nc.sbuf_tensor(*a))
        idf = sbt("idf4", [128, 128], F32)
        trib = sbt("trib", [128, 128], BF16)
        onesf = sbt("onesf4", [1, 64], F32)
        wo_sb = sbt("wo4", [128, 8 * 1024], BF16)
        bof = sbt("bof4", [128, D], F32)
        qfc = sbt("qfc", [128, 2 * 1024], BF16)
        kfc = sbt("kfc", [128, 2 * 1024], BF16)
        ktc = sbt("ktc", [128, 2 * 1024], BF16)
        vtc = sbt("vtc", [128, 2 * 1040], BF16)
        state = sbt("state4", [64, 1040], F32)
        stb16 = sbt("stb164", [128, 1040], BF16)
        psb = sbt("psb4", [128, 2 * 256], BF16)
        rec1 = sbt("rec14", [1, 2 * 256], F32)
        den1 = sbt("den14", [128, 2 * 256], F32)
        an1 = sbt("an14", [64, 2 * 256], BF16)
        ao = sbt("ao4", [128, 8 * 128], BF16)
        xt = sbt("xt4", [128, D], F32)
        y2 = sbt("y24", [128, D], F32)
        sqf = sbt("sqf4", [128, D], F32)
        stt = sbt("stt4", [128, 8], F32)
        pool = stack.enter_context(nc.psum_tensor("pool4", [128, 4096], F32))
        p = Prog(nc, "p4")
        p.step("sync", lambda e: e.dma_start(out=idf[:], in_=identf), writes=("idf",), inc=16)
        p.step("gpsimd", lambda g: g.dma_start(out=trib[:], in_=trilc), writes=("tri",), inc=16)
        p.step("gpsimd", lambda g: g.dma_start(out=wo_sb[:].rearrange("p (c j) -> p c j", c=8),
                                               in_=WoT.rearrange("(c p) j -> p c j", p=128)),
               writes=("wo",), inc=16)
        p.step("sync", lambda e: _repl(e, bof, bo), writes=("bo",), inc=16)
        p.step("sync", lambda e: e.dma_start(out=state[:], in_=state_prev), writes=("state",), inc=16)
        p.step("vector", lambda v: _last(
            v.memset(onesf[:], 1.0),
            v.memset(stb16[:], 0.0)), writes=("onesf", "stbo", "stbe"))
        st4 = state.rearrange("k (hp e c) -> k hp e c", hp=8, e=2)

        def chunk_dmas(c):
            sl = c % 2
            for e2 in range(2):
                p.step("sync", lambda e, c=c, sl=sl, e2=e2: e.dma_start(
                    out=qfc[e2 * 64:(e2 + 1) * 64, sl * 1024: (sl + 1) * 1024].rearrange(
                        "k (hp t) -> k hp t", hp=8),
                    in_=qf_dr.rearrange("(hp e) k t -> e k hp t", e=2)[e2, :, :, c * 128:(c + 1) * 128]),
                    writes=(("qc", sl, e2),), inc=16)
                p.step("sync", lambda e, c=c, sl=sl, e2=e2: e.dma_start(
                    out=kfc[e2 * 64:(e2 + 1) * 64, sl * 1024: (sl + 1) * 1024].rearrange(
                        "k (hp t) -> k hp t", hp=8),
                    in_=kf_dr.rearrange("(hp e) k t -> e k hp t", e=2)[e2, :, :, c * 128:(c + 1) * 128]),
                    writes=(("kc", sl, e2),), inc=16)
            p.step("sync", lambda e, c=c, sl=sl: e.dma_start(
                out=ktc[:, sl * 1024: (sl + 1) * 1024], in_=kt_dr[c * 128:(c + 1) * 128, :]),
                writes=(("kt", sl),), inc=16)
            p.step("sync", lambda e, c=c, sl=sl: e.dma_start(
                out=vtc[:, sl * 1040: (sl + 1) * 1040], in_=vt_dr[c]),
                writes=(("vt", sl),), inc=16)

        def stb_copies():
            p.step("vector", lambda v: v.tensor_copy(
                stb16[0:64, 0:520].rearrange("k (hp c) -> k hp c", hp=8), st4[:, :, 0, :]),
                reads=("state",), writes=("stbe",))
            p.step("gpsimd", lambda g: g.dma_start(
                out=stb16[64:128, 520:1040].rearrange("k (hp c) -> k hp c", hp=8), in_=st4[:, :, 1, :]),
                reads=("state",), writes=("stbo",), inc=16)

        for c in range(32):
            sl = c % 2
            if c == 0:
                chunk_dmas(0)
                stb_copies()
            def stA(hp, sl=sl):
                def f(te, hp=hp, sl=sl):
                    for e2 in range(2):
                        mm = te.matmul(
                            pool[:, e2 * 512: e2 * 512 + 128],
                            kfc[e2 * 64:(e2 + 1) * 64, sl * 1024 + hp * 128: sl * 1024 + (hp + 1) * 128],
                            qfc[e2 * 64:(e2 + 1) * 64, sl * 1024 + hp * 128: sl * 1024 + (hp + 1) * 128],
                            start=True, stop=True, tile_position=(e2 * 64, 0))
                    return mm
                p.step("tensor", f, reads=(("qc", sl, 0), ("qc", sl, 1), ("kc", sl, 0), ("kc", sl, 1)),
                       writes=(("bank", 0), ("bank", 1)))

            def stB(hp):
                ps = (hp % 2) * 256
                def f(v, ps=ps):
                    for e2 in range(2):
                        mm = v.tensor_tensor(psb[:, ps + e2 * 128: ps + e2 * 128 + 128],
                                             pool[:, e2 * 512: e2 * 512 + 128], trib[:], ALU.mult)
                    return mm
                p.step("vector", f, reads=(("bank", 0), ("bank", 1), "tri"), writes=(("psb", hp % 2),))

            def stC(hp, sl=sl):
                nb = 2 + hp % 2
                nc0 = nb * 512
                ps = (hp % 2) * 256
                def f(te, hp=hp, sl=sl, nb=nb, nc0=nc0, ps=ps):
                    for e2 in range(2):
                        h = 2 * hp + e2
                        te.matmul(pool[0:65, nc0 + e2 * 256: nc0 + e2 * 256 + 128],
                                  vtc[:, sl * 1040 + h * 65: sl * 1040 + (h + 1) * 65],
                                  psb[:, ps + e2 * 128: ps + e2 * 128 + 128], start=True, stop=False)
                        te.matmul(pool[0:65, nc0 + e2 * 256: nc0 + e2 * 256 + 128],
                                  stb16[:, e2 * 520 + hp * 65: e2 * 520 + (hp + 1) * 65],
                                  qfc[:, sl * 1024 + hp * 128: sl * 1024 + (hp + 1) * 128],
                                  start=False, stop=True)
                    for e2 in range(2):
                        h = 2 * hp + e2
                        mm = te.matmul(pool[0:64, 2048 + h * 128: 2048 + h * 128 + 65],
                                       ktc[:, sl * 1024 + h * 64: sl * 1024 + (h + 1) * 64],
                                       vtc[:, sl * 1040 + h * 65: sl * 1040 + (h + 1) * 65],
                                       start=True, stop=True)
                    return mm
                p.step("tensor", f,
                       reads=(("psb", hp % 2), ("vt", sl), ("kt", sl), "stbe", "stbo",
                              ("qc", sl, 0), ("qc", sl, 1)),
                       writes=(("bank", nb), ("upd", 2 * hp), ("upd", 2 * hp + 1)))

            def stD(hp):
                nb = 2 + hp % 2
                nc0 = nb * 512
                ds = (hp % 2) * 256
                def f(s, nc0=nc0, ds=ds):
                    for e2 in range(2):
                        mm = s.activation(den1[64:65, ds + e2 * 128: ds + e2 * 128 + 128],
                                          pool[64:65, nc0 + e2 * 256: nc0 + e2 * 256 + 128],
                                          AF.Copy, bias=1e-6, scale=1.0)
                    return mm
                p.step("scalar", f, reads=(("bank", nb),), writes=(("den", hp % 2),))

            def stE(hp):
                ds = (hp % 2) * 256
                p.step("vector", lambda v, ds=ds: v.reciprocal(
                    rec1[0:1, ds:ds + 256], den1[64:65, ds:ds + 256]),
                    reads=(("den", hp % 2),), writes=(("rec", hp % 2),))

            def stF(hp):
                nb = 2 + hp % 2
                nc0 = nb * 512
                ds = (hp % 2) * 256
                def f(s, nc0=nc0, ds=ds):
                    for e2 in range(2):
                        mm = s.copy(an1[:, ds + e2 * 128: ds + e2 * 128 + 128],
                                    pool[0:64, nc0 + e2 * 256: nc0 + e2 * 256 + 128])
                    return mm
                p.step("scalar", f, reads=(("bank", nb),), writes=(("an1", hp % 2),))

            def stG(hp):
                nb = 2 + hp % 2
                nc0 = nb * 512
                ds = (hp % 2) * 256
                def f(te, nc0=nc0, ds=ds):
                    for e2 in range(2):
                        mm = te.matmul(pool[0:64, nc0 + e2 * 256 + 128: nc0 + e2 * 256 + 256],
                                       onesf[0:1, :], rec1[0:1, ds + e2 * 128: ds + e2 * 128 + 128],
                                       start=True, stop=True)
                    return mm
                p.step("tensor", f, reads=(("rec", hp % 2), "onesf", ("an1", hp % 2)),
                       writes=(("bank", nb),))

            def stH(hp):
                nb = 2 + hp % 2
                nc0 = nb * 512
                ds = (hp % 2) * 256
                def f(v, hp=hp, nc0=nc0, ds=ds):
                    for e2 in range(2):
                        mm = v.tensor_tensor(ao[e2 * 64:(e2 + 1) * 64, hp * 128:(hp + 1) * 128],
                                             an1[:, ds + e2 * 128: ds + e2 * 128 + 128],
                                             pool[0:64, nc0 + e2 * 256 + 128: nc0 + e2 * 256 + 256],
                                             ALU.mult)
                    return mm
                p.step("vector", f, reads=(("an1", hp % 2), ("bank", nb)),
                       writes=(("ao", 2 * hp), ("ao", 2 * hp + 1)))

            for hp in range(8):
                if not (c > 0 and hp < 2):
                    stA(hp)
                    stB(hp)
                if hp >= 1:
                    stC(hp - 1); stD(hp - 1); stE(hp - 1); stF(hp - 1)
                if hp >= 2:
                    stG(hp - 2); stH(hp - 2)
            stC(7); stD(7); stE(7); stF(7)
            stG(6); stH(6)
            stG(7); stH(7)
            if c + 1 < 32:
                chunk_dmas(c + 1)
                stA(0, sl=(c + 1) % 2)
                stB(0)
                stA(1, sl=(c + 1) % 2)
                stB(1)
            p.step("vector", lambda v: v.tensor_tensor(
                state[:], state[:], pool[0:64, 2048:4096].rearrange("p (c t) -> p c t", c=16)[:, :, 0:65],
                ALU.add),
                reads=("state",) + tuple(("upd", h) for h in range(16)), writes=("state",))
            for jh in range(2):
                def mm_o4(te, jh=jh):
                    for kc in range(8):
                        mm = te.matmul(pool[:, (4 + jh) * 512:(5 + jh) * 512],
                                       ao[:, kc * 128:(kc + 1) * 128],
                                       wo_sb[:, kc * 1024 + jh * 512: kc * 1024 + jh * 512 + 512],
                                       start=(kc == 0), stop=(kc == 7))
                    return mm
                p.step("tensor", mm_o4, reads=tuple(("ao", h) for h in range(16)) +
                       ("wo", "state"),
                       writes=(("bank", 4 + jh),))
            p.step("sync", lambda e, c=c: e.dma_start(out=xt[:], in_=x2i[c * 128:(c + 1) * 128, :]),
                   writes=("xt",), inc=16)
            p.step("vector", lambda v: _last(
                v.tensor_tensor(y2[:, 0:512], pool[:, 2048:2560], xt[:, 0:512], ALU.add),
                v.tensor_tensor(y2[:, 512:1024], pool[:, 2560:3072], xt[:, 512:1024], ALU.add),
                v.tensor_tensor(y2[:, 0:512], y2[:, 0:512], bof[:, 0:512], ALU.add),
                v.tensor_tensor(y2[:, 512:1024], y2[:, 512:1024], bof[:, 512:1024], ALU.add)),
                reads=(("bank", 4), ("bank", 5), "xt", "bo"), writes=("y2",))
            p.step("vector", lambda v: _last(
                v.tensor_tensor(sqf[:], y2[:], y2[:], ALU.mult),
                v.reduce_sum(stt[:, 1:2], sqf[:], AXX),
                v.reduce_sum(stt[:, 0:1], y2[:], AXX),
                v.tensor_scalar(stt[:, 2:3], stt[:, 0:1], 1.0 / D, None, ALU.mult),
                v.tensor_tensor(stt[:, 3:4], stt[:, 2:3], stt[:, 2:3], ALU.mult),
                v.scalar_tensor_tensor(stt[:, 3:4], stt[:, 1:2], 1.0 / D, stt[:, 3:4], ALU.mult, ALU.subtract),
                v.tensor_scalar(stt[:, 3:4], stt[:, 3:4], LN_EPS, None, ALU.add)),
                reads=("y2",), writes=("st", "sqf"))
            p.step("scalar", lambda s: s.activation(stt[:, 6:7], stt[:, 3:4], AF.Sqrt),
                   reads=("st",), writes=("stb2",))
            p.step("vector", lambda v: _last(
                v.reciprocal(stt[:, 4:5], stt[:, 6:7]),
                v.tensor_scalar(stt[:, 5:6], stt[:, 2:3], stt[:, 4:5], -1.0, ALU.mult, ALU.mult),
                v.tensor_scalar(sqf[:], y2[:], stt[:, 4:5], stt[:, 5:6], ALU.mult, ALU.add)),
                reads=("st", "stb2", "y2"), writes=("x3t", "sqf"))
            p.step("sync", lambda e, c=c: e.dma_start(out=x3o[c * 128:(c + 1) * 128, :], in_=sqf[:]),
                   reads=("x3t",), writes=(("x3d", c),), inc=16)
            if c + 1 < 32:
                stb_copies()
        p.step("gpsimd", lambda g: g.nop(), reads=tuple(("x3d", c) for c in range(32)))
        p.emit(stack)


def build_p5():
    nc = bass.Bass("TRN2", target_bir_lowering=False)
    io = {}
    io["x3i"] = nc.dram_tensor("x3i", [TOK, D], F32, kind="ExternalInput").ap()
    io["Wr"] = nc.dram_tensor("Wr", [D, 16], F32, kind="ExternalInput").ap()
    io["Wg"] = nc.dram_tensor("Wg", [16, D, 512], F32, kind="ExternalInput").ap()
    io["Wu"] = nc.dram_tensor("Wu", [16, D, 512], F32, kind="ExternalInput").ap()
    io["Wd"] = nc.dram_tensor("Wd", [16, 512, D], F32, kind="ExternalInput").ap()
    io["Sg"] = nc.dram_tensor("Sg", [D, D], F32, kind="ExternalInput").ap()
    io["Su"] = nc.dram_tensor("Su", [D, D], F32, kind="ExternalInput").ap()
    io["Sd"] = nc.dram_tensor("Sd", [D, D], F32, kind="ExternalInput").ap()
    io["identf"] = nc.dram_tensor("identf", [128, 128], F32, kind="ExternalInput").ap()
    io["sel16c"] = nc.dram_tensor("sel16c", [16, 2048], F32, kind="ExternalInput").ap()
    io["outo"] = nc.dram_tensor("outo", [TOK, D], F32, kind="ExternalOutput").ap()
    emit_p5(nc, io)
    return nc


def emit_p5(nc, io):
    x3i, Wr, Wg, Wu, Wd = io["x3i"], io["Wr"], io["Wg"], io["Wu"], io["Wd"]
    Sg, Su, Sd, identf, sel16c = io["Sg"], io["Su"], io["Sd"], io["identf"], io["sel16c"]
    outo = io["outo"]
    xf_dr = nc.dram_tensor("xf_dr5", [8, 128, TOK], BF16).ap()

    stack = contextlib.ExitStack()
    with stack:
        sbt = lambda *a: stack.enter_context(nc.sbuf_tensor(*a))
        idf = sbt("idf5", [128, 128], F32)
        idb = sbt("idb5", [128, 128], BF16)
        onesb1 = sbt("onesb15", [1, 128], BF16)
        acc = sbt("acc5", [128, 32 * 1024], BF16)
        cwT = sbt("cwT5", [16, TOK], BF16)
        sel16b = sbt("sel16b", [16, 2048], BF16)
        pool = stack.enter_context(nc.psum_tensor("pool5", [128, 4096], F32))

        with contextlib.ExitStack() as stP:
            s5 = lambda *a: stP.enter_context(nc.sbuf_tensor(*a))
            wrf = s5("wrf", [128, 8 * 16], F32)
            wrb = s5("wrb", [128, 8 * 16], BF16)
            wrl = s5("wrl", [128, 8 * 16], BF16)
            wrt = s5("wrt", [128, 8 * 16], F32)
            xt = s5("xt5", [128, 2 * D], F32)
            xhi = s5("xhi5", [128, 2 * D], BF16)
            xlo = s5("xlo5", [128, 2 * D], BF16)
            stg = s5("stg5", [128, 2 * D], BF16)
            xlp = s5("xlp5", [128, 2 * D], BF16)
            prb = s5("prb5", [128, 2 * 16], F32)
            f1 = s5("f15", [128, 2 * 16], F32)
            w2 = s5("w25", [128, 2 * 16], F32)
            sc = s5("sc5", [128, 2 * 16], F32)
            onesd = s5("onesd5", [128, 16], F32)
            cwb = s5("cwb5", [128, 2 * 16], BF16)
            rr = s5("rr5", [128, 2 * 4], F32)
            p = Prog(nc, "p5a")
            p.step("sync", lambda e: e.dma_start(out=idf[:], in_=identf), writes=("idf",), inc=16)
            p.step("sync", lambda e: e.dma_start(out=wrf[:].rearrange("p (c j) -> p c j", c=8),
                                                 in_=Wr.rearrange("(c p) j -> p c j", p=128)),
                   writes=("wrf",), inc=16)
            p.step("gpsimd", lambda g: g.dma_start(out=sel16b[:], in_=sel16c), writes=("sel16",), inc=16)
            p.step("vector", lambda v: _last(
                v.tensor_copy(idb[:], idf[:]),
                v.memset(onesb1[:], 1.0),
                v.memset(onesd[:], 1.0),
                v.tensor_copy(wrb[:], wrf[:]),
                v.tensor_copy(wrt[:], wrb[:]),
                v.tensor_tensor(wrt[:], wrf[:], wrt[:], ALU.subtract),
                v.tensor_copy(wrl[:], wrt[:])),
                reads=("idf", "wrf"), writes=("idb", "wr2"))
            for i in range(32):
                pr = i % 2
                xti = xt[:, pr * D:(pr + 1) * D]
                xhii = xhi[:, pr * D:(pr + 1) * D]
                xloi = xlo[:, pr * D:(pr + 1) * D]
                xlpi = xlp[:, pr * D:(pr + 1) * D]
                prbi = prb[:, pr * 16:(pr + 1) * 16]
                f1i = f1[:, pr * 16:(pr + 1) * 16]
                w2i = w2[:, pr * 16:(pr + 1) * 16]
                sci = sc[:, pr * 16:(pr + 1) * 16]
                cwbi = cwb[:, pr * 16:(pr + 1) * 16]
                rri = rr[:, pr * 4:(pr + 1) * 4]
                lgb = pool[:, 2048 + pr * 1024:2064 + pr * 1024]
                ctb = pool[0:16, 2560 + pr * 1024:2624 + pr * 1024]
                p.step("sync", lambda e, i=i, xti=xti: e.dma_start(out=xti, in_=x3i[i * 128:(i + 1) * 128, :]),
                       writes=(("xt", pr),), inc=16)
                p.step("vector", lambda v, xti=xti, xhii=xhii, xloi=xloi: _last(
                    v.tensor_copy(xhii, xti),
                    v.tensor_tensor(xloi, xti, xhii, ALU.subtract)),
                    reads=(("xt", pr),), writes=(("xhi", pr), ("xlo", pr)))
                for cp in range(4):
                    sl = cp % 2

                    def trh(te, cp=cp, sl=sl, xhii=xhii):
                        te.transpose(pool[:, sl * 512: sl * 512 + 64].bitcast(BF16),
                                     xhii[:, 2 * cp * 128:(2 * cp + 1) * 128], idb[:])
                        return te.transpose(pool[:, sl * 512 + 64: sl * 512 + 128].bitcast(BF16),
                                            xhii[:, (2 * cp + 1) * 128:(2 * cp + 2) * 128], idb[:])
                    p.step("tensor", trh, reads=(("xhi", pr), "idb"), writes=(("tr", sl),))
                    p.step("scalar", lambda s, cp=cp, i=i, sl=sl: s.copy(
                        stg[:, (i % 2) * D + 2 * cp * 128:(i % 2) * D + (2 * cp + 2) * 128],
                        pool[:, sl * 512: sl * 512 + 128].bitcast(BF16)),
                        reads=(("tr", sl),), writes=(("sg", i % 2, cp),))

                    def trl(te, cp=cp, sl=sl, xloi=xloi):
                        te.transpose(pool[:, (2 + sl) * 512: (2 + sl) * 512 + 64].bitcast(BF16),
                                     xloi[:, 2 * cp * 128:(2 * cp + 1) * 128], idb[:])
                        return te.transpose(pool[:, (2 + sl) * 512 + 64: (2 + sl) * 512 + 128].bitcast(BF16),
                                            xloi[:, (2 * cp + 1) * 128:(2 * cp + 2) * 128], idb[:])
                    p.step("tensor", trl, reads=(("xlo", pr), "idb"), writes=(("trl", sl),))
                    p.step("scalar", lambda s, cp=cp, sl=sl, xlpi=xlpi: s.copy(
                        xlpi[:, 2 * cp * 128:(2 * cp + 2) * 128],
                        pool[:, (2 + sl) * 512: (2 + sl) * 512 + 128].bitcast(BF16)),
                        reads=(("trl", sl),), writes=(("lp", pr, cp),))
                p.step("sync", lambda e, i=i: e.dma_start(
                    out=xf_dr[:, :, i * 128:(i + 1) * 128].rearrange("c p t -> p c t"),
                    in_=stg[:, (i % 2) * D:(i % 2 + 1) * D].rearrange("p (c t) -> p c t", c=8)),
                    reads=tuple(("sg", i % 2, cp) for cp in range(4)), writes=(("spl", i),), inc=16)

                def mm_lg(te, i=i, lgb=lgb, xlpi=xlpi):
                    first = True
                    mm = None
                    for kc in range(8):
                        sgc = stg[:, (i % 2) * D + kc * 128:(i % 2) * D + (kc + 1) * 128]
                        mm = te.matmul(lgb, sgc, wrb[:, kc * 16:(kc + 1) * 16],
                                       start=first, stop=False); first = False
                        mm = te.matmul(lgb, sgc, wrl[:, kc * 16:(kc + 1) * 16],
                                       start=False, stop=False)
                        mm = te.matmul(lgb, xlpi[:, kc * 128:(kc + 1) * 128],
                                       wrb[:, kc * 16:(kc + 1) * 16], start=False, stop=(kc == 7))
                    return mm
                p.step("tensor", mm_lg, reads=tuple(("sg", i % 2, cp) for cp in range(4)) +
                       tuple(("lp", pr, cp) for cp in range(4)) + ("wr2",), writes=(("lg", pr),))
                p.step("vector", lambda v, rri=rri, lgb=lgb: _last(
                    v.reduce_max(rri[:, 0:1], lgb, AXX),
                    v.tensor_scalar(rri[:, 1:2], rri[:, 0:1], -1.0, None, ALU.mult)),
                    reads=(("lg", pr),), writes=(("rrm", pr),))
                p.step("scalar", lambda s, prbi=prbi, rri=rri, lgb=lgb: s.activation(
                    prbi, lgb, AF.Exp, bias=rri[:, 1:2]),
                    reads=(("lg", pr), ("rrm", pr)), writes=(("prb", pr),))
                p.step("vector", lambda v, prbi=prbi, f1i=f1i, w2i=w2i, sci=sci, cwbi=cwbi, rri=rri: _last(
                    v.reduce_sum(rri[:, 2:3], prbi, AXX),
                    v.reciprocal(rri[:, 3:4], rri[:, 2:3]),
                    v.tensor_scalar(prbi, prbi, rri[:, 3:4], None, ALU.mult),
                    v.reduce_max(rri[:, 0:1], prbi, AXX),
                    v.tensor_scalar(f1i, prbi, rri[:, 0:1], None, ALU.is_equal),
                    v.tensor_tensor_scan(sci, onesd[:], f1i, 0.0, ALU.mult, ALU.add),
                    v.tensor_tensor(sci, sci, f1i, ALU.subtract),
                    v.tensor_scalar(sci, sci, 0.0, None, ALU.is_equal),
                    v.tensor_tensor(f1i, f1i, sci, ALU.mult),
                    v.tensor_scalar(w2i, f1i, -1.0, 1.0, ALU.mult, ALU.add),
                    v.tensor_tensor(w2i, prbi, w2i, ALU.mult),
                    v.reduce_max(rri[:, 0:1], w2i, AXX),
                    v.tensor_scalar(w2i, w2i, rri[:, 0:1], None, ALU.is_equal),
                    v.tensor_tensor_scan(sci, onesd[:], w2i, 0.0, ALU.mult, ALU.add),
                    v.tensor_tensor(sci, sci, w2i, ALU.subtract),
                    v.tensor_scalar(sci, sci, 0.0, None, ALU.is_equal),
                    v.tensor_tensor(w2i, w2i, sci, ALU.mult),
                    v.tensor_tensor(f1i, f1i, w2i, ALU.add),
                    v.tensor_tensor(f1i, prbi, f1i, ALU.mult),
                    v.tensor_copy(cwbi, f1i)),
                    reads=(("prb", pr),), writes=(("cw", pr),))
                p.step("tensor", lambda te, cwbi=cwbi, ctb=ctb: te.transpose(
                    ctb.bitcast(BF16), cwbi, idb[:]),
                    reads=(("cw", pr), "idb"), writes=(("ct", pr),))
                p.step("vector", lambda v, i=i, ctb=ctb: v.tensor_copy(
                    cwT[:, i * 128:(i + 1) * 128], ctb.bitcast(BF16)),
                    reads=(("ct", pr),), writes=(("cwt", i),))
            p.emit(stP)

        with contextlib.ExitStack() as stE:
            s5 = lambda *a: stE.enter_context(nc.sbuf_tensor(*a))
            wge = s5("wge", [128, 2 * 8 * 512], BF16)
            wue = s5("wue", [128, 2 * 8 * 512], BF16)
            wde = s5("wde", [128, 2 * 4 * 1024], BF16)
            xfg = s5("xfg", [128, 2 * 8 * 512], BF16)
            gsil = s5("gsil", [128, 512], BF16)
            tmu = s5("tmu", [128, 512], BF16)
            h5 = s5("h55", [128, 4 * 512], BF16)
            p = Prog(nc, "p5b")
            p.step("vector", lambda v: v.memset(acc[:], 0.0), writes=("acc_all",))
            for e in range(16):
                esl = e % 2
                p.step("gpsimd", lambda g, e=e, esl=esl: g.dma_start(
                    out=wge[:, esl * 4096:(esl + 1) * 4096].rearrange("p (c j) -> p c j", c=8),
                    in_=Wg[e].rearrange("(c p) j -> p c j", p=128)),
                    writes=(("wg", esl),), inc=16)
                p.step("gpsimd", lambda g, e=e, esl=esl: g.dma_start(
                    out=wue[:, esl * 4096:(esl + 1) * 4096].rearrange("p (c j) -> p c j", c=8),
                    in_=Wu[e].rearrange("(c p) j -> p c j", p=128)),
                    writes=(("wu", esl),), inc=16)
                p.step("gpsimd", lambda g, e=e, esl=esl: g.dma_start(
                    out=wde[:, esl * 4096:(esl + 1) * 4096].rearrange("p (c j) -> p c j", c=4),
                    in_=Wd[e].rearrange("(c p) j -> p c j", p=128)),
                    writes=(("wd", esl),), inc=16)
                for t5 in range(8):
                    tsl = t5 % 2
                    p.step("sync", lambda e_, t5=t5, tsl=tsl: e_.dma_start(
                        out=xfg[:, tsl * 4096:(tsl + 1) * 4096].rearrange("p (c t) -> p c t", c=8),
                        in_=xf_dr[:, :, t5 * 512:(t5 + 1) * 512].rearrange("c p t -> p c t")),
                        writes=(("xg", tsl),), inc=16)
                    p.step("tensor", lambda te, e=e, t5=t5: te.matmul(
                        pool[:, 2560:3072], sel16b[:, e * 128:(e + 1) * 128], cwT[:, t5 * 512:(t5 + 1) * 512],
                        start=True, stop=True),
                        reads=tuple(("cwt", i) for i in range(32)) + ("sel16",), writes=(("cb",),))
                    for jt in range(4):
                        def mm_g(te, jt=jt, esl=esl, tsl=tsl, w=wge):
                            for kc in range(8):
                                mm = te.matmul(pool[:, (jt % 2) * 512:(jt % 2 + 1) * 512],
                                               w[:, esl * 4096 + kc * 512 + jt * 128: esl * 4096 + kc * 512 + jt * 128 + 128],
                                               xfg[:, tsl * 4096 + kc * 512: tsl * 4096 + (kc + 1) * 512],
                                               start=(kc == 0), stop=(kc == 7))
                            return mm
                        p.step("tensor", mm_g, reads=(("wg", esl), ("xg", tsl)), writes=(("gb", jt % 2),))
                        p.step("scalar", lambda s, jt=jt: s.activation(gsil[:], pool[:, (jt % 2) * 512:(jt % 2 + 1) * 512], AF.Silu),
                               reads=(("gb", jt % 2),), writes=("gsil",))
                        def mm_u(te, jt=jt, esl=esl, tsl=tsl, w=wue):
                            for kc in range(8):
                                mm = te.matmul(pool[:, 1024 + (jt % 2) * 512:1024 + (jt % 2 + 1) * 512],
                                               w[:, esl * 4096 + kc * 512 + jt * 128: esl * 4096 + kc * 512 + jt * 128 + 128],
                                               xfg[:, tsl * 4096 + kc * 512: tsl * 4096 + (kc + 1) * 512],
                                               start=(kc == 0), stop=(kc == 7))
                            return mm
                        p.step("tensor", mm_u, reads=(("wu", esl), ("xg", tsl)), writes=(("ub", jt % 2),))
                        p.step("vector", lambda v, jt=jt: v.tensor_tensor(
                            tmu[:], gsil[:], pool[:, 1024 + (jt % 2) * 512:1024 + (jt % 2 + 1) * 512], ALU.mult),
                            reads=("gsil", ("ub", jt % 2)), writes=("tmu",))
                        p.step("vector", lambda v, jt=jt: v.tensor_tensor(
                            h5[:, jt * 512:(jt + 1) * 512], tmu[:], pool[:, 2560:3072], ALU.mult),
                            reads=("tmu", ("cb",)), writes=(("h5", jt),))
                    for it4 in range(4):
                        for jh in range(2):
                            def mm_d(te, it4=it4, jh=jh, esl=esl):
                                for kc in range(4):
                                    mm = te.matmul(pool[:, 3072 + jh * 512:3072 + (jh + 1) * 512],
                                                   h5[:, kc * 512 + it4 * 128: kc * 512 + it4 * 128 + 128],
                                                   wde[:, esl * 4096 + kc * 1024 + jh * 512: esl * 4096 + kc * 1024 + jh * 512 + 512],
                                                   start=(kc == 0), stop=(kc == 3))
                                return mm
                            p.step("tensor", mm_d, reads=tuple(("h5", j) for j in range(4)) + (("wd", esl),),
                                   writes=(("db", jh),))
                            p.step("vector", lambda v, t5=t5, it4=it4, jh=jh: v.tensor_tensor(
                                acc[:, (t5 * 4 + it4) * 1024 + jh * 512:(t5 * 4 + it4) * 1024 + (jh + 1) * 512],
                                acc[:, (t5 * 4 + it4) * 1024 + jh * 512:(t5 * 4 + it4) * 1024 + (jh + 1) * 512],
                                pool[:, 3072 + jh * 512:3072 + (jh + 1) * 512], ALU.add),
                                reads=(("db", jh), "acc_all"), writes=(("acc", t5, it4, jh),))
            p.emit(stE)

        with contextlib.ExitStack() as stS:
            s5 = lambda *a: stS.enter_context(nc.sbuf_tensor(*a))
            sg_sb = s5("sg_sb", [128, 8 * 1024], BF16)
            su_sb = s5("su_sb", [128, 8 * 1024], BF16)
            sd_sb = s5("sd_sb", [128, 8 * 1024], BF16)
            xfg = s5("xfg2", [128, 2 * 8 * 512], BF16)
            gsil = s5("gsil2", [128, 2 * 512], BF16)
            tmu = s5("tmu2", [128, 512], BF16)
            hs = s5("hs5", [128, 2 * 8 * 512], BF16)
            xt = s5("xt5b", [128, 2 * D], F32)
            y2 = s5("y25", [128, 2 * D], F32)
            sqf = s5("sqf5", [128, 2 * D], F32)
            stt = s5("stt5", [128, 16], F32)
            p = Prog(nc, "p5c")
            for nm, wsb, wdr in (("sg", sg_sb, Sg), ("su", su_sb, Su), ("sd", sd_sb, Sd)):
                p.step("gpsimd", lambda g, wsb=wsb, wdr=wdr: g.dma_start(
                    out=wsb[:].rearrange("p (c j) -> p c j", c=8),
                    in_=wdr.rearrange("(c p) j -> p c j", p=128)),
                    writes=(nm,), inc=16)
            for t5 in range(8):
                tsl = t5 % 2
                p.step("sync", lambda e_, t5=t5, tsl=tsl: e_.dma_start(
                    out=xfg[:, tsl * 4096:(tsl + 1) * 4096].rearrange("p (c t) -> p c t", c=8),
                    in_=xf_dr[:, :, t5 * 512:(t5 + 1) * 512].rearrange("c p t -> p c t")),
                    writes=(("xg", tsl),), inc=16)
                for jt in range(8):
                    gsl = gsil[:, (jt % 2) * 512:(jt % 2 + 1) * 512]
                    def mm_g(te, jt=jt, tsl=tsl):
                        for kc in range(8):
                            mm = te.matmul(pool[:, (jt % 2) * 512:(jt % 2 + 1) * 512],
                                           sg_sb[:, kc * 1024 + jt * 128: kc * 1024 + jt * 128 + 128],
                                           xfg[:, tsl * 4096 + kc * 512: tsl * 4096 + (kc + 1) * 512],
                                           start=(kc == 0), stop=(kc == 7))
                        return mm
                    p.step("tensor", mm_g, reads=("sg", ("xg", tsl)), writes=(("gb", jt % 2),))
                    p.step("scalar", lambda s, jt=jt, gsl=gsl: s.activation(
                        gsl, pool[:, (jt % 2) * 512:(jt % 2 + 1) * 512], AF.Silu),
                        reads=(("gb", jt % 2),), writes=(("gsil", jt % 2),))
                    def mm_u(te, jt=jt, tsl=tsl):
                        for kc in range(8):
                            mm = te.matmul(pool[:, 1024 + (jt % 2) * 512:1024 + (jt % 2 + 1) * 512],
                                           su_sb[:, kc * 1024 + jt * 128: kc * 1024 + jt * 128 + 128],
                                           xfg[:, tsl * 4096 + kc * 512: tsl * 4096 + (kc + 1) * 512],
                                           start=(kc == 0), stop=(kc == 7))
                        return mm
                    p.step("tensor", mm_u, reads=("su", ("xg", tsl)), writes=(("ub", jt % 2),))
                    p.step("vector", lambda v, jt=jt, gsl=gsl, tsl=tsl: v.tensor_tensor(
                        hs[:, tsl * 4096 + jt * 512:tsl * 4096 + (jt + 1) * 512], gsl,
                        pool[:, 1024 + (jt % 2) * 512:1024 + (jt % 2 + 1) * 512], ALU.mult),
                        reads=(("gsil", jt % 2), ("ub", jt % 2)), writes=(("hs", tsl, jt),))
                for it4 in range(4):
                    i = t5 * 4 + it4
                    pr = it4 % 2
                    db0 = 4 + 2 * pr
                    xti = xt[:, pr * D:(pr + 1) * D]
                    y2i = y2[:, pr * D:(pr + 1) * D]
                    sqi = sqf[:, pr * D:(pr + 1) * D]
                    st8 = stt[:, pr * 8:pr * 8 + 8]
                    for jh in range(2):
                        def mm_d(te, it4=it4, jh=jh, db0=db0, tsl=tsl):
                            for kc in range(8):
                                mm = te.matmul(pool[:, (db0 + jh) * 512:(db0 + jh + 1) * 512],
                                               hs[:, tsl * 4096 + kc * 512 + it4 * 128:
                                                  tsl * 4096 + kc * 512 + it4 * 128 + 128],
                                               sd_sb[:, kc * 1024 + jh * 512: kc * 1024 + jh * 512 + 512],
                                               start=(kc == 0), stop=(kc == 7))
                            return mm
                        p.step("tensor", mm_d, reads=tuple(("hs", tsl, j) for j in range(8)) + ("sd",),
                               writes=(("db", pr, jh),))
                    p.step("sync", lambda e_, i=i, xti=xti: e_.dma_start(
                        out=xti, in_=x3i[i * 128:(i + 1) * 128, :]),
                        writes=(("xt", pr),), inc=16)
                    p.step("vector", lambda v, i=i, pr=pr, db0=db0, xti=xti, y2i=y2i: _last(
                        v.tensor_tensor(y2i[:, 0:512], pool[:, db0 * 512:(db0 + 1) * 512], xti[:, 0:512], ALU.add),
                        v.tensor_tensor(y2i[:, 512:1024], pool[:, (db0 + 1) * 512:(db0 + 2) * 512], xti[:, 512:1024], ALU.add),
                        v.tensor_tensor(y2i[:, 0:512], y2i[:, 0:512], acc[:, i * 1024: i * 1024 + 512], ALU.add),
                        v.tensor_tensor(y2i[:, 512:1024], y2i[:, 512:1024], acc[:, i * 1024 + 512:(i + 1) * 1024], ALU.add)),
                        reads=(("db", pr, 0), ("db", pr, 1), ("xt", pr)), writes=(("y2", pr),))
                    p.step("vector", lambda v, y2i=y2i, sqi=sqi, st8=st8: _last(
                        v.tensor_tensor(sqi, y2i, y2i, ALU.mult),
                        v.reduce_sum(st8[:, 1:2], sqi, AXX),
                        v.reduce_sum(st8[:, 0:1], y2i, AXX),
                        v.tensor_scalar(st8[:, 2:3], st8[:, 0:1], 1.0 / D, None, ALU.mult),
                        v.tensor_tensor(st8[:, 3:4], st8[:, 2:3], st8[:, 2:3], ALU.mult),
                        v.scalar_tensor_tensor(st8[:, 3:4], st8[:, 1:2], 1.0 / D, st8[:, 3:4], ALU.mult, ALU.subtract),
                        v.tensor_scalar(st8[:, 3:4], st8[:, 3:4], LN_EPS, None, ALU.add)),
                        reads=(("y2", pr),), writes=(("st", pr), ("sqf", pr)))
                    p.step("scalar", lambda s, st8=st8: s.activation(st8[:, 6:7], st8[:, 3:4], AF.Sqrt),
                           reads=(("st", pr),), writes=(("stb", pr),))
                    p.step("vector", lambda v, y2i=y2i, sqi=sqi, st8=st8: _last(
                        v.reciprocal(st8[:, 4:5], st8[:, 6:7]),
                        v.tensor_scalar(st8[:, 5:6], st8[:, 2:3], st8[:, 4:5], -1.0, ALU.mult, ALU.mult),
                        v.tensor_scalar(sqi, y2i, st8[:, 4:5], st8[:, 5:6], ALU.mult, ALU.add)),
                        reads=(("st", pr), ("stb", pr), ("y2", pr)), writes=(("xot", pr), ("sqf", pr)))
                    p.step("sync", lambda e_, i=i, sqi=sqi: e_.dma_start(
                        out=outo[i * 128:(i + 1) * 128, :], in_=sqi),
                        reads=(("xot", pr),), writes=(("od", i),), inc=16)
            p.step("gpsimd", lambda g: g.nop(), reads=tuple(("od", i) for i in range(32)))
            p.emit(stS)


class _NullBlk:
    def gpsimd(self, f):
        pass

    def sync(self, f):
        pass

    def vector(self, f):
        pass

    def scalar(self, f):
        pass

    def tensor(self, f):
        pass


def build_full():
    nc = bass.Bass("TRN2", target_bir_lowering=False)
    io = {}
    for nm, shape in (("xo", [TOK, D]), ("xp", [TOK, D]), ("wqT", [D, D]), ("wkT", [D, D]),
                      ("wvT", [D, D]), ("woT", [D, D]), ("lng", [D]), ("lnb", [D]),
                      ("qg", [DH]), ("kg", [DH]), ("bo", [D]), ("rtok", [D]),
                      ("identf", [128, 128]), ("sel2c", [2, 128]), ("b2c", [128, 2]),
                      ("bsel", [128, 4]), ("wsel", [64, 8]), ("sel16c", [16, 2048]),
                      ("trilc", [128, 128])):
        io[nm] = nc.dram_tensor(nm, shape, F32, kind="ExternalInput").ap()
    for nm, shape in P2_W:
        io[nm] = nc.dram_tensor(nm, shape, F32, kind="ExternalInput").ap()
    for nm in ("WqT", "WkT", "WvT", "WoT"):
        io[nm] = nc.dram_tensor(nm, [D, D], F32, kind="ExternalInput").ap()
    for nm in ("bq", "bk", "bv", "bo4"):
        io[nm] = nc.dram_tensor(nm, [D], F32, kind="ExternalInput").ap()
    io["Wr"] = nc.dram_tensor("Wr", [D, 16], F32, kind="ExternalInput").ap()
    io["Wg"] = nc.dram_tensor("Wg", [16, D, 512], F32, kind="ExternalInput").ap()
    io["Wu"] = nc.dram_tensor("Wu", [16, D, 512], F32, kind="ExternalInput").ap()
    io["Wd"] = nc.dram_tensor("Wd", [16, 512, D], F32, kind="ExternalInput").ap()
    io["Sg"] = nc.dram_tensor("Sg", [D, D], F32, kind="ExternalInput").ap()
    io["Su"] = nc.dram_tensor("Su", [D, D], F32, kind="ExternalInput").ap()
    io["Sd"] = nc.dram_tensor("Sd", [D, D], F32, kind="ExternalInput").ap()
    x1_dr = nc.dram_tensor("x1_dr", [TOK, D], F32).ap()
    s_own = nc.dram_tensor("s_own", [TOK], F32).ap()
    ag_s = nc.dram_tensor("ag_s", [4, S], F32).ap()
    sfull_dr = nc.dram_tensor("sfull_dr", [S], F32).ap()
    x2_dr = nc.dram_tensor("x2_dr", [TOK, D], F32).ap()
    qf3 = nc.dram_tensor("qf3", [16, 64, TOK], BF16).ap()
    kf3 = nc.dram_tensor("kf3", [16, 64, TOK], BF16).ap()
    kt3 = nc.dram_tensor("kt3", [TOK, D], BF16).ap()
    vt3 = nc.dram_tensor("vt3", [32, 128, 1040], BF16).ap()
    st3 = nc.dram_tensor("st3", [64, 1040], F32).ap()
    ag_st = nc.dram_tensor("ag_st", [8, 64, 1040], F32).ap()
    stprev_dr = nc.dram_tensor("stprev_dr", [64, 1040], F32).ap()
    x3_dr = nc.dram_tensor("x3_dr", [TOK, D], F32).ap()
    io["outo"] = nc.dram_tensor("outo", [TOK, D], F32, kind="ExternalOutput").ap()

    _FP = os.environ.get("FULL_PHASES", "1,ag1,2,3,ag2,4,5").split(",")
    io1 = dict(io)
    io1["x1o"] = x1_dr
    io1["so"] = s_own
    if "1" in _FP:
        emit_p1(nc, io1)

    with contextlib.ExitStack() as stG:
        s4 = stG.enter_context(nc.sbuf_tensor("s4g", [128, 4 * 64], F32))
        bsl = stG.enter_context(nc.sbuf_tensor("bslg", [128, 4], F32))
        ssel = stG.enter_context(nc.sbuf_tensor("sselg", [128, 64], F32))
        blk = stG.enter_context(nc.Block())
        sem = stG.enter_context(nc.semaphore("ags_sem"))
        csem = stG.enter_context(nc.semaphore("agc_sem"))
        if "ag1" not in _FP:
            blk = _NullBlk()

        @blk.gpsimd
        def _(g):
            g.collective_compute("AllGather", ALU.bypass, [[0, 1, 2, 3, 4, 5, 6, 7]],
                                 [s_own.rearrange("(a b) -> a b", b=128)],
                                 [ag_s.rearrange("b (a c) -> (b a) c", c=128)]).then_inc(csem, 1)

        @blk.sync
        def _(sy):
            sy.wait_ge(csem, 1)
            for b_ in range(4):
                sy.dma_start(out=s4[:, b_ * 64:(b_ + 1) * 64],
                             in_=ag_s[b_].rearrange("(p f) -> p f", p=128)).then_inc(sem, 16)
            sy.dma_start(out=bsl[:], in_=io["bsel"]).then_inc(sem, 16)
            sy.wait_ge(sem, 80)

        @blk.vector
        def _(v):
            v.wait_ge(sem, 80)
            v.tensor_scalar(ssel[:], s4[:, 0:64], bsl[:, 0:1], None, ALU.mult).then_inc(sem, 1)
            v.wait_ge(sem, 81)
            for b_ in range(1, 4):
                v.scalar_tensor_tensor(ssel[:], s4[:, b_ * 64:(b_ + 1) * 64], bsl[:, b_:b_ + 1],
                                       ssel[:], ALU.mult, ALU.add).then_inc(sem, 1)
                v.wait_ge(sem, 81 + b_)

        @blk.gpsimd
        def _(g):
            g.wait_ge(sem, 84)
            g.dma_start(out=sfull_dr.rearrange("(p f) -> p f", p=128), in_=ssel[:]).then_inc(sem, 16)
            g.wait_ge(sem, 100)

    io2 = dict(io)
    io2["x1i"] = x1_dr
    io2["sfull"] = sfull_dr
    io2["x2o"] = x2_dr
    if "2" in _FP:
        emit_p2(nc, io2)

    io3 = dict(io)
    io3["x2i"] = x2_dr
    io3["qf_dr"], io3["kf_dr"], io3["kt_dr"], io3["vt_dr"], io3["state_o"] = qf3, kf3, kt3, vt3, st3
    if "3" in _FP:
        emit_p3(nc, io3)

    with contextlib.ExitStack() as stG2:
        sall = stG2.enter_context(nc.sbuf_tensor("sallg", [64, 8 * 1040], F32))
        wsl = stG2.enter_context(nc.sbuf_tensor("wslg", [64, 8], F32))
        sout = stG2.enter_context(nc.sbuf_tensor("soutg", [64, 1040], F32))
        blk = stG2.enter_context(nc.Block())
        sem = stG2.enter_context(nc.semaphore("ag2_sem"))
        csem = stG2.enter_context(nc.semaphore("ag2c_sem"))
        if "ag2" not in _FP:
            blk = _NullBlk()

        @blk.gpsimd
        def _(g):
            g.collective_compute("AllGather", ALU.bypass, [[0, 1, 2, 3, 4, 5, 6, 7]],
                                 [st3.rearrange("a b -> a b")],
                                 [ag_st.rearrange("r a b -> (r a) b")]).then_inc(csem, 1)

        @blk.sync
        def _(sy):
            sy.wait_ge(csem, 1)
            sy.dma_start(out=sall[:].rearrange("p (r t) -> p r t", r=8),
                         in_=ag_st.rearrange("r a b -> a r b")).then_inc(sem, 16)
            sy.dma_start(out=wsl[:], in_=io["wsel"]).then_inc(sem, 16)
            sy.wait_ge(sem, 32)

        @blk.vector
        def _(v):
            v.wait_ge(sem, 32)
            v.memset(sout[:], 0.0).then_inc(sem, 1)
            v.wait_ge(sem, 33)
            for r in range(8):
                v.scalar_tensor_tensor(sout[:], sall[:, r * 1040:(r + 1) * 1040], wsl[:, r:r + 1],
                                       sout[:], ALU.mult, ALU.add).then_inc(sem, 1)
                v.wait_ge(sem, 34 + r)

        @blk.gpsimd
        def _(g):
            g.wait_ge(sem, 41)
            g.dma_start(out=stprev_dr, in_=sout[:]).then_inc(sem, 16)
            g.wait_ge(sem, 57)

    io4 = dict(io)
    io4["x2i"] = x2_dr
    io4["qf_dr"], io4["kf_dr"], io4["kt_dr"], io4["vt_dr"] = qf3, kf3, kt3, vt3
    io4["state_prev"] = stprev_dr
    io4["x3o"] = x3_dr
    if "4" in _FP:
        emit_p4(nc, io4)

    io5 = dict(io)
    io5["x3i"] = x3_dr
    if "5" in _FP:
        emit_p5(nc, io5)
    return nc


def _full_inmaps(inputs):
    u64, l128, ones = _tri_consts()
    ident = _ident()
    sel16 = _sel16()
    tril = np.triu(np.ones((128, 128), np.float32))
    x = np.asarray(inputs["L0_attn_x"])
    maps = []
    for core in range(8):
        b, h = core // 2, core % 2
        ohv = np.zeros((1, 2), np.float32); ohv[0, h] = 1.0
        bsel = np.zeros((128, 4), np.float32); bsel[:, b] = 1.0
        wsel = np.zeros((64, 8), np.float32)
        if h == 1:
            wsel[:, 2 * b] = 1.0
        maps.append({
            "xo": np.ascontiguousarray(x[b, h * TOK:(h + 1) * TOK, :]),
            "xp": np.ascontiguousarray(x[b, (1 - h) * TOK:(2 - h) * TOK, :]),
            "wqT": np.ascontiguousarray(np.asarray(inputs["L0_attn_w_q"]).T),
            "wkT": np.ascontiguousarray(np.asarray(inputs["L0_attn_w_k"]).T),
            "wvT": np.ascontiguousarray(np.asarray(inputs["L0_attn_w_v"]).T),
            "woT": np.ascontiguousarray(np.asarray(inputs["L0_attn_w_o"]).T),
            "lng": np.asarray(inputs["L0_attn_ln_gamma"]),
            "lnb": np.asarray(inputs["L0_attn_ln_beta"]),
            "qg": np.asarray(inputs["L0_attn_q_gamma"]),
            "kg": np.asarray(inputs["L0_attn_k_gamma"]),
            "bo": np.asarray(inputs["L0_attn_b_o"]),
            "rtok": np.asarray(inputs["L0_ffn_routing_token"]),
            "identf": ident, "sel2c": _sel2c(), "b2c": _b2c(),
            "bsel": bsel, "wsel": wsel, "sel16c": sel16, "trilc": tril, "oh": ohv,
            "w1l": np.asarray(inputs["L0_ffn_w1_light"]), "b1l": np.asarray(inputs["L0_ffn_b1_light"]),
            "w2l": np.asarray(inputs["L0_ffn_w2_light"]), "b2l": np.asarray(inputs["L0_ffn_b2_light"]),
            "gl": np.asarray(inputs["L0_ffn_gamma_light"]),
            "w1h": np.asarray(inputs["L0_ffn_w1_heavy"]), "b1h": np.asarray(inputs["L0_ffn_b1_heavy"]),
            "w2h": np.asarray(inputs["L0_ffn_w2_heavy"]), "b2h": np.asarray(inputs["L0_ffn_b2_heavy"]),
            "gh": np.asarray(inputs["L0_ffn_gamma_heavy"]),
            "u64c": u64, "l128c": l128, "onesc": ones,
            "WqT": np.ascontiguousarray(np.asarray(inputs["L1_attn_Wq"]).T),
            "WkT": np.ascontiguousarray(np.asarray(inputs["L1_attn_Wk"]).T),
            "WvT": np.ascontiguousarray(np.asarray(inputs["L1_attn_Wv"]).T),
            "WoT": np.ascontiguousarray(np.asarray(inputs["L1_attn_Wo"]).T),
            "bq": np.asarray(inputs["L1_attn_bq"]), "bk": np.asarray(inputs["L1_attn_bk"]),
            "bv": np.asarray(inputs["L1_attn_bv"]), "bo4": np.asarray(inputs["L1_attn_bo"]),
            "Wr": np.asarray(inputs["L1_ffn_Wr"]),
            "Wg": np.asarray(inputs["L1_ffn_Wg"]), "Wu": np.asarray(inputs["L1_ffn_Wu"]),
            "Wd": np.asarray(inputs["L1_ffn_Wd"]),
            "Sg": np.asarray(inputs["L1_ffn_Sg"]), "Su": np.asarray(inputs["L1_ffn_Su"]),
            "Sd": np.asarray(inputs["L1_ffn_Sd"]),
        })
    return maps


def _tri_consts():
    u64 = np.triu(np.ones((64, 64), np.float32))
    l128 = np.triu(np.ones((128, 128), np.float32), 1)
    ones = np.ones((128, 128), np.float32)
    return u64, l128, ones


def _p2_inmaps(inputs, p1res):
    u64, l128, ones = _tri_consts()
    ident = _ident()
    maps = []
    for core in range(8):
        b, h = core // 2, core % 2
        sf = np.concatenate([np.asarray(p1res[2 * b]["so"]), np.asarray(p1res[2 * b + 1]["so"])])
        ohv = np.zeros((1, 2), np.float32)
        ohv[0, h] = 1.0
        maps.append({
            "x1i": np.asarray(p1res[core]["x1o"]),
            "sfull": sf,
            "oh": ohv,
            "w1l": np.asarray(inputs["L0_ffn_w1_light"]),
            "b1l": np.asarray(inputs["L0_ffn_b1_light"]),
            "w2l": np.asarray(inputs["L0_ffn_w2_light"]),
            "b2l": np.asarray(inputs["L0_ffn_b2_light"]),
            "gl": np.asarray(inputs["L0_ffn_gamma_light"]),
            "w1h": np.asarray(inputs["L0_ffn_w1_heavy"]),
            "b1h": np.asarray(inputs["L0_ffn_b1_heavy"]),
            "w2h": np.asarray(inputs["L0_ffn_w2_heavy"]),
            "b2h": np.asarray(inputs["L0_ffn_b2_heavy"]),
            "gh": np.asarray(inputs["L0_ffn_gamma_heavy"]),
            "identf": ident, "u64c": u64, "l128c": l128, "onesc": ones,
        })
    return maps


def run_p2(inputs, p1res):
    if "p2" not in _CACHE:
        _CACHE["p2"] = build_p2()
    res = run_bass_kernel_spmd(_CACHE["p2"], _p2_inmaps(inputs, p1res), core_ids=list(range(8)))
    return res.results


def _p3_inmaps(inputs, p2res):
    ident = _ident()
    maps = []
    for core in range(8):
        maps.append({
            "x2i": np.asarray(p2res[core]["x2o"]),
            "WqT": np.ascontiguousarray(np.asarray(inputs["L1_attn_Wq"]).T),
            "WkT": np.ascontiguousarray(np.asarray(inputs["L1_attn_Wk"]).T),
            "WvT": np.ascontiguousarray(np.asarray(inputs["L1_attn_Wv"]).T),
            "bq": np.asarray(inputs["L1_attn_bq"]),
            "bk": np.asarray(inputs["L1_attn_bk"]),
            "bv": np.asarray(inputs["L1_attn_bv"]),
            "identf": ident,
        })
    return maps


def _p4_inmaps(inputs, p2res, p3res):
    ident = _ident()
    tril = np.triu(np.ones((128, 128), np.float32))
    maps = []
    for core in range(8):
        b, h = core // 2, core % 2
        state_prev = (np.asarray(p3res[2 * b]["state_o"]) if h == 1
                      else np.zeros((64, 1040), np.float32))
        maps.append({
            "x2i": np.asarray(p2res[core]["x2o"]),
            "qf_dr": np.asarray(p3res[core]["qf_dr"]),
            "kf_dr": np.asarray(p3res[core]["kf_dr"]),
            "kt_dr": np.asarray(p3res[core]["kt_dr"]),
            "vt_dr": np.asarray(p3res[core]["vt_dr"]),
            "state_prev": state_prev,
            "WoT": np.ascontiguousarray(np.asarray(inputs["L1_attn_Wo"]).T),
            "bo4": np.asarray(inputs["L1_attn_bo"]),
            "identf": ident, "trilc": tril,
        })
    return maps


def _sel16():
    m = np.zeros((16, 2048), np.float32)
    for e in range(16):
        m[e, e * 128:(e + 1) * 128] = 1.0
    return m


def _p5_inmaps(inputs, p4res):
    ident = _ident()
    sel = _sel16()
    maps = []
    for core in range(8):
        maps.append({
            "x3i": np.asarray(p4res[core]["x3o"]),
            "Wr": np.asarray(inputs["L1_ffn_Wr"]),
            "Wg": np.asarray(inputs["L1_ffn_Wg"]),
            "Wu": np.asarray(inputs["L1_ffn_Wu"]),
            "Wd": np.asarray(inputs["L1_ffn_Wd"]),
            "Sg": np.asarray(inputs["L1_ffn_Sg"]),
            "Su": np.asarray(inputs["L1_ffn_Su"]),
            "Sd": np.asarray(inputs["L1_ffn_Sd"]),
            "identf": ident, "sel16c": sel,
        })
    return maps


def run_p3(inputs, p2res):
    if "p3" not in _CACHE:
        _CACHE["p3"] = build_p3()
    return run_bass_kernel_spmd(_CACHE["p3"], _p3_inmaps(inputs, p2res), core_ids=list(range(8))).results


def run_p4(inputs, p2res, p3res):
    if "p4" not in _CACHE:
        _CACHE["p4"] = build_p4()
    return run_bass_kernel_spmd(_CACHE["p4"], _p4_inmaps(inputs, p2res, p3res), core_ids=list(range(8))).results


def run_p5(inputs, p4res):
    if "p5" not in _CACHE:
        _CACHE["p5"] = build_p5()
    return run_bass_kernel_spmd(_CACHE["p5"], _p5_inmaps(inputs, p4res), core_ids=list(range(8))).results


def kernel(**inputs):
    if "full" not in _CACHE:
        _CACHE["full"] = build_full()
    res = run_bass_kernel_spmd(_CACHE["full"], _full_inmaps(inputs), core_ids=list(range(8)))
    out = np.empty((B, S, D), np.float32)
    for core in range(8):
        b, h = core // 2, core % 2
        out[b, h * TOK:(h + 1) * TOK, :] = np.asarray(res.results[core]["outo"])
    return out


def kernel_unfused(**inputs):
    p1 = run_p1(inputs)
    p2 = run_p2(inputs, p1)
    p3 = run_p3(inputs, p2)
    p4 = run_p4(inputs, p2, p3)
    p5 = run_p5(inputs, p4)
    out = np.empty((B, S, D), np.float32)
    for core in range(8):
        b, h = core // 2, core % 2
        out[b, h * TOK:(h + 1) * TOK, :] = np.asarray(p5[core]["outo"])
    return out


if __name__ == "__main__":
    nc = build_p1()
    n = sum(1 for _ in nc.all_instructions())
    print("P1 built ok; instructions:", n)


# version 22
# speedup vs baseline: 1.0237x; 1.0237x over previous
import contextlib
import os

import numpy as np
import ml_dtypes
import concourse.bass as bass
import concourse.mybir as mybir
from concourse.bass_utils import run_bass_kernel_spmd

F32 = mybir.dt.float32
BF16 = mybir.dt.bfloat16
AF = mybir.ActivationFunctionType
ALU = mybir.AluOpType
AXX = mybir.AxisListType.X

B, S, D = 4, 8192, 1024
H, DH = 16, 64
TOK = 4096
LN_EPS = 1e-5
CUT = -8.73365498
C0_ROUTE = 0.20794415416798357
BF = ml_dtypes.bfloat16
ENGINES = ("sync", "gpsimd", "scalar", "vector", "tensor")


class Prog:

    def __init__(self, nc, name):
        self.nc = nc
        self.name = name
        self.steps = []
        self.last_w = {}
        self.readers = {}
        self.eng_seq = {}

    def step(self, eng, emit, reads=(), writes=(), inc=1):
        deps = set()
        for r in reads:
            if r in self.last_w:
                deps.add(self.last_w[r])
        for w in writes:
            if w in self.last_w:
                deps.add(self.last_w[w])
            for rd in self.readers.get(w, ()):
                deps.add(rd)
        i = len(self.steps)
        kind = "d" if inc == 16 else "c"
        key = (eng, kind)
        self.eng_seq[key] = self.eng_seq.get(key, 0) + inc
        self.steps.append(dict(i=i, eng=eng, emit=emit, deps=deps, key=key,
                               seq=self.eng_seq[key], inc=inc))
        for r in reads:
            self.readers.setdefault(r, []).append(i)
        for w in writes:
            self.last_w[w] = i
            self.readers[w] = []
        return i

    def emit(self, stack):
        used_k = sorted({s["key"] for s in self.steps})
        used_e = [e for e in ENGINES if any(s["eng"] == e for s in self.steps)]
        sems = {k: stack.enter_context(self.nc.semaphore(f"{self.name}_{k[0]}_{k[1]}"))
                for k in used_k}
        blk = stack.enter_context(self.nc.Block())
        steps_all = self.steps

        chain_sems = {e: stack.enter_context(self.nc.semaphore(f"{self.name}_{e}_ch"))
                      for e in used_e if e in ("scalar", "vector", "gpsimd")}

        class _Chained:

            _NOCHAIN = {"wait_ge", "dma_start", "indirect_dma_start",
                        "collective_compute", "sem_inc", "nop"}

            def __init__(self, eng, sem, ctr):
                self._eng = eng
                self._sem = sem
                self._ctr = ctr

            def __getattr__(self, nm):
                fn = getattr(self._eng, nm)
                if nm in self._NOCHAIN or not callable(fn):
                    return fn

                def call(*a, **kw):
                    instr = fn(*a, **kw)
                    try:
                        instr.then_inc(self._sem, 1)
                    except Exception:
                        return instr
                    self._ctr[0] += 1
                    self._eng.wait_ge(self._sem, self._ctr[0])
                    return instr
                return call

        def make_body(e):
            my = [s for s in steps_all if s["eng"] == e]

            def body(eng):
                hw = {k: 0 for k in used_k}
                ctr = [0]
                wrapped = _Chained(eng, chain_sems[e], ctr) if e in chain_sems else eng
                for s in my:
                    need = {}
                    for d in s["deps"]:
                        ds = steps_all[d]
                        if ds["eng"] == e:
                            continue
                        k = ds["key"]
                        need[k] = max(need.get(k, 0), ds["seq"])
                    for k, val in sorted(need.items()):
                        if val > hw[k]:
                            eng.wait_ge(sems[k], val)
                            hw[k] = val
                    if s["inc"] == 16:
                        s["emit"](eng).then_inc(sems[s["key"]], 16)
                        eng.wait_ge(sems[s["key"]], s["seq"])
                    elif e in chain_sems:
                        s["emit"](wrapped)
                        eng.sem_inc(sems[s["key"]], 1)
                    else:
                        s["emit"](eng).then_inc(sems[s["key"]], s["inc"])
            return body

        for e in used_e:
            getattr(blk, e)(make_body(e))


def _repl(e, dst, row):
    w = row.shape[0]
    return e.dma_start(out=dst[:, 0:w], in_=row.unsqueeze(0).broadcast_to([dst.shape[0], w]))


def _last(*instrs):
    return instrs[-1]


P1_EXTRA = None


def build_p1():
    nc = bass.Bass("TRN2", target_bir_lowering=False)
    io = {}
    for nm, shape in (("xo", [TOK, D]), ("xp", [TOK, D]), ("wqT", [D, D]), ("wkT", [D, D]),
                      ("wvT", [D, D]), ("woT", [D, D]), ("lng", [D]), ("lnb", [D]),
                      ("qg", [DH]), ("kg", [DH]), ("bo", [D]), ("rtok", [D]),
                      ("identf", [128, 128]), ("sel2c", [2, 128]), ("b2c", [128, 2])):
        io[nm] = nc.dram_tensor(nm, shape, F32, kind="ExternalInput").ap()
    io["x1o"] = nc.dram_tensor("x1o", [TOK, D], F32, kind="ExternalOutput").ap()
    io["so"] = nc.dram_tensor("so", [TOK], F32, kind="ExternalOutput").ap()
    emit_p1(nc, io)
    return nc


def emit_p1(nc, io):
    xo, xp, wqT, wkT, wvT, woT = io["xo"], io["xp"], io["wqT"], io["wkT"], io["wvT"], io["woT"]
    lng, lnb, qg, kg, bo, rtok = io["lng"], io["lnb"], io["qg"], io["kg"], io["bo"], io["rtok"]
    identf, sel2c, b2c = io["identf"], io["sel2c"], io["b2c"]
    x1o, so = io["x1o"], io["so"]
    attn_dr = nc.dram_tensor("attn_dr1", [8, 128, TOK], BF16).ap()
    xn_dr = nc.dram_tensor("xn_dr1", [8, 128, S], BF16).ap()
    rec_dr = nc.dram_tensor("rec_dr1", [2, 1, 1024], F32).ap()
    stack = contextlib.ExitStack()
    with stack:
        sbt = lambda *a: stack.enter_context(nc.sbuf_tensor(*a))
        gfull = sbt("gfull", [128, D], F32)
        bfull = sbt("bfull", [128, D], F32)
        bofull = sbt("bofull", [128, D], F32)
        rtfull = sbt("rtfull", [128, D], F32)
        idf = sbt("idf", [128, 128], F32)
        idb = sbt("idb", [128, 128], BF16)
        qg2 = sbt("qg2", [128, 1], F32)
        kg2 = sbt("kg2", [128, 1], F32)
        negM = sbt("negM", [128, 1], F32)
        onesf = sbt("onesf", [1, 128], F32)
        b2ones = sbt("b2ones", [128, 2], BF16)
        sel2 = sbt("sel2", [2, 128], F32)

        with contextlib.ExitStack() as st0:
            gk_row = st0.enter_context(nc.sbuf_tensor("gk_row", [1, 2 * DH], F32))
            mtmp = st0.enter_context(nc.sbuf_tensor("mtmp", [1, 2], F32))
            pbc = st0.enter_context(nc.psum_tensor("pbc", [128, 1], F32))
            p = Prog(nc, "cst")
            p.step("sync", lambda e: _repl(e, gfull, lng), writes=("g",), inc=16)
            p.step("sync", lambda e: _repl(e, bfull, lnb), writes=("b",), inc=16)
            p.step("sync", lambda e: _repl(e, bofull, bo), writes=("bo",), inc=16)
            p.step("sync", lambda e: _repl(e, rtfull, rtok), writes=("rt",), inc=16)
            p.step("sync", lambda e: e.dma_start(out=idf[:], in_=identf), writes=("idf",), inc=16)
            p.step("sync", lambda e: e.dma_start(out=qg2[0:64, 0], in_=qg), writes=("q1",), inc=16)
            p.step("sync", lambda e: e.dma_start(out=qg2[64:128, 0], in_=qg), writes=("q2",), inc=16)
            p.step("sync", lambda e: e.dma_start(out=kg2[0:64, 0], in_=kg), writes=("k1",), inc=16)
            p.step("sync", lambda e: e.dma_start(out=kg2[64:128, 0], in_=kg), writes=("k2",), inc=16)
            p.step("sync", lambda e: e.dma_start(out=gk_row[0:1, 0:DH], in_=qg.unsqueeze(0)), writes=("g1",), inc=16)
            p.step("sync", lambda e: e.dma_start(out=gk_row[0:1, DH:2 * DH], in_=kg.unsqueeze(0)), writes=("g2",), inc=16)
            p.step("vector", lambda v: v.tensor_copy(idb[:], idf[:]), reads=("idf",), writes=("idb",))
            p.step("vector", lambda v: v.memset(onesf[:], 1.0), writes=("ones",))
            p.step("sync", lambda e: e.dma_start(out=sel2[:], in_=sel2c), writes=("sel2",), inc=16)
            p.step("gpsimd", lambda e: e.dma_start(out=b2ones[:], in_=b2c), writes=("b2",), inc=16)
            p.step("vector", lambda v: _last(
                v.tensor_reduce(mtmp[0:1, 0:1], gk_row[0:1, 0:DH], AXX, ALU.max, apply_absolute_value=True),
                v.tensor_reduce(mtmp[0:1, 1:2], gk_row[0:1, DH:2 * DH], AXX, ALU.max, apply_absolute_value=True),
                v.tensor_tensor(mtmp[0:1, 0:1], mtmp[0:1, 0:1], mtmp[0:1, 1:2], ALU.mult),
                v.tensor_scalar(mtmp[0:1, 0:1], mtmp[0:1, 0:1], -8.0, None, ALU.mult)),
                reads=("g1", "g2"), writes=("mt",))
            p.step("tensor", lambda te: te.matmul(pbc[:], onesf[0:1, :], mtmp[0:1, 0:1], start=True, stop=True),
                   reads=("mt", "ones"), writes=("pbc",))
            p.step("vector", lambda v: v.tensor_copy(negM[:], pbc[:]), reads=("pbc",), writes=("nm",))
            p.emit(st0)


        with contextlib.ExitStack() as stA:
            xt = stA.enter_context(nc.sbuf_tensor("xt", [128, 2 * D], F32))
            trash = stA.enter_context(nc.sbuf_tensor("trashA", [128, D], F32))
            stt = stA.enter_context(nc.sbuf_tensor("stA", [128, 16], F32))
            xc = stA.enter_context(nc.sbuf_tensor("xcA", [128, 2 * D], BF16))
            xn_t = stA.enter_context(nc.sbuf_tensor("xn_t", [128, 2 * D], BF16))
            ptr = stA.enter_context(nc.psum_tensor("ptrA", [128, 4 * 1024], BF16))
            xn_s = stA.enter_context(nc.sbuf_tensor("xn_s", [128, 2 * D], BF16))
            p = Prog(nc, "pA")
            for i in range(64):
                src = xo[i * 128:(i + 1) * 128, :] if i < 32 else xp[(i - 32) * 128:(i - 31) * 128, :]
                pr = i % 2
                xb = ("xb", pr)
                xti = xt[:, pr * D:(pr + 1) * D]
                st8 = stt[:, pr * 8:pr * 8 + 8]
                xci = xc[:, pr * D:(pr + 1) * D]
                xnti = xn_t[:, pr * D:(pr + 1) * D]
                p.step("sync", lambda e, src=src, xti=xti: e.dma_start(out=xti, in_=src),
                       writes=(xb,), inc=16)
                p.step("scalar", lambda s, xti=xti, st8=st8: _last(
                    s.activation(trash[:], xti, AF.Square, accum_out=st8[:, 1:2]),
                    s.activation(trash[:], xti, AF.Copy, accum_out=st8[:, 0:1])),
                    reads=(xb,), writes=(("stS", pr), "trash"))
                p.step("vector", lambda v, st8=st8: _last(
                    v.tensor_scalar(st8[:, 2:3], st8[:, 0:1], 1.0 / D, None, ALU.mult),
                    v.tensor_tensor(st8[:, 3:4], st8[:, 2:3], st8[:, 2:3], ALU.mult),
                    v.scalar_tensor_tensor(st8[:, 3:4], st8[:, 1:2], 1.0 / D, st8[:, 3:4], ALU.mult, ALU.subtract),
                    v.tensor_scalar(st8[:, 3:4], st8[:, 3:4], LN_EPS, None, ALU.add)),
                    reads=(("stS", pr),), writes=(("stV", pr),))
                p.step("scalar", lambda s, st8=st8: s.activation(st8[:, 6:7], st8[:, 3:4], AF.Sqrt),
                       reads=(("stV", pr),), writes=(("stQ", pr),))
                p.step("vector", lambda v, st8=st8: _last(
                    v.reciprocal(st8[:, 4:5], st8[:, 6:7]),
                    v.tensor_scalar(st8[:, 5:6], st8[:, 2:3], st8[:, 4:5], -1.0, ALU.mult, ALU.mult)),
                    reads=(("stQ", pr), ("stV", pr)), writes=(("stR", pr),))
                p.step("vector", lambda v, xti=xti, xci=xci, st8=st8: v.tensor_scalar(
                    xci, xti, st8[:, 4:5], st8[:, 5:6], ALU.mult, ALU.add),
                    reads=(("stR", pr), xb), writes=(("xc", pr),))
                p.step("vector", lambda v, xci=xci, xnti=xnti: _last(
                    v.tensor_tensor(xnti, xci, gfull[:], ALU.mult),
                    v.tensor_tensor(xnti, xnti, bfull[:], ALU.add)),
                    reads=(("xc", pr),), writes=(("xnt", pr),))
                for cp in range(4):
                    bk = (4 * i + cp) % 4
                    def trp(te, cp=cp, bk=bk, xnti=xnti):
                        te.transpose(ptr[:, bk * 1024:bk * 1024 + 128],
                                     xnti[:, 2 * cp * 128:(2 * cp + 1) * 128], idb[:])
                        return te.transpose(ptr[:, bk * 1024 + 128:bk * 1024 + 256],
                                            xnti[:, (2 * cp + 1) * 128:(2 * cp + 2) * 128], idb[:])
                    p.step("tensor", trp, reads=(("xnt", pr),), writes=(("ptr", bk),))
                    if cp % 2 == 0:
                        p.step("scalar", lambda s, cp=cp, i=i, bk=bk: s.copy(
                            xn_s[:, (i % 2) * D + 2 * cp * 128: (i % 2) * D + (2 * cp + 2) * 128],
                            ptr[:, bk * 1024:bk * 1024 + 256]),
                            reads=(("ptr", bk),), writes=(("xns", i % 2, cp),))
                    else:
                        p.step("vector", lambda v, cp=cp, i=i, bk=bk: v.tensor_copy(
                            xn_s[:, (i % 2) * D + 2 * cp * 128: (i % 2) * D + (2 * cp + 2) * 128],
                            ptr[:, bk * 1024:bk * 1024 + 256]),
                            reads=(("ptr", bk),), writes=(("xns", i % 2, cp),))
                p.step("gpsimd", lambda e, i=i: e.dma_start(
                    out=xn_dr[:, :, i * 128:(i + 1) * 128].rearrange("c p t -> p c t"),
                    in_=xn_s[:, (i % 2) * D:(i % 2 + 1) * D].rearrange("p (c t) -> p c t", c=8)),
                    reads=tuple(("xns", i % 2, cp) for cp in range(4)), writes=(("xnd", i),), inc=16)
            p.emit(stA)

        with contextlib.ExitStack() as stB:
            w3 = stB.enter_context(nc.sbuf_tensor("w3", [128, 2 * 3 * 1024], BF16))
            xnb = stB.enter_context(nc.sbuf_tensor("xnb", [128, 2 * 8 * 512], BF16))
            qn_hp = stB.enter_context(nc.sbuf_tensor("qn_hp", [128, TOK], BF16))
            kn_hp = stB.enter_context(nc.sbuf_tensor("kn_hp", [128, S], BF16))
            vx_hp = stB.enter_context(nc.sbuf_tensor("vx_hp", [128, 64 * 130], BF16))
            vtmp = stB.enter_context(nc.sbuf_tensor("vtmp", [128, 512], BF16))
            sq2 = stB.enter_context(nc.sbuf_tensor("sq2", [128, 512], BF16))
            nrm = stB.enter_context(nc.sbuf_tensor("nrm", [2, 2 * 512], F32))
            p_sb = stB.enter_context(nc.sbuf_tensor("p_sb", [128, 4 * 1024], BF16))
            recb = stB.enter_context(nc.sbuf_tensor("recb", [64, 4 * 512], F32))
            recd = stB.enter_context(nc.sbuf_tensor("recd", [1, 2 * 1024], F32))
            attn_hp = stB.enter_context(nc.sbuf_tensor("attn_hp", [128, TOK], BF16))
            pool = stB.enter_context(nc.psum_tensor("poolB", [128, 4096], F32))
            bank = lambda k: pool[:, k * 512:(k + 1) * 512]
            p = Prog(nc, "pBC")

            p.step("gpsimd", lambda g: _last(
                g.memset(vx_hp[:].rearrange("p (c t) -> p c t", c=64)[:, :, 64:65], 1.0),
                g.memset(vx_hp[:].rearrange("p (c t) -> p c t", c=64)[:, :, 129:130], 1.0)),
                writes=("vxones",))

            for hp in range(8):
                sl = hp % 2
                woff = sl * 3 * 1024
                for wi, wsrc in enumerate((wqT, wkT, wvT)):
                    dst = w3[:, woff + wi * 1024: woff + (wi + 1) * 1024]
                    src = wsrc.rearrange("(c p) j -> p c j", p=128)[:, :, hp * 128:(hp + 1) * 128]
                    p.step("gpsimd", lambda g, dst=dst, src=src: g.dma_start(
                        out=dst.rearrange("p (c j) -> p c j", c=8), in_=src),
                        writes=(("w3", sl, wi),), inc=16)

                ops = []
                for tt in range(16):
                    if tt < 8:
                        ops.append(("q", tt))
                    ops.append(("k", tt))
                    ops.append(("v", tt))
                xdma_done = set()
                stage2 = {}
                stage2b = {}

                def emit_op(n, kind, tt, hp=hp, sl=sl, woff=woff):
                    xsl = tt % 2
                    xv = xnb[:, xsl * 4096:(xsl + 1) * 4096]
                    if hp > 0:
                        xdma_done.update((0, 1))
                    if tt not in xdma_done:
                        xdma_done.add(tt)
                        p.step("sync", lambda e, tt=tt, xv=xv: e.dma_start(
                            out=xv.rearrange("p (c t) -> p c t", c=8),
                            in_=xn_dr[:, :, tt * 512:(tt + 1) * 512].rearrange("c p t -> p c t")),
                            writes=(("xnb", xsl),), inc=16)
                    ppb = n % 3
                    wi = {"q": 0, "k": 1, "v": 2}[kind]
                    wsl = w3[:, woff + wi * 1024: woff + (wi + 1) * 1024]

                    def mm_proj(te, wsl=wsl, xv=xv, ppb=ppb):
                        for kc in range(8):
                            mm = te.matmul(bank(ppb), wsl[:, kc * 128:(kc + 1) * 128],
                                           xv[:, kc * 512:(kc + 1) * 512],
                                           start=(kc == 0), stop=(kc == 7))
                        return mm
                    p.step("tensor", mm_proj, reads=(("w3", sl, wi), ("xnb", xsl)),
                           writes=(("bank", ppb),))

                    if kind == "v":
                        def st2v(n=n, tt=tt, ppb=ppb):
                            p.step("scalar", lambda s, ppb=ppb: s.copy(vtmp[:], bank(ppb)),
                                   reads=(("bank", ppb),), writes=("vtmp",))
                            for sub in range(4):
                                c = tt * 4 + sub
                                psl = c % 2

                                def tr2slot(psl):
                                    return pool[:, (6 + psl) * 512:(6 + psl) * 512 + 64].bitcast(BF16)
                                p.step("tensor", lambda te, sub=sub, psl=psl, tr2slot=tr2slot: te.transpose(
                                    tr2slot(psl), vtmp[:, sub * 128:(sub + 1) * 128], idb[:]),
                                    reads=("vtmp",), writes=(("bank", 6 + psl),))

                                def vevac(s_, c=c, psl=psl, tr2slot=tr2slot):
                                    dst = vx_hp[:, c * 130: c * 130 + 130].rearrange(
                                        "p (e j) -> p e j", e=2)[:, :, 0:64]
                                    srcv = tr2slot(psl).rearrange("p (e j) -> p e j", e=2)
                                    return s_.copy(dst, srcv)
                                p.step("scalar", vevac, reads=(("bank", 6 + psl),), writes=(("vx", c),))
                        stage2[n] = st2v
                    else:
                        gvec, dstbuf, scale_st, clampv, ptok = (
                            (qg2, qn_hp, 1.0, 8e-8, "prjq") if kind == "q"
                            else (kg2, kn_hp, 1.0 / 64.0, 1e-8, "prjk"))

                        def st2a(n=n, tt=tt, ppb=ppb, gvec=gvec, dstbuf=dstbuf,
                                 scale_st=scale_st, ptok=ptok):
                            nsl = nrm[:, (n % 2) * 512:(n % 2) * 512 + 512]
                            dsl = dstbuf[:, tt * 512:(tt + 1) * 512]
                            p.step("scalar", lambda s, ppb=ppb: s.activation(sq2[:], bank(ppb), AF.Square),
                                   reads=(("bank", ppb),), writes=("sq2",))
                            p.step("scalar", lambda s, dsl=dsl, gvec=gvec, ppb=ppb: s.activation(
                                dsl, bank(ppb), AF.Copy, scale=gvec[:, 0:1]),
                                reads=(("bank", ppb),), writes=(ptok + "g",))
                            p.step("tensor", lambda te: te.matmul(pool[0:2, 1536:2048], b2ones[:], sq2[:],
                                                                  start=True, stop=True),
                                   reads=("sq2",), writes=(("bank", 3),))
                            p.step("scalar", lambda s, nsl=nsl, scale_st=scale_st: s.activation(
                                nsl, pool[0:2, 1536:2048], AF.Sqrt, scale=scale_st),
                                reads=(("bank", 3),), writes=(("nrm", n % 2),))

                        def st2b(n=n, tt=tt, gvec=gvec, dstbuf=dstbuf, clampv=clampv, ptok=ptok):
                            nsl = nrm[:, (n % 2) * 512:(n % 2) * 512 + 512]
                            pfxb = 4 + n % 2
                            dsl = dstbuf[:, tt * 512:(tt + 1) * 512]
                            p.step("vector", lambda v, nsl=nsl, clampv=clampv: _last(
                                v.tensor_scalar(nsl, nsl, clampv, None, ALU.max),
                                v.reciprocal(nsl, nsl)),
                                reads=(("nrm", n % 2),), writes=(("nrm", n % 2),))
                            p.step("tensor", lambda te, nsl=nsl, pfxb=pfxb: te.matmul(
                                bank(pfxb), sel2[:], nsl, start=True, stop=True),
                                reads=(("nrm", n % 2),), writes=(("bank", pfxb),))
                            p.step("vector", lambda v, dsl=dsl, pfxb=pfxb: v.tensor_tensor(
                                dsl, dsl, bank(pfxb), ALU.mult),
                                reads=(("bank", pfxb), ptok + "g"), writes=((ptok, tt),))
                        stage2[n] = st2a
                        stage2b[n] = st2b

                for n, (kind, tt) in enumerate(ops):
                    emit_op(n, kind, tt)
                    if n >= 2:
                        stage2.pop(n - 2)()
                    if n >= 3 and (n - 3) in stage2b:
                        stage2b.pop(n - 3)()
                for n in (len(ops) - 2, len(ops) - 1):
                    stage2.pop(n)()
                    if n - 1 in stage2b:
                        stage2b.pop(n - 1)()
                for n in sorted(stage2b):
                    stage2b.pop(n)()

                def mm_sp(te, h, it, qt):
                    bk = h * 2 + it % 2
                    return te.matmul(bank(bk),
                                     kn_hp[h * 64:(h + 1) * 64, it * 128:(it + 1) * 128],
                                     qn_hp[h * 64:(h + 1) * 64, qt * 512:(qt + 1) * 512],
                                     start=True, stop=True, tile_position=(h * 64, 0))

                def emit_av(h, j, qt):
                    obk = 4 + 2 * (qt % 2) + h

                    def av2(te, h=h, j=j, obk=obk):
                        for dd in range(2):
                            it = 2 * j + dd
                            mm = te.matmul(
                                pool[0:65, obk * 512:(obk + 1) * 512],
                                vx_hp[:, it * 130 + h * 65: it * 130 + h * 65 + 65],
                                p_sb[:, (h * 2 + j % 2) * 1024 + dd * 512:
                                     (h * 2 + j % 2) * 1024 + dd * 512 + 512],
                                start=(it == 0), stop=(it == 63))
                        return mm
                    p.step("tensor", av2,
                           reads=(("pexp", h, j % 2), ("vx", 2 * j), ("vx", 2 * j + 1)),
                           writes=(("bank", obk),))

                def emit_epiA(qt, hp=hp):
                    ob0 = 4 + 2 * (qt % 2)
                    p.step("vector", lambda v, qt=qt, ob0=ob0: v.reciprocal(
                        recd[0:1, (qt % 2) * 1024:(qt % 2) * 1024 + 1024],
                        pool[64:65, ob0 * 512:(ob0 + 2) * 512]),
                        reads=(("bank", ob0), ("bank", ob0 + 1)), writes=(("recd", qt % 2),))
                    p.step("sync", lambda e, qt=qt: e.dma_start(
                        out=rec_dr[qt % 2], in_=recd[0:1, (qt % 2) * 1024:(qt % 2) * 1024 + 1024]),
                        reads=(("recd", qt % 2),), writes=(("recdd", qt % 2),), inc=16)
                    for h in range(2):
                        rb = recb[0:64, (2 * (qt % 2) + h) * 512:(2 * (qt % 2) + h) * 512 + 512]
                        p.step("gpsimd", lambda g, rb=rb, h=h, qt=qt: g.dma_start(
                            out=rb, in_=rec_dr[qt % 2, 0, h * 512:(h + 1) * 512]
                            .unsqueeze(0).broadcast_to([64, 512])),
                            reads=(("recdd", qt % 2),), writes=(("recb", h, qt % 2),), inc=16)

                def emit_epiB(qt, hp=hp):
                    ob0 = 4 + 2 * (qt % 2)
                    for h in range(2):
                        obk = ob0 + h
                        rb = recb[0:64, (2 * (qt % 2) + h) * 512:(2 * (qt % 2) + h) * 512 + 512]
                        p.step("vector", lambda v, h=h, qt=qt, rb=rb, obk=obk: v.tensor_tensor(
                            attn_hp[h * 64:(h + 1) * 64, qt * 512:(qt + 1) * 512],
                            pool[0:64, obk * 512:(obk + 1) * 512], rb, ALU.mult),
                            reads=(("recb", h, qt % 2), ("bank", obk)), writes=(("ahp", h, qt),))

                pend_epiA = None
                pend_epiB = None
                for qt in range(8):
                    for j in range(32):
                        for h in range(2):
                            p.step("tensor", lambda te, h=h, j=j, qt=qt, mm_sp=mm_sp: _last(
                                mm_sp(te, h, 2 * j, qt), mm_sp(te, h, 2 * j + 1, qt)),
                                reads=(("prjk", (2 * j) // 4), ("prjq", qt)),
                                writes=(("bank", h * 2), ("bank", h * 2 + 1)))
                        if j > 0:
                            for h in range(2):
                                emit_av(h, j - 1, qt)
                        for h in range(2):
                            p.step("scalar", lambda s, h=h, j=j: s.activation(
                                p_sb[:, (h * 2 + j % 2) * 1024:(h * 2 + j % 2) * 1024 + 1024],
                                pool[:, h * 1024:(h + 1) * 1024],
                                AF.Exp, bias=negM[:, 0:1], scale=1.0),
                                reads=(("bank", h * 2), ("bank", h * 2 + 1)),
                                writes=(("pexp", h, j % 2),))
                        if j == 1 and pend_epiA is not None:
                            pend_epiA()
                            pend_epiA = None
                        if j == 8 and pend_epiB is not None:
                            pend_epiB()
                            pend_epiB = None
                        if j == 16 and qt == 5 and hp < 7:
                            for ttp in (0, 1):
                                xvp = xnb[:, ttp * 4096:(ttp + 1) * 4096]
                                p.step("sync", lambda e, ttp=ttp, xvp=xvp: e.dma_start(
                                    out=xvp.rearrange("p (c t) -> p c t", c=8),
                                    in_=xn_dr[:, :, ttp * 512:(ttp + 1) * 512].rearrange("c p t -> p c t")),
                                    writes=(("xnb", ttp),), inc=16)
                    for h in range(2):
                        emit_av(h, 31, qt)
                    pend_epiA = lambda qt=qt, emit_epiA=emit_epiA: emit_epiA(qt)
                    pend_epiB = lambda qt=qt, emit_epiB=emit_epiB: emit_epiB(qt)
                pend_epiA()
                pend_epiB()
                p.step("sync", lambda e, hp=hp: e.dma_start(out=attn_dr[hp], in_=attn_hp[:]),
                       reads=tuple(("ahp", e2, qt) for e2 in range(2) for qt in range(8)),
                       writes=(("adr", hp),), inc=16)
            p.emit(stB)

        with contextlib.ExitStack() as stD:
            attn_all = stD.enter_context(nc.sbuf_tensor("attn_all", [128, 8 * TOK], BF16))
            wo_sb = stD.enter_context(nc.sbuf_tensor("wo_sb", [128, 8 * 1024], BF16))
            xt2 = stD.enter_context(nc.sbuf_tensor("xt2", [128, 2 * D], F32))
            y2 = stD.enter_context(nc.sbuf_tensor("y2", [128, D], F32))
            sqd = stD.enter_context(nc.sbuf_tensor("sqd", [128, D], F32))
            std_ = stD.enter_context(nc.sbuf_tensor("stD", [128, 8], F32))
            x1t = stD.enter_context(nc.sbuf_tensor("x1t", [128, D], F32))
            s_all = stD.enter_context(nc.sbuf_tensor("s_all", [128, 32], F32))
            s_tr = stD.enter_context(nc.sbuf_tensor("s_tr", [32, 128], F32))
            pod = stD.enter_context(nc.psum_tensor("pod", [128, 2 * 512], F32))
            pstr = stD.enter_context(nc.psum_tensor("pstr", [32, 128], F32))
            p = Prog(nc, "pD")
            p.step("gpsimd", lambda g: g.dma_start(out=wo_sb[:].rearrange("p (c j) -> p c j", c=8),
                                                   in_=woT.rearrange("(c p) j -> p c j", p=128)),
                   writes=("wo",), inc=16)
            p.step("sync", lambda e: e.dma_start(out=attn_all[:].rearrange("p (c t) -> p c t", c=8),
                                                 in_=attn_dr.rearrange("c p t -> p c t")),
                   writes=("aall",), inc=16)
            for i in range(32):
                xb = ("x2b", i % 2)
                xti = xt2[:, (i % 2) * D:(i % 2 + 1) * D]
                p.step("sync", lambda e, i=i, xti=xti: e.dma_start(out=xti, in_=xo[i * 128:(i + 1) * 128, :]),
                       writes=(xb,), inc=16)
                for jh in range(2):
                    def mm_od(te, i=i, jh=jh):
                        for kc in range(8):
                            mm = te.matmul(pod[:, jh * 512:(jh + 1) * 512],
                                           attn_all[:, kc * TOK + i * 128: kc * TOK + (i + 1) * 128],
                                           wo_sb[:, kc * 1024 + jh * 512: kc * 1024 + jh * 512 + 512],
                                           start=(kc == 0), stop=(kc == 7))
                        return mm
                    p.step("tensor", mm_od, reads=("aall", "wo"), writes=(("pod", jh),))
                    p.step("vector", lambda v, jh=jh, xti=xti: _last(
                        v.tensor_tensor(y2[:, jh * 512:(jh + 1) * 512], pod[:, jh * 512:(jh + 1) * 512],
                                        bofull[:, jh * 512:(jh + 1) * 512], ALU.add),
                        v.tensor_tensor(y2[:, jh * 512:(jh + 1) * 512], y2[:, jh * 512:(jh + 1) * 512],
                                        xti[:, jh * 512:(jh + 1) * 512], ALU.add)),
                        reads=(("pod", jh), xb), writes=(("y2", jh),))
                p.step("vector", lambda v: _last(
                    v.tensor_tensor(sqd[:], y2[:], y2[:], ALU.mult),
                    v.reduce_sum(std_[:, 1:2], sqd[:], AXX),
                    v.reduce_sum(std_[:, 0:1], y2[:], AXX),
                    v.tensor_scalar(std_[:, 2:3], std_[:, 0:1], 1.0 / D, None, ALU.mult),
                    v.tensor_tensor(std_[:, 3:4], std_[:, 2:3], std_[:, 2:3], ALU.mult),
                    v.scalar_tensor_tensor(std_[:, 3:4], std_[:, 1:2], 1.0 / D, std_[:, 3:4], ALU.mult, ALU.subtract),
                    v.tensor_scalar(std_[:, 3:4], std_[:, 3:4], LN_EPS, None, ALU.add)),
                    reads=(("y2", 0), ("y2", 1)), writes=("stD", "sqd"))
                p.step("scalar", lambda s: s.activation(std_[:, 6:7], std_[:, 3:4], AF.Sqrt),
                       reads=("stD",), writes=("stDb",))
                p.step("vector", lambda v: _last(
                    v.reciprocal(std_[:, 4:5], std_[:, 6:7]),
                    v.tensor_scalar(std_[:, 5:6], std_[:, 2:3], std_[:, 4:5], -1.0, ALU.mult, ALU.mult)),
                    reads=("stD", "stDb"), writes=("stD",))
                p.step("vector", lambda v: v.tensor_scalar(x1t[:], y2[:], std_[:, 4:5], std_[:, 5:6], ALU.mult, ALU.add),
                       reads=("stD", ("y2", 0), ("y2", 1)), writes=("x1t",))
                p.step("sync", lambda e, i=i: e.dma_start(out=x1o[i * 128:(i + 1) * 128, :], in_=x1t[:]),
                       reads=("x1t",), writes=(("x1d", i),), inc=16)
                p.step("vector", lambda v, i=i: _last(
                    v.tensor_tensor(sqd[:], x1t[:], rtfull[:], ALU.mult),
                    v.reduce_sum(s_all[:, i:i + 1], sqd[:], AXX)),
                    reads=("x1t",), writes=(("s", i), "sqd"))
            p.step("tensor", lambda te: te.transpose(pstr[:], s_all[:, 0:32], idf[:]),
                   reads=tuple(("s", i) for i in range(32)), writes=("pstr",))
            p.step("vector", lambda v: v.tensor_copy(s_tr[:], pstr[:]), reads=("pstr",), writes=("str",))
            p.step("sync", lambda e: e.dma_start(out=so.rearrange("(i p) -> i p", p=128), in_=s_tr[:]),
                   reads=("str",), writes=("sd",), inc=16)
            p.step("gpsimd", lambda g: g.engine_nop(),
                   reads=tuple(("x1d", i) for i in range(32)) + ("sd",))
            p.emit(stD)

    return nc



P2_W = [("w1l", [D, 512]), ("b1l", [512]), ("w2l", [512, D]), ("b2l", [D]), ("gl", [D]),
        ("w1h", [D, 4096]), ("b1h", [4096]), ("w2h", [4096, D]), ("b2h", [D]), ("gh", [D]),
        ("u64c", [64, 64]), ("l128c", [128, 128]), ("onesc", [128, 128]), ("oh", [1, 2])]


def build_p2():
    nc = bass.Bass("TRN2", target_bir_lowering=False)
    io = {}
    io["x1i"] = nc.dram_tensor("x1i", [TOK, D], F32, kind="ExternalInput").ap()
    io["sfull"] = nc.dram_tensor("sfull", [S], F32, kind="ExternalInput").ap()
    io["identf"] = nc.dram_tensor("identf", [128, 128], F32, kind="ExternalInput").ap()
    for nm, shape in P2_W:
        io[nm] = nc.dram_tensor(nm, shape, F32, kind="ExternalInput").ap()
    io["x2o"] = nc.dram_tensor("x2o", [TOK, D], F32, kind="ExternalOutput").ap()
    emit_p2(nc, io)
    return nc


def emit_p2(nc, io):
    x1i, sfull, oh = io["x1i"], io["sfull"], io["oh"]
    w1l, b1l, w2l, b2l, gl = io["w1l"], io["b1l"], io["w2l"], io["b2l"], io["gl"]
    w1h, b1h, w2h, b2h, gh = io["w1h"], io["b1h"], io["w2h"], io["b2h"], io["gh"]
    identf, u64c, l128c, onesc = io["identf"], io["u64c"], io["l128c"], io["onesc"]
    x2o = io["x2o"]
    mrow = nc.dram_tensor("mrow2", [TOK], F32).ap()

    stack = contextlib.ExitStack()
    with stack:
        sbt = lambda *a: stack.enter_context(nc.sbuf_tensor(*a))
        idf = sbt("idf2", [128, 128], F32)
        idb = sbt("idb2", [128, 128], BF16)
        onesf = sbt("onesf2", [128, 128], F32)
        u64b = sbt("u64b", [64, 64], BF16)
        l128b = sbt("l128b", [128, 128], BF16)
        pool = stack.enter_context(nc.psum_tensor("pool2", [128, 4096], F32))

        def bank(i, rows=128):
            return pool[0:rows, i * 512:(i + 1) * 512]

        with contextlib.ExitStack() as stR:
            s_sb = stR.enter_context(nc.sbuf_tensor("s_sb", [128, 64], F32))
            rr = stR.enter_context(nc.sbuf_tensor("rr", [128, 12], F32))
            e_sb = stR.enter_context(nc.sbuf_tensor("e_sb", [128, 64], F32))
            u_sb = stR.enter_context(nc.sbuf_tensor("u_sb", [128, 64], F32))
            trow = stR.enter_context(nc.sbuf_tensor("trow", [1, 128], F32))
            pmask = stR.enter_context(nc.sbuf_tensor("pmask", [128, 64], F32))
            qmaskb = stR.enter_context(nc.sbuf_tensor("qmaskb", [128, 64], BF16))
            qT = stR.enter_context(nc.sbuf_tensor("qT", [64, 128], BF16))
            pref = stR.enter_context(nc.sbuf_tensor("pref", [128, 64], F32))
            rb16 = stR.enter_context(nc.sbuf_tensor("rb16", [128, 1], BF16))
            mk_own = stR.enter_context(nc.sbuf_tensor("mk_own", [64, 64], F32))
            mk_shift = stR.enter_context(nc.sbuf_tensor("mk_shift", [64, 64], F32))
            o01 = stR.enter_context(nc.sbuf_tensor("o01", [64, 2], F32))
            ohs = stR.enter_context(nc.sbuf_tensor("ohs", [1, 2], F32))
            bis = stR.enter_context(nc.sbuf_tensor("bis", [128, 10], F32))
            p = Prog(nc, "pR")
            p.step("sync", lambda e: e.dma_start(out=s_sb[:], in_=sfull.rearrange("(p f) -> p f", p=128)),
                   writes=("s",), inc=16)
            p.step("sync", lambda e: e.dma_start(out=idf[:], in_=identf), writes=("idf",), inc=16)
            p.step("sync", lambda e: e.dma_start(out=onesf[:], in_=onesc), writes=("ones",), inc=16)
            p.step("gpsimd", lambda e: e.dma_start(out=u64b[:], in_=u64c), writes=("u64",), inc=16)
            p.step("gpsimd", lambda e: e.dma_start(out=l128b[:], in_=l128c), writes=("l128",), inc=16)
            p.step("sync", lambda e: e.dma_start(out=ohs[:], in_=oh), writes=("oh",), inc=16)
            p.step("vector", lambda v: v.tensor_copy(idb[:], idf[:]), reads=("idf",), writes=("idb",))
            p.step("vector", lambda v: v.reduce_max(rr[:, 8:9], s_sb[:], AXX),
                   reads=("s",), writes=("r8",))
            p.step("tensor", lambda te: te.transpose(pool[0:1, 0:128], rr[:, 8:9], idf[:]),
                   reads=("r8", "idf"), writes=(("rb", 0),))
            p.step("vector", lambda v: _last(
                v.tensor_copy(trow[0:1, :], pool[0:1, 0:128]),
                v.reduce_max(trow[0:1, 0:1], trow[0:1, :], AXX)), reads=(("rb", 0),), writes=("trow",))
            p.step("tensor", lambda te: te.matmul(pool[:, 128:129], onesf[0:1, :], trow[0:1, 0:1],
                                                  start=True, stop=True),
                   reads=("trow", "ones"), writes=(("rb", 0),))
            p.step("vector", lambda v: _last(
                v.tensor_copy(rr[:, 3:4], pool[:, 128:129]),
                v.memset(rr[:, 0:1], 0.0)), reads=(("rb", 0),), writes=("rr",))
            for it in range(50):
                p.step("vector", lambda v: _last(
                    v.tensor_tensor(rr[:, 1:2], rr[:, 3:4], rr[:, 0:1], ALU.min),
                    v.tensor_scalar(rr[:, 2:3], rr[:, 1:2], -10.0, None, ALU.mult),
                    v.tensor_scalar(u_sb[:], s_sb[:], rr[:, 0:1], None, ALU.min)),
                    reads=("rr", "s"), writes=("rr", "u"))
                p.step("scalar", lambda sc: sc.activation(e_sb[:], u_sb[:], AF.Exp,
                                                          bias=rr[:, 2:3], scale=10.0),
                       reads=("u", "rr"), writes=("e",))
                p.step("vector", lambda v: v.reduce_sum(rr[:, 9:10], e_sb[:], AXX),
                       reads=("e",), writes=("r9",))
                p.step("tensor", lambda te: te.matmul(pool[:, 256:257], onesf[:], rr[:, 9:10],
                                                      start=True, stop=True),
                       reads=("r9", "ones"), writes=(("rb", 0),))
                p.step("vector", lambda v: v.tensor_copy(rr[:, 4:5], pool[:, 256:257]),
                       reads=(("rb", 0),), writes=("r4",))
                p.step("scalar", lambda sc: sc.activation(rr[:, 5:6], rr[:, 4:5], AF.Ln),
                       reads=("r4",), writes=("r5",))
                p.step("vector", lambda v: _last(
                    v.tensor_scalar(rr[:, 11:12], rr[:, 5:6], 0.1, -C0_ROUTE, ALU.mult, ALU.add),
                    v.tensor_tensor(rr[:, 0:1], rr[:, 11:12], rr[:, 1:2], ALU.add)),
                    reads=("r5", "rr"), writes=("rr",))
            p.step("vector", lambda v: _last(
                v.tensor_scalar(rr[:, 10:11], rr[:, 0:1], CUT, None, ALU.add),
                v.tensor_scalar(pmask[:], s_sb[:], rr[:, 0:1], None, ALU.min),
                v.tensor_scalar(pmask[:], pmask[:], rr[:, 10:11], None, ALU.max),
                v.tensor_scalar(bis[:, 0:1], rr[:, 10:11], -1.0, None, ALU.add),
                v.tensor_copy(bis[:, 1:2], rr[:, 0:1])),
                reads=("rr", "s"), writes=("pm", "bis"))
            for it in range(45):
                p.step("vector", lambda v: _last(
                    v.tensor_tensor(bis[:, 2:3], bis[:, 0:1], bis[:, 1:2], ALU.add),
                    v.tensor_scalar(bis[:, 2:3], bis[:, 2:3], 0.5, None, ALU.mult),
                    v.tensor_scalar(e_sb[:], pmask[:], bis[:, 2:3], None, ALU.is_gt),
                    v.reduce_sum(rr[:, 8:9], e_sb[:], AXX)),
                    reads=("bis", "pm"), writes=("bis", "r89b"))
                p.step("tensor", lambda te: te.matmul(pool[:, 256:257], onesf[:], rr[:, 8:9], start=True, stop=True),
                       reads=("r89b", "ones"), writes=(("rb", 0),))
                p.step("vector", lambda v: _last(
                    v.tensor_copy(bis[:, 3:4], pool[:, 256:257]),
                    v.tensor_scalar(bis[:, 4:5], bis[:, 3:4], 1024.0, None, ALU.is_ge),
                    v.tensor_tensor(rr[:, 8:9], bis[:, 2:3], bis[:, 0:1], ALU.subtract),
                    v.tensor_tensor(rr[:, 8:9], rr[:, 8:9], bis[:, 4:5], ALU.mult),
                    v.tensor_tensor(bis[:, 0:1], bis[:, 0:1], rr[:, 8:9], ALU.add),
                    v.tensor_tensor(rr[:, 8:9], bis[:, 2:3], bis[:, 1:2], ALU.subtract),
                    v.tensor_scalar(rr[:, 9:10], bis[:, 4:5], -1.0, 1.0, ALU.mult, ALU.add),
                    v.tensor_tensor(rr[:, 8:9], rr[:, 8:9], rr[:, 9:10], ALU.mult),
                    v.tensor_tensor(bis[:, 1:2], bis[:, 1:2], rr[:, 8:9], ALU.add)),
                    reads=(("rb", 0), "bis"), writes=("bis",))
            p.step("vector", lambda v: _last(
                v.tensor_scalar(e_sb[:], pmask[:], bis[:, 1:2], None, ALU.is_gt),
                v.reduce_sum(rr[:, 8:9], e_sb[:], AXX)),
                reads=("bis", "pm"), writes=("eA", "r89c"))
            p.step("tensor", lambda te: te.matmul(pool[:, 256:257], onesf[:], rr[:, 8:9], start=True, stop=True),
                   reads=("r89c", "ones"), writes=(("rb", 0),))
            p.step("vector", lambda v: v.tensor_copy(bis[:, 8:9], pool[:, 256:257]),
                   reads=(("rb", 0),), writes=("cntA",))
            p.step("vector", lambda v: _last(
                v.tensor_scalar(u_sb[:], pmask[:], bis[:, 0:1], None, ALU.is_gt),
                v.tensor_tensor(u_sb[:], u_sb[:], e_sb[:], ALU.subtract),
                v.tensor_copy(qmaskb[:], u_sb[:])),
                reads=("bis", "pm", "eA"), writes=("qm",))
            p.step("tensor", lambda te: te.transpose(pool[0:64, 2048:2112].bitcast(BF16), qmaskb[:], idb[:]),
                   reads=("qm", "idb"), writes=(("rb", 4),))
            p.step("vector", lambda v: v.tensor_copy(qT[:], pool[0:64, 2048:2112].bitcast(BF16)),
                   reads=(("rb", 4),), writes=("qT",))
            p.step("tensor", lambda te: te.matmul(pool[:, 0:64], qT[:], u64b[:], start=True, stop=True),
                   reads=("qT", "u64"), writes=(("rb", 0),))
            p.step("vector", lambda v: _last(
                v.tensor_copy(pref[:], pool[:, 0:64]),
                v.tensor_copy(rb16[:], pool[:, 63:64])), reads=(("rb", 0),), writes=("pref", "rb16"))
            p.step("tensor", lambda te: te.matmul(pool[:, 512:513], l128b[:], rb16[:], start=True, stop=True),
                   reads=("l128", "rb16"), writes=(("rb", 1),))
            p.step("vector", lambda v: _last(
                v.tensor_copy(rr[:, 7:8], pool[:, 512:513]),
                v.tensor_scalar(rr[:, 6:7], bis[:, 8:9], -1.0, 1024.0, ALU.mult, ALU.add),
                v.tensor_scalar(pref[:], pref[:], rr[:, 7:8], None, ALU.add),
                v.tensor_scalar(pref[:], pref[:], rr[:, 6:7], None, ALU.is_le),
                v.tensor_tensor(pref[:], pref[:], qmaskb[:], ALU.mult),
                v.tensor_tensor(pmask[:], e_sb[:], pref[:], ALU.add)),
                reads=(("rb", 1), "qm", "bis", "eA", "pref"), writes=("pm",))
            p.step("tensor", lambda te: te.matmul(pool[0:64, 1536:1538], onesf[0:1, 0:64], ohs[0:1, 0:2],
                                                  start=True, stop=True),
                   reads=("oh", "ones"), writes=(("rb", 3),))
            p.step("gpsimd", lambda g: g.dma_start(out=mk_shift[:], in_=pmask[64:128, :]),
                   reads=("pm",), writes=("mks",), inc=16)
            p.step("vector", lambda v: _last(
                v.tensor_copy(o01[:], pool[0:64, 1536:1538]),
                v.tensor_scalar(mk_own[:], pmask[0:64, :], o01[:, 0:1], None, ALU.mult),
                v.scalar_tensor_tensor(mk_own[:], mk_shift[:], o01[:, 1:2], mk_own[:], ALU.mult, ALU.add)),
                reads=(("rb", 3), "pm", "mks"), writes=("mko",))
            p.step("sync", lambda e: e.dma_start(out=mrow.rearrange("(p f) -> p f", p=64), in_=mk_own[:]),
                   reads=("mko",), writes=("mrowd",), inc=16)
            p.emit(stR)

        with contextlib.ExitStack() as stF:
            w1l_sb = stF.enter_context(nc.sbuf_tensor("w1l_sb", [128, 8 * 512], BF16))
            w2l_sb = stF.enter_context(nc.sbuf_tensor("w2l_sb", [128, 4 * 1024], BF16))
            w1h_sb = stF.enter_context(nc.sbuf_tensor("w1h_sb", [128, 8 * 4096], BF16))
            w2h_sb = stF.enter_context(nc.sbuf_tensor("w2h_sb", [128, 32 * 1024], BF16))
            gcol = stF.enter_context(nc.sbuf_tensor("gcol", [128, 16], F32))
            b1c = stF.enter_context(nc.sbuf_tensor("b1c", [128, 36], F32))
            b2lf = stF.enter_context(nc.sbuf_tensor("b2lf", [128, D], BF16))
            b2hf = stF.enter_context(nc.sbuf_tensor("b2hf", [128, D], BF16))
            x1t = stF.enter_context(nc.sbuf_tensor("x1t2", [128, D], F32))
            sqf = stF.enter_context(nc.sbuf_tensor("sqf2", [128, D], F32))
            stt = stF.enter_context(nc.sbuf_tensor("st2", [128, 8], F32))
            mks = stF.enter_context(nc.sbuf_tensor("mks", [128, 6], F32))
            xmini = stF.enter_context(nc.sbuf_tensor("xmini", [128, 2 * 256], BF16))
            xhp = stF.enter_context(nc.sbuf_tensor("xhp", [128, 8 * 512], BF16))
            hl = stF.enter_context(nc.sbuf_tensor("hl", [128, 4 * 512], BF16))
            hh = stF.enter_context(nc.sbuf_tensor("hh", [128, 32 * 512], BF16))
            y2 = stF.enter_context(nc.sbuf_tensor("y22", [128, D], F32))
            p = Prog(nc, "pF")
            p.step("gpsimd", lambda g: g.dma_start(out=w1l_sb[:].rearrange("p (c j) -> p c j", c=8),
                                                   in_=w1l.rearrange("(c p) j -> p c j", p=128)),
                   writes=("w1l",), inc=16)
            p.step("gpsimd", lambda g: g.dma_start(out=w2l_sb[:].rearrange("p (c j) -> p c j", c=4),
                                                   in_=w2l.rearrange("(c p) j -> p c j", p=128)),
                   writes=("w2l",), inc=16)
            for hf in range(4):
                p.step("gpsimd", lambda g, hf=hf: g.dma_start(
                    out=w1h_sb[:].rearrange("p (c j) -> p c j", c=8)[:, :, hf * 1024:(hf + 1) * 1024],
                    in_=w1h.rearrange("(c p) j -> p c j", p=128)[:, :, hf * 1024:(hf + 1) * 1024]),
                    writes=(("w1h", hf),), inc=16)
                p.step("gpsimd", lambda g, hf=hf: g.dma_start(
                    out=w2h_sb[:, hf * 8192: (hf + 1) * 8192].rearrange("p (c j) -> p c j", c=8),
                    in_=w2h.rearrange("(c p) j -> p c j", p=128)[:, 8 * hf: 8 * (hf + 1), :]),
                    writes=(("w2h", hf),), inc=16)
            for c in range(8):
                p.step("sync", lambda e, c=c: e.dma_start(out=gcol[:, c], in_=gl[c * 128:(c + 1) * 128]),
                       writes=(("gc", c),), inc=16)
                p.step("sync", lambda e, c=c: e.dma_start(out=gcol[:, 8 + c], in_=gh[c * 128:(c + 1) * 128]),
                       writes=(("gc", 8 + c),), inc=16)
            for c in range(4):
                p.step("sync", lambda e, c=c: e.dma_start(out=b1c[:, c], in_=b1l[c * 128:(c + 1) * 128]),
                       writes=(("b1", c),), inc=16)
            for c in range(32):
                p.step("sync", lambda e, c=c: e.dma_start(out=b1c[:, 4 + c], in_=b1h[c * 128:(c + 1) * 128]),
                       writes=(("b1", 4 + c),), inc=16)
            p.step("gpsimd", lambda g: _repl(g, b2lf, b2l), writes=("b2l",), inc=16)
            p.step("gpsimd", lambda g: _repl(g, b2hf, b2h), writes=("b2h",), inc=16)
            for c in range(8):
                p.step("vector", lambda v, c=c: v.tensor_scalar(
                    w1l_sb[:, c * 512:(c + 1) * 512], w1l_sb[:, c * 512:(c + 1) * 512],
                    gcol[:, c:c + 1], None, ALU.mult),
                    reads=(("gc", c), "w1l"), writes=("w1l",))
                p.step("vector", lambda v, c=c: v.tensor_scalar(
                    w1h_sb[:, c * 4096:(c + 1) * 4096], w1h_sb[:, c * 4096:(c + 1) * 4096],
                    gcol[:, 8 + c:9 + c], None, ALU.mult),
                    reads=(("gc", 8 + c), ("w1h", 0), ("w1h", 1), ("w1h", 2), ("w1h", 3)),
                    writes=(("w1hg", c),))
            for blk in range(8):
                for it4 in range(4):
                    i = blk * 4 + it4
                    p.step("sync", lambda e, i=i: e.dma_start(out=x1t[:], in_=x1i[i * 128:(i + 1) * 128, :]),
                           writes=("x1b",), inc=16)
                    p.step("vector", lambda v: _last(
                        v.tensor_tensor(sqf[:], x1t[:], x1t[:], ALU.mult),
                        v.reduce_sum(stt[:, 0:1], sqf[:], AXX)),
                        reads=("x1b",), writes=("st", "sqf"))
                    p.step("scalar", lambda sc: sc.activation(stt[:, 1:2], stt[:, 0:1], AF.Sqrt),
                           reads=("st",), writes=("stb",))
                    p.step("vector", lambda v: _last(
                        v.tensor_scalar(stt[:, 2:3], stt[:, 1:2], 1e-12, None, ALU.max),
                        v.reciprocal(stt[:, 3:4], stt[:, 2:3]),
                        v.tensor_scalar(stt[:, 4:5], stt[:, 3:4], 32.0, None, ALU.mult)),
                        reads=("st", "stb"), writes=("st",))
                    for cp in range(4):
                        sl = (4 * i + cp) % 2
                        p.step("vector", lambda v, cp=cp, sl=sl: v.tensor_scalar(
                            xmini[:, sl * 256:(sl + 1) * 256], x1t[:, 2 * cp * 128:(2 * cp + 2) * 128],
                            stt[:, 4:5], None, ALU.mult),
                            reads=("st", "x1b"), writes=(("xm", sl),))

                        def tr2(te, sl=sl):
                            te.transpose(pool[:, (6 + sl) * 512: (6 + sl) * 512 + 64].bitcast(BF16),
                                         xmini[:, sl * 256:sl * 256 + 128], idb[:])
                            return te.transpose(pool[:, (6 + sl) * 512 + 64: (6 + sl) * 512 + 128].bitcast(BF16),
                                                xmini[:, sl * 256 + 128:sl * 256 + 256], idb[:])
                        p.step("tensor", tr2, reads=(("xm", sl), "idb"), writes=(("trb", sl),))
                        p.step("scalar", lambda s, cp=cp, it4=it4, sl=sl: s.copy(
                            xhp[:].rearrange("p (c t) -> p c t", c=8)[:, 2 * cp:2 * cp + 2,
                                                                     it4 * 128:(it4 + 1) * 128],
                            pool[:, (6 + sl) * 512: (6 + sl) * 512 + 128].bitcast(BF16)
                            .rearrange("p (c t) -> p c t", c=2)),
                            reads=(("trb", sl),), writes=(("xp", it4, cp),))
                xp_all = tuple(("xp", j, cp) for j in range(4) for cp in range(4))
                for jt in range(4):
                    def mm_upl(te, jt=jt):
                        for kc in range(8):
                            mm = te.matmul(bank(jt % 2), w1l_sb[:, kc * 512 + jt * 128: kc * 512 + jt * 128 + 128],
                                           xhp[:, kc * 512: (kc + 1) * 512],
                                           start=(kc == 0), stop=(kc == 7))
                        return mm
                    p.step("tensor", mm_upl, reads=("w1l",) + xp_all, writes=(("upb", jt % 2),))
                    p.step("scalar", lambda s, jt=jt: s.activation(
                        hl[:, jt * 512:(jt + 1) * 512], bank(jt % 2), AF.Gelu, bias=b1c[:, jt:jt + 1]),
                        reads=(("upb", jt % 2), ("b1", jt)), writes=(("hl", jt),))
                for jt in range(32):
                    def mm_uph(te, jt=jt):
                        for kc in range(8):
                            mm = te.matmul(bank(jt % 2), w1h_sb[:, kc * 4096 + jt * 128: kc * 4096 + jt * 128 + 128],
                                           xhp[:, kc * 512: (kc + 1) * 512],
                                           start=(kc == 0), stop=(kc == 7))
                        return mm
                    p.step("tensor", mm_uph, reads=tuple(("w1hg", c) for c in range(8)) + xp_all,
                           writes=(("upb", jt % 2),))
                    p.step("scalar", lambda s, jt=jt: s.activation(
                        hh[:, jt * 512:(jt + 1) * 512], bank(jt % 2), AF.Gelu, bias=b1c[:, 4 + jt:5 + jt]),
                        reads=(("upb", jt % 2), ("b1", 4 + jt)), writes=(("hh", jt),))
                for it4 in range(4):
                    i = blk * 4 + it4
                    for jh in range(2):
                        def mm_downl(te, it4=it4, jh=jh):
                            for kc in range(4):
                                mm = te.matmul(bank(2 + jh), hl[:, kc * 512 + it4 * 128: kc * 512 + it4 * 128 + 128],
                                               w2l_sb[:, kc * 1024 + jh * 512: kc * 1024 + jh * 512 + 512],
                                               start=(kc == 0), stop=(kc == 3))
                            return mm
                        p.step("tensor", mm_downl,
                               reads=tuple(("hl", j) for j in range(4)) + ("w2l",),
                               writes=(("pod", jh),))
                        def mm_downh(te, it4=it4, jh=jh):
                            for kc in range(32):
                                mm = te.matmul(bank(4 + jh), hh[:, kc * 512 + it4 * 128: kc * 512 + it4 * 128 + 128],
                                               w2h_sb[:, kc * 1024 + jh * 512: kc * 1024 + jh * 512 + 512],
                                               start=(kc == 0), stop=(kc == 31))
                            return mm
                        p.step("tensor", mm_downh,
                               reads=tuple(("hh", j) for j in range(32)) + tuple(("w2h", h) for h in range(4)),
                               writes=(("podh", jh),))
                    p.step("sync", lambda e, i=i: e.dma_start(out=x1t[:], in_=x1i[i * 128:(i + 1) * 128, :]),
                           writes=("x1b",), inc=16)
                    p.step("sync", lambda e, i=i, it4=it4: e.dma_start(out=mks[:, 4 + it4 % 2], in_=mrow[i * 128:(i + 1) * 128]),
                           writes=(("mk2", it4 % 2),), inc=16)
                    p.step("vector", lambda v, it4=it4: _last(
                        v.tensor_tensor(y2[:, 0:512], bank(2), x1t[:, 0:512], ALU.add),
                        v.tensor_tensor(y2[:, 512:1024], bank(3), x1t[:, 512:1024], ALU.add),
                        v.tensor_tensor(y2[:, 0:512], y2[:, 0:512], b2lf[:, 0:512], ALU.add),
                        v.tensor_tensor(y2[:, 512:1024], y2[:, 512:1024], b2lf[:, 512:1024], ALU.add),
                        v.scalar_tensor_tensor(y2[:, 0:512], bank(4), mks[:, 4 + it4 % 2:5 + it4 % 2], y2[:, 0:512], ALU.mult, ALU.add),
                        v.scalar_tensor_tensor(y2[:, 512:1024], bank(5), mks[:, 4 + it4 % 2:5 + it4 % 2], y2[:, 512:1024], ALU.mult, ALU.add),
                        v.scalar_tensor_tensor(y2[:, 0:512], b2hf[:, 0:512], mks[:, 4 + it4 % 2:5 + it4 % 2], y2[:, 0:512], ALU.mult, ALU.add),
                        v.scalar_tensor_tensor(y2[:, 512:1024], b2hf[:, 512:1024], mks[:, 4 + it4 % 2:5 + it4 % 2], y2[:, 512:1024], ALU.mult, ALU.add)),
                        reads=(("pod", 0), ("pod", 1), ("podh", 0), ("podh", 1), "x1b", ("mk2", it4 % 2), "b2l", "b2h"), writes=("y2",))
                    p.step("vector", lambda v: _last(
                        v.tensor_tensor(sqf[:], y2[:], y2[:], ALU.mult),
                        v.reduce_sum(stt[:, 1:2], sqf[:], AXX),
                        v.reduce_sum(stt[:, 0:1], y2[:], AXX),
                        v.tensor_scalar(stt[:, 2:3], stt[:, 0:1], 1.0 / D, None, ALU.mult),
                        v.tensor_tensor(stt[:, 3:4], stt[:, 2:3], stt[:, 2:3], ALU.mult),
                        v.scalar_tensor_tensor(stt[:, 3:4], stt[:, 1:2], 1.0 / D, stt[:, 3:4], ALU.mult, ALU.subtract),
                        v.tensor_scalar(stt[:, 3:4], stt[:, 3:4], LN_EPS, None, ALU.add)),
                        reads=("y2",), writes=("st", "sqf"))
                    p.step("scalar", lambda s: s.activation(stt[:, 6:7], stt[:, 3:4], AF.Sqrt),
                           reads=("st",), writes=("stb",))
                    p.step("vector", lambda v: _last(
                        v.reciprocal(stt[:, 4:5], stt[:, 6:7]),
                        v.tensor_scalar(stt[:, 5:6], stt[:, 2:3], stt[:, 4:5], -1.0, ALU.mult, ALU.mult),
                        v.tensor_scalar(sqf[:], y2[:], stt[:, 4:5], stt[:, 5:6], ALU.mult, ALU.add)),
                        reads=("st", "stb", "y2"), writes=("x2t", "sqf"))
                    p.step("sync", lambda e, i=i: e.dma_start(out=x2o[i * 128:(i + 1) * 128, :], in_=sqf[:]),
                           reads=("x2t",), writes=(("x2d", i),), inc=16)
            p.step("gpsimd", lambda g: g.nop(), reads=tuple(("x2d", i) for i in range(32)))
            p.emit(stF)


_CACHE = {}


def _ident():
    return np.eye(128, dtype=np.float32)


def _sel2c():
    m = np.zeros((2, 128), np.float32)
    m[0, 0:64] = 1.0
    m[1, 64:128] = 1.0
    return m


def _b2c():
    m = np.zeros((128, 2), np.float32)
    m[0:64, 0] = 1.0
    m[64:128, 1] = 1.0
    return m


def _p1_inmaps(inputs):
    x = np.asarray(inputs["L0_attn_x"])
    ident = _ident()
    maps = []
    for core in range(8):
        b, h = core // 2, core % 2
        xo = np.ascontiguousarray(x[b, h * TOK:(h + 1) * TOK, :])
        xp = np.ascontiguousarray(x[b, (1 - h) * TOK:(2 - h) * TOK, :])
        maps.append({
            "xo": xo, "xp": xp,
            "wqT": np.ascontiguousarray(np.asarray(inputs["L0_attn_w_q"]).T),
            "wkT": np.ascontiguousarray(np.asarray(inputs["L0_attn_w_k"]).T),
            "wvT": np.ascontiguousarray(np.asarray(inputs["L0_attn_w_v"]).T),
            "woT": np.ascontiguousarray(np.asarray(inputs["L0_attn_w_o"]).T),
            "lng": np.asarray(inputs["L0_attn_ln_gamma"]),
            "lnb": np.asarray(inputs["L0_attn_ln_beta"]),
            "qg": np.asarray(inputs["L0_attn_q_gamma"]),
            "kg": np.asarray(inputs["L0_attn_k_gamma"]),
            "bo": np.asarray(inputs["L0_attn_b_o"]),
            "rtok": np.asarray(inputs["L0_ffn_routing_token"]),
            "identf": ident, "sel2c": _sel2c(), "b2c": _b2c(),
        })
    return maps


def run_p1(inputs):
    if "p1" not in _CACHE:
        _CACHE["p1"] = build_p1()
    res = run_bass_kernel_spmd(_CACHE["p1"], _p1_inmaps(inputs), core_ids=list(range(8)))
    return res.results


def build_p3():
    nc = bass.Bass("TRN2", target_bir_lowering=False)
    io = {}
    io["x2i"] = nc.dram_tensor("x2i", [TOK, D], F32, kind="ExternalInput").ap()
    for nm in ("WqT", "WkT", "WvT"):
        io[nm] = nc.dram_tensor(nm, [D, D], F32, kind="ExternalInput").ap()
    for nm in ("bq", "bk", "bv"):
        io[nm] = nc.dram_tensor(nm, [D], F32, kind="ExternalInput").ap()
    io["identf"] = nc.dram_tensor("identf", [128, 128], F32, kind="ExternalInput").ap()
    io["qf_dr"] = nc.dram_tensor("qf_dr", [16, 64, TOK], BF16, kind="ExternalOutput").ap()
    io["kf_dr"] = nc.dram_tensor("kf_dr", [16, 64, TOK], BF16, kind="ExternalOutput").ap()
    io["kt_dr"] = nc.dram_tensor("kt_dr", [TOK, D], BF16, kind="ExternalOutput").ap()
    io["vt_dr"] = nc.dram_tensor("vt_dr", [32, 128, 1040], BF16, kind="ExternalOutput").ap()
    io["state_o"] = nc.dram_tensor("state_o", [64, 1040], F32, kind="ExternalOutput").ap()
    emit_p3(nc, io)
    return nc


def emit_p3(nc, io):
    x2i, WqT, WkT, WvT = io["x2i"], io["WqT"], io["WkT"], io["WvT"]
    bq, bk, bv, identf = io["bq"], io["bk"], io["bv"], io["identf"]
    qf_dr, kf_dr, kt_dr, vt_dr, state_o = io["qf_dr"], io["kf_dr"], io["kt_dr"], io["vt_dr"], io["state_o"]

    stack = contextlib.ExitStack()
    with stack:
        sbt = lambda *a: stack.enter_context(nc.sbuf_tensor(*a))
        idf = sbt("idf3", [128, 128], F32)
        idb = sbt("idb3", [128, 128], BF16)
        w3q = sbt("w3q", [128, 8 * 1024], BF16)
        w3k = sbt("w3k", [128, 8 * 1024], BF16)
        w3v = sbt("w3v", [128, 8 * 1024], BF16)
        bqf = sbt("bqf", [128, D], F32)
        bkf = sbt("bkf", [128, D], F32)
        bvf = sbt("bvf", [128, D], F32)
        xt = sbt("xt3", [128, 2 * D], F32)
        xb16 = sbt("xb163", [128, 2 * D], BF16)
        x2p = sbt("x2p", [128, 2 * 8 * 128], BF16)
        raw = sbt("raw3", [128, 2 * D], F32)
        t0f = sbt("t0f", [128, 2 * D], F32)
        t1f = sbt("t1f", [128, 2 * D], F32)
        phb = sbt("phb", [128, 4 * D], BF16)
        stg = sbt("stg3", [128, 4 * D], BF16)
        vxs = sbt("vxs", [128, 2 * 1040], BF16)
        sst = sbt("sst", [64, 1040], F32)
        pool = stack.enter_context(nc.psum_tensor("pool3", [128, 4096], F32))
        p = Prog(nc, "p3")
        p.step("sync", lambda e: e.dma_start(out=idf[:], in_=identf), writes=("idf",), inc=16)
        p.step("vector", lambda v: _last(
            v.tensor_copy(idb[:], idf[:]),
            v.memset(sst[:], 0.0),
            v.memset(vxs[:].rearrange("p (b c t) -> p b c t", b=2, c=16)[:, :, :, 64:65], 1.0)),
            reads=("idf",), writes=("idb", "vxones", "sst0"))
        for wsb, wdr, tok in ((w3q, WqT, "wq"), (w3k, WkT, "wk"), (w3v, WvT, "wv")):
            p.step("gpsimd", lambda g, wsb=wsb, wdr=wdr: g.dma_start(
                out=wsb[:].rearrange("p (c j) -> p c j", c=8),
                in_=wdr.rearrange("(c p) j -> p c j", p=128)),
                writes=(tok,), inc=16)
        for bf, bdr, tok in ((bqf, bq, "bqf"), (bkf, bk, "bkf"), (bvf, bv, "bvf")):
            p.step("sync", lambda e, bf=bf, bdr=bdr: _repl(e, bf, bdr), writes=(tok,), inc=16)
        for i in range(32):
            ip = i % 2
            xti = xt[:, ip * D:(ip + 1) * D]
            xbi = xb16[:, ip * D:(ip + 1) * D]
            x2pi = x2p[:, ip * 1024:(ip + 1) * 1024]
            p.step("sync", lambda e, i=i, xti=xti: e.dma_start(out=xti, in_=x2i[i * 128:(i + 1) * 128, :]),
                   writes=(("xt", ip),), inc=16)
            p.step("vector", lambda v, xti=xti, xbi=xbi: v.tensor_copy(xbi, xti),
                   reads=(("xt", ip),), writes=(("xb", ip),))
            for cp in range(4):
                sl = cp % 2

                def trx(te, cp=cp, sl=sl, xbi=xbi):
                    te.transpose(pool[:, sl * 512: sl * 512 + 64].bitcast(BF16),
                                 xbi[:, 2 * cp * 128:(2 * cp + 1) * 128], idb[:])
                    return te.transpose(pool[:, sl * 512 + 64: sl * 512 + 128].bitcast(BF16),
                                        xbi[:, (2 * cp + 1) * 128:(2 * cp + 2) * 128], idb[:])
                p.step("tensor", trx, reads=(("xb", ip), "idb"), writes=(("tr", sl),))
                p.step("scalar", lambda s, cp=cp, sl=sl, x2pi=x2pi: s.copy(
                    x2pi[:, 2 * cp * 128:(2 * cp + 2) * 128],
                    pool[:, sl * 512: sl * 512 + 128].bitcast(BF16)),
                    reads=(("tr", sl),), writes=(("xp", ip, cp),))
            for pidx, (pj, wsb, bf, wtok, btok) in enumerate((
                    ("q", w3q, bqf, "wq", "bqf"),
                    ("v", w3v, bvf, "wv", "bvf"),
                    ("k", w3k, bkf, "wk", "bkf"))):
                pp = (3 * i + pidx) % 2
                rawp = raw[:, pp * D:(pp + 1) * D]
                t0p = t0f[:, pp * D:(pp + 1) * D]
                t1p = t1f[:, pp * D:(pp + 1) * D]
                php = phb[:, (2 * ip + (0 if pj == "q" else 1)) * D:
                           (2 * ip + (0 if pj == "q" else 1)) * D + D]
                for jh in range(2):
                    def mm_p3(te, wsb=wsb, jh=jh, x2pi=x2pi):
                        for kc in range(8):
                            mm = te.matmul(pool[:, (2 + jh) * 512:(3 + jh) * 512],
                                           x2pi[:, kc * 128:(kc + 1) * 128],
                                           wsb[:, kc * 1024 + jh * 512: kc * 1024 + jh * 512 + 512],
                                           start=(kc == 0), stop=(kc == 7))
                        return mm
                    p.step("tensor", mm_p3, reads=(wtok,) + tuple(("xp", ip, cp) for cp in range(4)),
                           writes=(("pb", jh),))
                p.step("vector", lambda v, bf=bf, rawp=rawp: v.tensor_tensor(
                    rawp, pool[:, 1024:2048], bf[:], ALU.add),
                    reads=(("pb", 0), ("pb", 1), btok), writes=(("raw", pp),))
                if pj == "v":
                    p.step("vector", lambda v, i=i, rawp=rawp: v.tensor_copy(
                        vxs[:, (i % 2) * 1040:(i % 2) * 1040 + 1040].rearrange("p (c t) -> p c t", c=16)[:, :, 0:64],
                        rawp.rearrange("p (c t) -> p c t", c=16)),
                        reads=(("raw", pp), "vxones"), writes=(("vx", i % 2),))
                    p.step("sync", lambda e, i=i: e.dma_start(
                        out=vt_dr[i], in_=vxs[:, (i % 2) * 1040:(i % 2) * 1040 + 1040]),
                        reads=(("vx", i % 2),), writes=(("vtd", i),), inc=16)
                else:
                    p.step("vector", lambda v, rawp=rawp, t0p=t0p: v.tensor_scalar(
                        t0p, rawp, 0.0, None, ALU.min),
                        reads=(("raw", pp),), writes=(("t0", pp),))
                    p.step("scalar", lambda s, t0p=t0p, t1p=t1p: s.activation(t1p, t0p, AF.Exp),
                           reads=(("t0", pp),), writes=(("t1", pp),))
                    p.step("vector", lambda v, rawp=rawp, t1p=t1p, php=php: v.scalar_tensor_tensor(
                        php, rawp, 0.0, t1p, ALU.max, ALU.add),
                        reads=(("raw", pp), ("t1", pp)), writes=(("ph", pj, ip),))
                    if pj == "k":
                        p.step("sync", lambda e, i=i, php=php: e.dma_start(
                            out=kt_dr[i * 128:(i + 1) * 128, :], in_=php),
                            reads=(("ph", pj, ip),), writes=(("ktd", i),), inc=16)
                        def mm_st(te, i=i, php=php):
                            for h in range(16):
                                mm = te.matmul(pool[0:64, 2048 + h * 128: 2048 + h * 128 + 65],
                                               php[:, h * 64:(h + 1) * 64],
                                               vxs[:, (i % 2) * 1040 + h * 65:(i % 2) * 1040 + (h + 1) * 65],
                                               start=True, stop=True)
                            return mm
                        p.step("tensor", mm_st, reads=(("ph", pj, ip), ("vx", i % 2)), writes=("stacc",))
                        p.step("vector", lambda v: v.tensor_tensor(
                            sst[:], sst[:],
                            pool[0:64, 2048:4096].rearrange("p (c t) -> p c t", c=16)[:, :, 0:65], ALU.add),
                            reads=("stacc", "sst0"), writes=("sstv",))
                    soff = (2 * ip + (0 if pj == "q" else 1)) * D
                    for cp in range(4):
                        sl = cp % 2

                        def trp(te, cp=cp, sl=sl, php=php):
                            te.transpose(pool[:, sl * 512: sl * 512 + 64].bitcast(BF16),
                                         php[:, 2 * cp * 128:(2 * cp + 1) * 128], idb[:])
                            return te.transpose(pool[:, sl * 512 + 64: sl * 512 + 128].bitcast(BF16),
                                                php[:, (2 * cp + 1) * 128:(2 * cp + 2) * 128], idb[:])
                        p.step("tensor", trp, reads=(("ph", pj, ip), "idb"), writes=(("tr", sl),))
                        p.step("scalar", lambda s, cp=cp, sl=sl, soff=soff: s.copy(
                            stg[:, soff + 2 * cp * 128:soff + (2 * cp + 2) * 128],
                            pool[:, sl * 512: sl * 512 + 128].bitcast(BF16)),
                            reads=(("tr", sl),), writes=(("st", pj, ip, cp),))
                    dst = qf_dr if pj == "q" else kf_dr
                    p.step("sync", lambda e, i=i, dst=dst, soff=soff: e.dma_start(
                        out=dst[:, :, i * 128:(i + 1) * 128].rearrange("(c e) q t -> (e q) c t", e=2),
                        in_=stg[:, soff:soff + D].rearrange("p (c t) -> p c t", c=8)),
                        reads=tuple(("st", pj, ip, cp) for cp in range(4)), writes=(("spl", pj, i),), inc=16)
        p.step("sync", lambda e: e.dma_start(out=state_o, in_=sst[:]), reads=("sstv",), writes=("std",), inc=16)
        p.step("gpsimd", lambda g: g.nop(),
               reads=("std",) + tuple(("spl", "q", i) for i in range(32)) + tuple(("spl", "k", i) for i in range(32)) +
                     tuple(("vtd", i) for i in range(32)) + tuple(("ktd", i) for i in range(32)))
        p.emit(stack)


def build_p4():
    nc = bass.Bass("TRN2", target_bir_lowering=False)
    io = {}
    io["x2i"] = nc.dram_tensor("x2i", [TOK, D], F32, kind="ExternalInput").ap()
    io["qf_dr"] = nc.dram_tensor("qf_dr", [16, 64, TOK], BF16, kind="ExternalInput").ap()
    io["kf_dr"] = nc.dram_tensor("kf_dr", [16, 64, TOK], BF16, kind="ExternalInput").ap()
    io["kt_dr"] = nc.dram_tensor("kt_dr", [TOK, D], BF16, kind="ExternalInput").ap()
    io["vt_dr"] = nc.dram_tensor("vt_dr", [32, 128, 1040], BF16, kind="ExternalInput").ap()
    io["state_prev"] = nc.dram_tensor("state_prev", [64, 1040], F32, kind="ExternalInput").ap()
    io["WoT"] = nc.dram_tensor("WoT", [D, D], F32, kind="ExternalInput").ap()
    io["bo4"] = nc.dram_tensor("bo4", [D], F32, kind="ExternalInput").ap()
    io["identf"] = nc.dram_tensor("identf", [128, 128], F32, kind="ExternalInput").ap()
    io["trilc"] = nc.dram_tensor("trilc", [128, 128], F32, kind="ExternalInput").ap()
    io["x3o"] = nc.dram_tensor("x3o", [TOK, D], F32, kind="ExternalOutput").ap()
    emit_p4(nc, io)
    return nc


def emit_p4(nc, io, state_emit=None):
    x2i, qf_dr, kf_dr, kt_dr, vt_dr = io["x2i"], io["qf_dr"], io["kf_dr"], io["kt_dr"], io["vt_dr"]
    state_prev, WoT, bo, identf, trilc = io["state_prev"], io["WoT"], io["bo4"], io["identf"], io["trilc"]
    x3o = io["x3o"]

    stack = contextlib.ExitStack()
    with stack:
        sbt = lambda *a: stack.enter_context(nc.sbuf_tensor(*a))
        idf = sbt("idf4", [128, 128], F32)
        trib = sbt("trib", [128, 128], BF16)
        onesf = sbt("onesf4", [1, 64], F32)
        wo_sb = sbt("wo4", [128, 8 * 1024], BF16)
        bof = sbt("bof4", [128, D], F32)
        qfc = sbt("qfc", [128, 2 * 1024], BF16)
        kfc = sbt("kfc", [128, 2 * 1024], BF16)
        ktc = sbt("ktc", [128, 2 * 1024], BF16)
        vtc = sbt("vtc", [128, 2 * 1040], BF16)
        state = sbt("state4", [64, 1040], F32)
        stb16 = sbt("stb164", [128, 1040], BF16)
        psb = sbt("psb4", [128, 2 * 256], BF16)
        rec1 = sbt("rec14", [1, 2 * 256], F32)
        den1 = sbt("den14", [128, 2 * 256], F32)
        an1 = sbt("an14", [64, 2 * 256], BF16)
        ao = sbt("ao4", [128, 8 * 128], BF16)
        xt = sbt("xt4", [128, D], F32)
        y2 = sbt("y24", [128, D], F32)
        sqf = sbt("sqf4", [128, D], F32)
        stt = sbt("stt4", [128, 8], F32)
        pool = stack.enter_context(nc.psum_tensor("pool4", [128, 4096], F32))
        p = Prog(nc, "p4")
        p.step("sync", lambda e: e.dma_start(out=idf[:], in_=identf), writes=("idf",), inc=16)
        p.step("gpsimd", lambda g: g.dma_start(out=trib[:], in_=trilc), writes=("tri",), inc=16)
        p.step("gpsimd", lambda g: g.dma_start(out=wo_sb[:].rearrange("p (c j) -> p c j", c=8),
                                               in_=WoT.rearrange("(c p) j -> p c j", p=128)),
               writes=("wo",), inc=16)
        p.step("sync", lambda e: _repl(e, bof, bo), writes=("bo",), inc=16)
        p.step("sync", lambda e: e.dma_start(out=state[:], in_=state_prev), writes=("state",), inc=16)
        p.step("vector", lambda v: _last(
            v.memset(onesf[:], 1.0),
            v.memset(stb16[:], 0.0)), writes=("onesf", "stbo", "stbe"))
        st4 = state.rearrange("k (hp e c) -> k hp e c", hp=8, e=2)

        def chunk_dmas(c):
            sl = c % 2
            for e2 in range(2):
                p.step("sync", lambda e, c=c, sl=sl, e2=e2: e.dma_start(
                    out=qfc[e2 * 64:(e2 + 1) * 64, sl * 1024: (sl + 1) * 1024].rearrange(
                        "k (hp t) -> k hp t", hp=8),
                    in_=qf_dr.rearrange("(hp e) k t -> e k hp t", e=2)[e2, :, :, c * 128:(c + 1) * 128]),
                    writes=(("qc", sl, e2),), inc=16)
                p.step("sync", lambda e, c=c, sl=sl, e2=e2: e.dma_start(
                    out=kfc[e2 * 64:(e2 + 1) * 64, sl * 1024: (sl + 1) * 1024].rearrange(
                        "k (hp t) -> k hp t", hp=8),
                    in_=kf_dr.rearrange("(hp e) k t -> e k hp t", e=2)[e2, :, :, c * 128:(c + 1) * 128]),
                    writes=(("kc", sl, e2),), inc=16)
            p.step("sync", lambda e, c=c, sl=sl: e.dma_start(
                out=ktc[:, sl * 1024: (sl + 1) * 1024], in_=kt_dr[c * 128:(c + 1) * 128, :]),
                writes=(("kt", sl),), inc=16)
            p.step("sync", lambda e, c=c, sl=sl: e.dma_start(
                out=vtc[:, sl * 1040: (sl + 1) * 1040], in_=vt_dr[c]),
                writes=(("vt", sl),), inc=16)

        def stb_copies():
            p.step("vector", lambda v: v.tensor_copy(
                stb16[0:64, 0:520].rearrange("k (hp c) -> k hp c", hp=8), st4[:, :, 0, :]),
                reads=("state",), writes=("stbe",))
            p.step("gpsimd", lambda g: g.dma_start(
                out=stb16[64:128, 520:1040].rearrange("k (hp c) -> k hp c", hp=8), in_=st4[:, :, 1, :]),
                reads=("state",), writes=("stbo",), inc=16)

        for c in range(32):
            sl = c % 2
            if c == 0:
                chunk_dmas(0)
                stb_copies()
            def stA(hp, sl=sl):
                def f(te, hp=hp, sl=sl):
                    for e2 in range(2):
                        mm = te.matmul(
                            pool[:, e2 * 512: e2 * 512 + 128],
                            kfc[e2 * 64:(e2 + 1) * 64, sl * 1024 + hp * 128: sl * 1024 + (hp + 1) * 128],
                            qfc[e2 * 64:(e2 + 1) * 64, sl * 1024 + hp * 128: sl * 1024 + (hp + 1) * 128],
                            start=True, stop=True, tile_position=(e2 * 64, 0))
                    return mm
                p.step("tensor", f, reads=(("qc", sl, 0), ("qc", sl, 1), ("kc", sl, 0), ("kc", sl, 1)),
                       writes=(("bank", 0), ("bank", 1)))

            def stB(hp):
                ps = (hp % 2) * 256
                def f(v, ps=ps):
                    for e2 in range(2):
                        mm = v.tensor_tensor(psb[:, ps + e2 * 128: ps + e2 * 128 + 128],
                                             pool[:, e2 * 512: e2 * 512 + 128], trib[:], ALU.mult)
                    return mm
                p.step("vector", f, reads=(("bank", 0), ("bank", 1), "tri"), writes=(("psb", hp % 2),))

            def stC(hp, sl=sl):
                nb = 2 + hp % 2
                nc0 = nb * 512
                ps = (hp % 2) * 256
                def f(te, hp=hp, sl=sl, nb=nb, nc0=nc0, ps=ps):
                    for e2 in range(2):
                        h = 2 * hp + e2
                        te.matmul(pool[0:65, nc0 + e2 * 256: nc0 + e2 * 256 + 128],
                                  vtc[:, sl * 1040 + h * 65: sl * 1040 + (h + 1) * 65],
                                  psb[:, ps + e2 * 128: ps + e2 * 128 + 128], start=True, stop=False)
                        te.matmul(pool[0:65, nc0 + e2 * 256: nc0 + e2 * 256 + 128],
                                  stb16[:, e2 * 520 + hp * 65: e2 * 520 + (hp + 1) * 65],
                                  qfc[:, sl * 1024 + hp * 128: sl * 1024 + (hp + 1) * 128],
                                  start=False, stop=True)
                    for e2 in range(2):
                        h = 2 * hp + e2
                        mm = te.matmul(pool[0:64, 2048 + h * 128: 2048 + h * 128 + 65],
                                       ktc[:, sl * 1024 + h * 64: sl * 1024 + (h + 1) * 64],
                                       vtc[:, sl * 1040 + h * 65: sl * 1040 + (h + 1) * 65],
                                       start=True, stop=True)
                    return mm
                p.step("tensor", f,
                       reads=(("psb", hp % 2), ("vt", sl), ("kt", sl), "stbe", "stbo",
                              ("qc", sl, 0), ("qc", sl, 1)),
                       writes=(("bank", nb), ("upd", 2 * hp), ("upd", 2 * hp + 1)))

            def stD(hp):
                nb = 2 + hp % 2
                nc0 = nb * 512
                ds = (hp % 2) * 256
                def f(s, nc0=nc0, ds=ds):
                    for e2 in range(2):
                        mm = s.activation(den1[64:65, ds + e2 * 128: ds + e2 * 128 + 128],
                                          pool[64:65, nc0 + e2 * 256: nc0 + e2 * 256 + 128],
                                          AF.Copy, bias=1e-6, scale=1.0)
                    return mm
                p.step("scalar", f, reads=(("bank", nb),), writes=(("den", hp % 2),))

            def stE(hp):
                ds = (hp % 2) * 256
                p.step("vector", lambda v, ds=ds: v.reciprocal(
                    rec1[0:1, ds:ds + 256], den1[64:65, ds:ds + 256]),
                    reads=(("den", hp % 2),), writes=(("rec", hp % 2),))

            def stF(hp):
                nb = 2 + hp % 2
                nc0 = nb * 512
                ds = (hp % 2) * 256
                def f(s, nc0=nc0, ds=ds):
                    for e2 in range(2):
                        mm = s.copy(an1[:, ds + e2 * 128: ds + e2 * 128 + 128],
                                    pool[0:64, nc0 + e2 * 256: nc0 + e2 * 256 + 128])
                    return mm
                p.step("scalar", f, reads=(("bank", nb),), writes=(("an1", hp % 2),))

            def stG(hp):
                nb = 2 + hp % 2
                nc0 = nb * 512
                ds = (hp % 2) * 256
                def f(te, nc0=nc0, ds=ds):
                    for e2 in range(2):
                        mm = te.matmul(pool[0:64, nc0 + e2 * 256 + 128: nc0 + e2 * 256 + 256],
                                       onesf[0:1, :], rec1[0:1, ds + e2 * 128: ds + e2 * 128 + 128],
                                       start=True, stop=True)
                    return mm
                p.step("tensor", f, reads=(("rec", hp % 2), "onesf", ("an1", hp % 2)),
                       writes=(("bank", nb),))

            def stH(hp):
                nb = 2 + hp % 2
                nc0 = nb * 512
                ds = (hp % 2) * 256
                def f(v, hp=hp, nc0=nc0, ds=ds):
                    for e2 in range(2):
                        mm = v.tensor_tensor(ao[e2 * 64:(e2 + 1) * 64, hp * 128:(hp + 1) * 128],
                                             an1[:, ds + e2 * 128: ds + e2 * 128 + 128],
                                             pool[0:64, nc0 + e2 * 256 + 128: nc0 + e2 * 256 + 256],
                                             ALU.mult)
                    return mm
                p.step("vector", f, reads=(("an1", hp % 2), ("bank", nb)),
                       writes=(("ao", 2 * hp), ("ao", 2 * hp + 1)))

            for hp in range(8):
                if not (c > 0 and hp < 2):
                    stA(hp)
                    stB(hp)
                if hp >= 1:
                    stC(hp - 1); stD(hp - 1); stE(hp - 1); stF(hp - 1)
                if hp >= 2:
                    stG(hp - 2); stH(hp - 2)
            stC(7); stD(7); stE(7); stF(7)
            stG(6); stH(6)
            stG(7); stH(7)
            if c + 1 < 32:
                chunk_dmas(c + 1)
                stA(0, sl=(c + 1) % 2)
                stB(0)
                stA(1, sl=(c + 1) % 2)
                stB(1)
            p.step("vector", lambda v: v.tensor_tensor(
                state[:], state[:], pool[0:64, 2048:4096].rearrange("p (c t) -> p c t", c=16)[:, :, 0:65],
                ALU.add),
                reads=("state",) + tuple(("upd", h) for h in range(16)), writes=("state",))
            for jh in range(2):
                def mm_o4(te, jh=jh):
                    for kc in range(8):
                        mm = te.matmul(pool[:, (4 + jh) * 512:(5 + jh) * 512],
                                       ao[:, kc * 128:(kc + 1) * 128],
                                       wo_sb[:, kc * 1024 + jh * 512: kc * 1024 + jh * 512 + 512],
                                       start=(kc == 0), stop=(kc == 7))
                    return mm
                p.step("tensor", mm_o4, reads=tuple(("ao", h) for h in range(16)) +
                       ("wo", "state"),
                       writes=(("bank", 4 + jh),))
            p.step("sync", lambda e, c=c: e.dma_start(out=xt[:], in_=x2i[c * 128:(c + 1) * 128, :]),
                   writes=("xt",), inc=16)
            p.step("vector", lambda v: _last(
                v.tensor_tensor(y2[:, 0:512], pool[:, 2048:2560], xt[:, 0:512], ALU.add),
                v.tensor_tensor(y2[:, 512:1024], pool[:, 2560:3072], xt[:, 512:1024], ALU.add),
                v.tensor_tensor(y2[:, 0:512], y2[:, 0:512], bof[:, 0:512], ALU.add),
                v.tensor_tensor(y2[:, 512:1024], y2[:, 512:1024], bof[:, 512:1024], ALU.add)),
                reads=(("bank", 4), ("bank", 5), "xt", "bo"), writes=("y2",))
            p.step("vector", lambda v: _last(
                v.tensor_tensor(sqf[:], y2[:], y2[:], ALU.mult),
                v.reduce_sum(stt[:, 1:2], sqf[:], AXX),
                v.reduce_sum(stt[:, 0:1], y2[:], AXX),
                v.tensor_scalar(stt[:, 2:3], stt[:, 0:1], 1.0 / D, None, ALU.mult),
                v.tensor_tensor(stt[:, 3:4], stt[:, 2:3], stt[:, 2:3], ALU.mult),
                v.scalar_tensor_tensor(stt[:, 3:4], stt[:, 1:2], 1.0 / D, stt[:, 3:4], ALU.mult, ALU.subtract),
                v.tensor_scalar(stt[:, 3:4], stt[:, 3:4], LN_EPS, None, ALU.add)),
                reads=("y2",), writes=("st", "sqf"))
            p.step("scalar", lambda s: s.activation(stt[:, 6:7], stt[:, 3:4], AF.Sqrt),
                   reads=("st",), writes=("stb2",))
            p.step("vector", lambda v: _last(
                v.reciprocal(stt[:, 4:5], stt[:, 6:7]),
                v.tensor_scalar(stt[:, 5:6], stt[:, 2:3], stt[:, 4:5], -1.0, ALU.mult, ALU.mult),
                v.tensor_scalar(sqf[:], y2[:], stt[:, 4:5], stt[:, 5:6], ALU.mult, ALU.add)),
                reads=("st", "stb2", "y2"), writes=("x3t", "sqf"))
            p.step("sync", lambda e, c=c: e.dma_start(out=x3o[c * 128:(c + 1) * 128, :], in_=sqf[:]),
                   reads=("x3t",), writes=(("x3d", c),), inc=16)
            if c + 1 < 32:
                stb_copies()
        p.step("gpsimd", lambda g: g.nop(), reads=tuple(("x3d", c) for c in range(32)))
        p.emit(stack)


def build_p5():
    nc = bass.Bass("TRN2", target_bir_lowering=False)
    io = {}
    io["x3i"] = nc.dram_tensor("x3i", [TOK, D], F32, kind="ExternalInput").ap()
    io["Wr"] = nc.dram_tensor("Wr", [D, 16], F32, kind="ExternalInput").ap()
    io["Wg"] = nc.dram_tensor("Wg", [16, D, 512], F32, kind="ExternalInput").ap()
    io["Wu"] = nc.dram_tensor("Wu", [16, D, 512], F32, kind="ExternalInput").ap()
    io["Wd"] = nc.dram_tensor("Wd", [16, 512, D], F32, kind="ExternalInput").ap()
    io["Sg"] = nc.dram_tensor("Sg", [D, D], F32, kind="ExternalInput").ap()
    io["Su"] = nc.dram_tensor("Su", [D, D], F32, kind="ExternalInput").ap()
    io["Sd"] = nc.dram_tensor("Sd", [D, D], F32, kind="ExternalInput").ap()
    io["identf"] = nc.dram_tensor("identf", [128, 128], F32, kind="ExternalInput").ap()
    io["sel16c"] = nc.dram_tensor("sel16c", [16, 2048], F32, kind="ExternalInput").ap()
    io["outo"] = nc.dram_tensor("outo", [TOK, D], F32, kind="ExternalOutput").ap()
    emit_p5(nc, io)
    return nc


def emit_p5(nc, io):
    x3i, Wr, Wg, Wu, Wd = io["x3i"], io["Wr"], io["Wg"], io["Wu"], io["Wd"]
    Sg, Su, Sd, identf, sel16c = io["Sg"], io["Su"], io["Sd"], io["identf"], io["sel16c"]
    outo = io["outo"]
    xf_dr = nc.dram_tensor("xf_dr5", [8, 128, TOK], BF16).ap()

    stack = contextlib.ExitStack()
    with stack:
        sbt = lambda *a: stack.enter_context(nc.sbuf_tensor(*a))
        idf = sbt("idf5", [128, 128], F32)
        idb = sbt("idb5", [128, 128], BF16)
        onesb1 = sbt("onesb15", [1, 128], BF16)
        acc = sbt("acc5", [128, 32 * 1024], BF16)
        cwT = sbt("cwT5", [16, TOK], BF16)
        sel16b = sbt("sel16b", [16, 2048], BF16)
        pool = stack.enter_context(nc.psum_tensor("pool5", [128, 4096], F32))

        with contextlib.ExitStack() as stP:
            s5 = lambda *a: stP.enter_context(nc.sbuf_tensor(*a))
            wrf = s5("wrf", [128, 8 * 16], F32)
            wrb = s5("wrb", [128, 8 * 16], BF16)
            wrl = s5("wrl", [128, 8 * 16], BF16)
            wrt = s5("wrt", [128, 8 * 16], F32)
            xt = s5("xt5", [128, 2 * D], F32)
            xhi = s5("xhi5", [128, 2 * D], BF16)
            xlo = s5("xlo5", [128, 2 * D], BF16)
            stg = s5("stg5", [128, 2 * D], BF16)
            xlp = s5("xlp5", [128, 2 * D], BF16)
            prb = s5("prb5", [128, 2 * 16], F32)
            f1 = s5("f15", [128, 2 * 16], F32)
            w2 = s5("w25", [128, 2 * 16], F32)
            sc = s5("sc5", [128, 2 * 16], F32)
            onesd = s5("onesd5", [128, 16], F32)
            cwb = s5("cwb5", [128, 2 * 16], BF16)
            rr = s5("rr5", [128, 2 * 4], F32)
            p = Prog(nc, "p5a")
            p.step("sync", lambda e: e.dma_start(out=idf[:], in_=identf), writes=("idf",), inc=16)
            p.step("sync", lambda e: e.dma_start(out=wrf[:].rearrange("p (c j) -> p c j", c=8),
                                                 in_=Wr.rearrange("(c p) j -> p c j", p=128)),
                   writes=("wrf",), inc=16)
            p.step("gpsimd", lambda g: g.dma_start(out=sel16b[:], in_=sel16c), writes=("sel16",), inc=16)
            p.step("vector", lambda v: _last(
                v.tensor_copy(idb[:], idf[:]),
                v.memset(onesb1[:], 1.0),
                v.memset(onesd[:], 1.0),
                v.tensor_copy(wrb[:], wrf[:]),
                v.tensor_copy(wrt[:], wrb[:]),
                v.tensor_tensor(wrt[:], wrf[:], wrt[:], ALU.subtract),
                v.tensor_copy(wrl[:], wrt[:])),
                reads=("idf", "wrf"), writes=("idb", "wr2"))
            for i in range(32):
                pr = i % 2
                xti = xt[:, pr * D:(pr + 1) * D]
                xhii = xhi[:, pr * D:(pr + 1) * D]
                xloi = xlo[:, pr * D:(pr + 1) * D]
                xlpi = xlp[:, pr * D:(pr + 1) * D]
                prbi = prb[:, pr * 16:(pr + 1) * 16]
                f1i = f1[:, pr * 16:(pr + 1) * 16]
                w2i = w2[:, pr * 16:(pr + 1) * 16]
                sci = sc[:, pr * 16:(pr + 1) * 16]
                cwbi = cwb[:, pr * 16:(pr + 1) * 16]
                rri = rr[:, pr * 4:(pr + 1) * 4]
                lgb = pool[:, 2048 + pr * 1024:2064 + pr * 1024]
                ctb = pool[0:16, 2560 + pr * 1024:2624 + pr * 1024]
                p.step("sync", lambda e, i=i, xti=xti: e.dma_start(out=xti, in_=x3i[i * 128:(i + 1) * 128, :]),
                       writes=(("xt", pr),), inc=16)
                p.step("vector", lambda v, xti=xti, xhii=xhii, xloi=xloi: _last(
                    v.tensor_copy(xhii, xti),
                    v.tensor_tensor(xloi, xti, xhii, ALU.subtract)),
                    reads=(("xt", pr),), writes=(("xhi", pr), ("xlo", pr)))
                for cp in range(4):
                    sl = cp % 2

                    def trh(te, cp=cp, sl=sl, xhii=xhii):
                        te.transpose(pool[:, sl * 512: sl * 512 + 64].bitcast(BF16),
                                     xhii[:, 2 * cp * 128:(2 * cp + 1) * 128], idb[:])
                        return te.transpose(pool[:, sl * 512 + 64: sl * 512 + 128].bitcast(BF16),
                                            xhii[:, (2 * cp + 1) * 128:(2 * cp + 2) * 128], idb[:])
                    p.step("tensor", trh, reads=(("xhi", pr), "idb"), writes=(("tr", sl),))
                    p.step("scalar", lambda s, cp=cp, i=i, sl=sl: s.copy(
                        stg[:, (i % 2) * D + 2 * cp * 128:(i % 2) * D + (2 * cp + 2) * 128],
                        pool[:, sl * 512: sl * 512 + 128].bitcast(BF16)),
                        reads=(("tr", sl),), writes=(("sg", i % 2, cp),))

                    def trl(te, cp=cp, sl=sl, xloi=xloi):
                        te.transpose(pool[:, (2 + sl) * 512: (2 + sl) * 512 + 64].bitcast(BF16),
                                     xloi[:, 2 * cp * 128:(2 * cp + 1) * 128], idb[:])
                        return te.transpose(pool[:, (2 + sl) * 512 + 64: (2 + sl) * 512 + 128].bitcast(BF16),
                                            xloi[:, (2 * cp + 1) * 128:(2 * cp + 2) * 128], idb[:])
                    p.step("tensor", trl, reads=(("xlo", pr), "idb"), writes=(("trl", sl),))
                    p.step("scalar", lambda s, cp=cp, sl=sl, xlpi=xlpi: s.copy(
                        xlpi[:, 2 * cp * 128:(2 * cp + 2) * 128],
                        pool[:, (2 + sl) * 512: (2 + sl) * 512 + 128].bitcast(BF16)),
                        reads=(("trl", sl),), writes=(("lp", pr, cp),))
                p.step("sync", lambda e, i=i: e.dma_start(
                    out=xf_dr[:, :, i * 128:(i + 1) * 128].rearrange("c p t -> p c t"),
                    in_=stg[:, (i % 2) * D:(i % 2 + 1) * D].rearrange("p (c t) -> p c t", c=8)),
                    reads=tuple(("sg", i % 2, cp) for cp in range(4)), writes=(("spl", i),), inc=16)

                def mm_lg(te, i=i, lgb=lgb, xlpi=xlpi):
                    first = True
                    mm = None
                    for kc in range(8):
                        sgc = stg[:, (i % 2) * D + kc * 128:(i % 2) * D + (kc + 1) * 128]
                        mm = te.matmul(lgb, sgc, wrb[:, kc * 16:(kc + 1) * 16],
                                       start=first, stop=False); first = False
                        mm = te.matmul(lgb, sgc, wrl[:, kc * 16:(kc + 1) * 16],
                                       start=False, stop=False)
                        mm = te.matmul(lgb, xlpi[:, kc * 128:(kc + 1) * 128],
                                       wrb[:, kc * 16:(kc + 1) * 16], start=False, stop=(kc == 7))
                    return mm
                p.step("tensor", mm_lg, reads=tuple(("sg", i % 2, cp) for cp in range(4)) +
                       tuple(("lp", pr, cp) for cp in range(4)) + ("wr2",), writes=(("lg", pr),))
                p.step("vector", lambda v, rri=rri, lgb=lgb: _last(
                    v.reduce_max(rri[:, 0:1], lgb, AXX),
                    v.tensor_scalar(rri[:, 1:2], rri[:, 0:1], -1.0, None, ALU.mult)),
                    reads=(("lg", pr),), writes=(("rrm", pr),))
                p.step("scalar", lambda s, prbi=prbi, rri=rri, lgb=lgb: s.activation(
                    prbi, lgb, AF.Exp, bias=rri[:, 1:2]),
                    reads=(("lg", pr), ("rrm", pr)), writes=(("prb", pr),))
                p.step("vector", lambda v, prbi=prbi, f1i=f1i, w2i=w2i, sci=sci, cwbi=cwbi, rri=rri: _last(
                    v.reduce_sum(rri[:, 2:3], prbi, AXX),
                    v.reciprocal(rri[:, 3:4], rri[:, 2:3]),
                    v.tensor_scalar(prbi, prbi, rri[:, 3:4], None, ALU.mult),
                    v.reduce_max(rri[:, 0:1], prbi, AXX),
                    v.tensor_scalar(f1i, prbi, rri[:, 0:1], None, ALU.is_equal),
                    v.tensor_tensor_scan(sci, onesd[:], f1i, 0.0, ALU.mult, ALU.add),
                    v.tensor_tensor(sci, sci, f1i, ALU.subtract),
                    v.tensor_scalar(sci, sci, 0.0, None, ALU.is_equal),
                    v.tensor_tensor(f1i, f1i, sci, ALU.mult),
                    v.tensor_scalar(w2i, f1i, -1.0, 1.0, ALU.mult, ALU.add),
                    v.tensor_tensor(w2i, prbi, w2i, ALU.mult),
                    v.reduce_max(rri[:, 0:1], w2i, AXX),
                    v.tensor_scalar(w2i, w2i, rri[:, 0:1], None, ALU.is_equal),
                    v.tensor_tensor_scan(sci, onesd[:], w2i, 0.0, ALU.mult, ALU.add),
                    v.tensor_tensor(sci, sci, w2i, ALU.subtract),
                    v.tensor_scalar(sci, sci, 0.0, None, ALU.is_equal),
                    v.tensor_tensor(w2i, w2i, sci, ALU.mult),
                    v.tensor_tensor(f1i, f1i, w2i, ALU.add),
                    v.tensor_tensor(f1i, prbi, f1i, ALU.mult),
                    v.tensor_copy(cwbi, f1i)),
                    reads=(("prb", pr),), writes=(("cw", pr),))
                p.step("tensor", lambda te, cwbi=cwbi, ctb=ctb: te.transpose(
                    ctb.bitcast(BF16), cwbi, idb[:]),
                    reads=(("cw", pr), "idb"), writes=(("ct", pr),))
                p.step("vector", lambda v, i=i, ctb=ctb: v.tensor_copy(
                    cwT[:, i * 128:(i + 1) * 128], ctb.bitcast(BF16)),
                    reads=(("ct", pr),), writes=(("cwt", i),))
            p.emit(stP)

        with contextlib.ExitStack() as stE:
            s5 = lambda *a: stE.enter_context(nc.sbuf_tensor(*a))
            wge = s5("wge", [128, 2 * 8 * 512], BF16)
            wue = s5("wue", [128, 2 * 8 * 512], BF16)
            wde = s5("wde", [128, 2 * 4 * 1024], BF16)
            xfg = s5("xfg", [128, 2 * 8 * 512], BF16)
            gsil = s5("gsil", [128, 512], BF16)
            tmu = s5("tmu", [128, 512], BF16)
            h5 = s5("h55", [128, 4 * 512], BF16)
            p = Prog(nc, "p5b")
            p.step("vector", lambda v: v.memset(acc[:], 0.0), writes=("acc_all",))
            for e in range(16):
                esl = e % 2
                p.step("gpsimd", lambda g, e=e, esl=esl: g.dma_start(
                    out=wge[:, esl * 4096:(esl + 1) * 4096].rearrange("p (c j) -> p c j", c=8),
                    in_=Wg[e].rearrange("(c p) j -> p c j", p=128)),
                    writes=(("wg", esl),), inc=16)
                p.step("gpsimd", lambda g, e=e, esl=esl: g.dma_start(
                    out=wue[:, esl * 4096:(esl + 1) * 4096].rearrange("p (c j) -> p c j", c=8),
                    in_=Wu[e].rearrange("(c p) j -> p c j", p=128)),
                    writes=(("wu", esl),), inc=16)
                p.step("gpsimd", lambda g, e=e, esl=esl: g.dma_start(
                    out=wde[:, esl * 4096:(esl + 1) * 4096].rearrange("p (c j) -> p c j", c=4),
                    in_=Wd[e].rearrange("(c p) j -> p c j", p=128)),
                    writes=(("wd", esl),), inc=16)
                for t5 in range(8):
                    tsl = t5 % 2
                    p.step("sync", lambda e_, t5=t5, tsl=tsl: e_.dma_start(
                        out=xfg[:, tsl * 4096:(tsl + 1) * 4096].rearrange("p (c t) -> p c t", c=8),
                        in_=xf_dr[:, :, t5 * 512:(t5 + 1) * 512].rearrange("c p t -> p c t")),
                        writes=(("xg", tsl),), inc=16)
                    p.step("tensor", lambda te, e=e, t5=t5: te.matmul(
                        pool[:, 2560:3072], sel16b[:, e * 128:(e + 1) * 128], cwT[:, t5 * 512:(t5 + 1) * 512],
                        start=True, stop=True),
                        reads=tuple(("cwt", i) for i in range(32)) + ("sel16",), writes=(("cb",),))
                    for jt in range(4):
                        def mm_g(te, jt=jt, esl=esl, tsl=tsl, w=wge):
                            for kc in range(8):
                                mm = te.matmul(pool[:, (jt % 2) * 512:(jt % 2 + 1) * 512],
                                               w[:, esl * 4096 + kc * 512 + jt * 128: esl * 4096 + kc * 512 + jt * 128 + 128],
                                               xfg[:, tsl * 4096 + kc * 512: tsl * 4096 + (kc + 1) * 512],
                                               start=(kc == 0), stop=(kc == 7))
                            return mm
                        p.step("tensor", mm_g, reads=(("wg", esl), ("xg", tsl)), writes=(("gb", jt % 2),))
                        p.step("scalar", lambda s, jt=jt: s.activation(gsil[:], pool[:, (jt % 2) * 512:(jt % 2 + 1) * 512], AF.Silu),
                               reads=(("gb", jt % 2),), writes=("gsil",))
                        def mm_u(te, jt=jt, esl=esl, tsl=tsl, w=wue):
                            for kc in range(8):
                                mm = te.matmul(pool[:, 1024 + (jt % 2) * 512:1024 + (jt % 2 + 1) * 512],
                                               w[:, esl * 4096 + kc * 512 + jt * 128: esl * 4096 + kc * 512 + jt * 128 + 128],
                                               xfg[:, tsl * 4096 + kc * 512: tsl * 4096 + (kc + 1) * 512],
                                               start=(kc == 0), stop=(kc == 7))
                            return mm
                        p.step("tensor", mm_u, reads=(("wu", esl), ("xg", tsl)), writes=(("ub", jt % 2),))
                        p.step("vector", lambda v, jt=jt: v.tensor_tensor(
                            tmu[:], gsil[:], pool[:, 1024 + (jt % 2) * 512:1024 + (jt % 2 + 1) * 512], ALU.mult),
                            reads=("gsil", ("ub", jt % 2)), writes=("tmu",))
                        p.step("vector", lambda v, jt=jt: v.tensor_tensor(
                            h5[:, jt * 512:(jt + 1) * 512], tmu[:], pool[:, 2560:3072], ALU.mult),
                            reads=("tmu", ("cb",)), writes=(("h5", jt),))
                    for it4 in range(4):
                        for jh in range(2):
                            def mm_d(te, it4=it4, jh=jh, esl=esl):
                                for kc in range(4):
                                    mm = te.matmul(pool[:, 3072 + jh * 512:3072 + (jh + 1) * 512],
                                                   h5[:, kc * 512 + it4 * 128: kc * 512 + it4 * 128 + 128],
                                                   wde[:, esl * 4096 + kc * 1024 + jh * 512: esl * 4096 + kc * 1024 + jh * 512 + 512],
                                                   start=(kc == 0), stop=(kc == 3))
                                return mm
                            p.step("tensor", mm_d, reads=tuple(("h5", j) for j in range(4)) + (("wd", esl),),
                                   writes=(("db", jh),))
                            p.step("vector", lambda v, t5=t5, it4=it4, jh=jh: v.tensor_tensor(
                                acc[:, (t5 * 4 + it4) * 1024 + jh * 512:(t5 * 4 + it4) * 1024 + (jh + 1) * 512],
                                acc[:, (t5 * 4 + it4) * 1024 + jh * 512:(t5 * 4 + it4) * 1024 + (jh + 1) * 512],
                                pool[:, 3072 + jh * 512:3072 + (jh + 1) * 512], ALU.add),
                                reads=(("db", jh), "acc_all"), writes=(("acc", t5, it4, jh),))
            p.emit(stE)

        with contextlib.ExitStack() as stS:
            s5 = lambda *a: stS.enter_context(nc.sbuf_tensor(*a))
            sg_sb = s5("sg_sb", [128, 8 * 1024], BF16)
            su_sb = s5("su_sb", [128, 8 * 1024], BF16)
            sd_sb = s5("sd_sb", [128, 8 * 1024], BF16)
            xfg = s5("xfg2", [128, 2 * 8 * 512], BF16)
            gsil = s5("gsil2", [128, 2 * 512], BF16)
            tmu = s5("tmu2", [128, 512], BF16)
            hs = s5("hs5", [128, 2 * 8 * 512], BF16)
            xt = s5("xt5b", [128, 2 * D], F32)
            y2 = s5("y25", [128, 2 * D], F32)
            sqf = s5("sqf5", [128, 2 * D], F32)
            stt = s5("stt5", [128, 16], F32)
            p = Prog(nc, "p5c")
            for nm, wsb, wdr in (("sg", sg_sb, Sg), ("su", su_sb, Su), ("sd", sd_sb, Sd)):
                p.step("gpsimd", lambda g, wsb=wsb, wdr=wdr: g.dma_start(
                    out=wsb[:].rearrange("p (c j) -> p c j", c=8),
                    in_=wdr.rearrange("(c p) j -> p c j", p=128)),
                    writes=(nm,), inc=16)
            for t5 in range(8):
                tsl = t5 % 2
                p.step("sync", lambda e_, t5=t5, tsl=tsl: e_.dma_start(
                    out=xfg[:, tsl * 4096:(tsl + 1) * 4096].rearrange("p (c t) -> p c t", c=8),
                    in_=xf_dr[:, :, t5 * 512:(t5 + 1) * 512].rearrange("c p t -> p c t")),
                    writes=(("xg", tsl),), inc=16)
                for jt in range(8):
                    gsl = gsil[:, (jt % 2) * 512:(jt % 2 + 1) * 512]
                    def mm_g(te, jt=jt, tsl=tsl):
                        for kc in range(8):
                            mm = te.matmul(pool[:, (jt % 2) * 512:(jt % 2 + 1) * 512],
                                           sg_sb[:, kc * 1024 + jt * 128: kc * 1024 + jt * 128 + 128],
                                           xfg[:, tsl * 4096 + kc * 512: tsl * 4096 + (kc + 1) * 512],
                                           start=(kc == 0), stop=(kc == 7))
                        return mm
                    p.step("tensor", mm_g, reads=("sg", ("xg", tsl)), writes=(("gb", jt % 2),))
                    p.step("scalar", lambda s, jt=jt, gsl=gsl: s.activation(
                        gsl, pool[:, (jt % 2) * 512:(jt % 2 + 1) * 512], AF.Silu),
                        reads=(("gb", jt % 2),), writes=(("gsil", jt % 2),))
                    def mm_u(te, jt=jt, tsl=tsl):
                        for kc in range(8):
                            mm = te.matmul(pool[:, 1024 + (jt % 2) * 512:1024 + (jt % 2 + 1) * 512],
                                           su_sb[:, kc * 1024 + jt * 128: kc * 1024 + jt * 128 + 128],
                                           xfg[:, tsl * 4096 + kc * 512: tsl * 4096 + (kc + 1) * 512],
                                           start=(kc == 0), stop=(kc == 7))
                        return mm
                    p.step("tensor", mm_u, reads=("su", ("xg", tsl)), writes=(("ub", jt % 2),))
                    p.step("vector", lambda v, jt=jt, gsl=gsl, tsl=tsl: v.tensor_tensor(
                        hs[:, tsl * 4096 + jt * 512:tsl * 4096 + (jt + 1) * 512], gsl,
                        pool[:, 1024 + (jt % 2) * 512:1024 + (jt % 2 + 1) * 512], ALU.mult),
                        reads=(("gsil", jt % 2), ("ub", jt % 2)), writes=(("hs", tsl, jt),))
                for it4 in range(4):
                    i = t5 * 4 + it4
                    pr = it4 % 2
                    db0 = 4 + 2 * pr
                    xti = xt[:, pr * D:(pr + 1) * D]
                    y2i = y2[:, pr * D:(pr + 1) * D]
                    sqi = sqf[:, pr * D:(pr + 1) * D]
                    st8 = stt[:, pr * 8:pr * 8 + 8]
                    for jh in range(2):
                        def mm_d(te, it4=it4, jh=jh, db0=db0, tsl=tsl):
                            for kc in range(8):
                                mm = te.matmul(pool[:, (db0 + jh) * 512:(db0 + jh + 1) * 512],
                                               hs[:, tsl * 4096 + kc * 512 + it4 * 128:
                                                  tsl * 4096 + kc * 512 + it4 * 128 + 128],
                                               sd_sb[:, kc * 1024 + jh * 512: kc * 1024 + jh * 512 + 512],
                                               start=(kc == 0), stop=(kc == 7))
                            return mm
                        p.step("tensor", mm_d, reads=tuple(("hs", tsl, j) for j in range(8)) + ("sd",),
                               writes=(("db", pr, jh),))
                    p.step("sync", lambda e_, i=i, xti=xti: e_.dma_start(
                        out=xti, in_=x3i[i * 128:(i + 1) * 128, :]),
                        writes=(("xt", pr),), inc=16)
                    p.step("vector", lambda v, i=i, pr=pr, db0=db0, xti=xti, y2i=y2i: _last(
                        v.tensor_tensor(y2i[:, 0:512], pool[:, db0 * 512:(db0 + 1) * 512], xti[:, 0:512], ALU.add),
                        v.tensor_tensor(y2i[:, 512:1024], pool[:, (db0 + 1) * 512:(db0 + 2) * 512], xti[:, 512:1024], ALU.add),
                        v.tensor_tensor(y2i[:, 0:512], y2i[:, 0:512], acc[:, i * 1024: i * 1024 + 512], ALU.add),
                        v.tensor_tensor(y2i[:, 512:1024], y2i[:, 512:1024], acc[:, i * 1024 + 512:(i + 1) * 1024], ALU.add)),
                        reads=(("db", pr, 0), ("db", pr, 1), ("xt", pr)), writes=(("y2", pr),))
                    p.step("vector", lambda v, y2i=y2i, sqi=sqi, st8=st8: _last(
                        v.tensor_tensor(sqi, y2i, y2i, ALU.mult),
                        v.reduce_sum(st8[:, 1:2], sqi, AXX),
                        v.reduce_sum(st8[:, 0:1], y2i, AXX),
                        v.tensor_scalar(st8[:, 2:3], st8[:, 0:1], 1.0 / D, None, ALU.mult),
                        v.tensor_tensor(st8[:, 3:4], st8[:, 2:3], st8[:, 2:3], ALU.mult),
                        v.scalar_tensor_tensor(st8[:, 3:4], st8[:, 1:2], 1.0 / D, st8[:, 3:4], ALU.mult, ALU.subtract),
                        v.tensor_scalar(st8[:, 3:4], st8[:, 3:4], LN_EPS, None, ALU.add)),
                        reads=(("y2", pr),), writes=(("st", pr), ("sqf", pr)))
                    p.step("scalar", lambda s, st8=st8: s.activation(st8[:, 6:7], st8[:, 3:4], AF.Sqrt),
                           reads=(("st", pr),), writes=(("stb", pr),))
                    p.step("vector", lambda v, y2i=y2i, sqi=sqi, st8=st8: _last(
                        v.reciprocal(st8[:, 4:5], st8[:, 6:7]),
                        v.tensor_scalar(st8[:, 5:6], st8[:, 2:3], st8[:, 4:5], -1.0, ALU.mult, ALU.mult),
                        v.tensor_scalar(sqi, y2i, st8[:, 4:5], st8[:, 5:6], ALU.mult, ALU.add)),
                        reads=(("st", pr), ("stb", pr), ("y2", pr)), writes=(("xot", pr), ("sqf", pr)))
                    p.step("sync", lambda e_, i=i, sqi=sqi: e_.dma_start(
                        out=outo[i * 128:(i + 1) * 128, :], in_=sqi),
                        reads=(("xot", pr),), writes=(("od", i),), inc=16)
            p.step("gpsimd", lambda g: g.nop(), reads=tuple(("od", i) for i in range(32)))
            p.emit(stS)


class _NullBlk:
    def gpsimd(self, f):
        pass

    def sync(self, f):
        pass

    def vector(self, f):
        pass

    def scalar(self, f):
        pass

    def tensor(self, f):
        pass


def build_full():
    nc = bass.Bass("TRN2", target_bir_lowering=False)
    io = {}
    for nm, shape in (("xo", [TOK, D]), ("xp", [TOK, D]), ("wqT", [D, D]), ("wkT", [D, D]),
                      ("wvT", [D, D]), ("woT", [D, D]), ("lng", [D]), ("lnb", [D]),
                      ("qg", [DH]), ("kg", [DH]), ("bo", [D]), ("rtok", [D]),
                      ("identf", [128, 128]), ("sel2c", [2, 128]), ("b2c", [128, 2]),
                      ("bsel", [128, 4]), ("wsel", [64, 8]), ("sel16c", [16, 2048]),
                      ("trilc", [128, 128])):
        io[nm] = nc.dram_tensor(nm, shape, F32, kind="ExternalInput").ap()
    for nm, shape in P2_W:
        io[nm] = nc.dram_tensor(nm, shape, F32, kind="ExternalInput").ap()
    for nm in ("WqT", "WkT", "WvT", "WoT"):
        io[nm] = nc.dram_tensor(nm, [D, D], F32, kind="ExternalInput").ap()
    for nm in ("bq", "bk", "bv", "bo4"):
        io[nm] = nc.dram_tensor(nm, [D], F32, kind="ExternalInput").ap()
    io["Wr"] = nc.dram_tensor("Wr", [D, 16], F32, kind="ExternalInput").ap()
    io["Wg"] = nc.dram_tensor("Wg", [16, D, 512], F32, kind="ExternalInput").ap()
    io["Wu"] = nc.dram_tensor("Wu", [16, D, 512], F32, kind="ExternalInput").ap()
    io["Wd"] = nc.dram_tensor("Wd", [16, 512, D], F32, kind="ExternalInput").ap()
    io["Sg"] = nc.dram_tensor("Sg", [D, D], F32, kind="ExternalInput").ap()
    io["Su"] = nc.dram_tensor("Su", [D, D], F32, kind="ExternalInput").ap()
    io["Sd"] = nc.dram_tensor("Sd", [D, D], F32, kind="ExternalInput").ap()
    x1_dr = nc.dram_tensor("x1_dr", [TOK, D], F32).ap()
    s_own = nc.dram_tensor("s_own", [TOK], F32).ap()
    ag_s = nc.dram_tensor("ag_s", [4, S], F32).ap()
    sfull_dr = nc.dram_tensor("sfull_dr", [S], F32).ap()
    x2_dr = nc.dram_tensor("x2_dr", [TOK, D], F32).ap()
    qf3 = nc.dram_tensor("qf3", [16, 64, TOK], BF16).ap()
    kf3 = nc.dram_tensor("kf3", [16, 64, TOK], BF16).ap()
    kt3 = nc.dram_tensor("kt3", [TOK, D], BF16).ap()
    vt3 = nc.dram_tensor("vt3", [32, 128, 1040], BF16).ap()
    st3 = nc.dram_tensor("st3", [64, 1040], F32).ap()
    ag_st = nc.dram_tensor("ag_st", [8, 64, 1040], F32).ap()
    stprev_dr = nc.dram_tensor("stprev_dr", [64, 1040], F32).ap()
    x3_dr = nc.dram_tensor("x3_dr", [TOK, D], F32).ap()
    io["outo"] = nc.dram_tensor("outo", [TOK, D], F32, kind="ExternalOutput").ap()

    _FP = os.environ.get("FULL_PHASES", "1,ag1,2,3,ag2,4,5").split(",")
    io1 = dict(io)
    io1["x1o"] = x1_dr
    io1["so"] = s_own
    if "1" in _FP:
        emit_p1(nc, io1)

    with contextlib.ExitStack() as stG:
        s4 = stG.enter_context(nc.sbuf_tensor("s4g", [128, 4 * 64], F32))
        bsl = stG.enter_context(nc.sbuf_tensor("bslg", [128, 4], F32))
        ssel = stG.enter_context(nc.sbuf_tensor("sselg", [128, 64], F32))
        blk = stG.enter_context(nc.Block())
        sem = stG.enter_context(nc.semaphore("ags_sem"))
        csem = stG.enter_context(nc.semaphore("agc_sem"))
        if "ag1" not in _FP:
            blk = _NullBlk()

        @blk.gpsimd
        def _(g):
            g.collective_compute("AllGather", ALU.bypass, [[0, 1, 2, 3, 4, 5, 6, 7]],
                                 [s_own.rearrange("(a b) -> a b", b=128)],
                                 [ag_s.rearrange("b (a c) -> (b a) c", c=128)]).then_inc(csem, 1)

        @blk.sync
        def _(sy):
            sy.wait_ge(csem, 1)
            for b_ in range(4):
                sy.dma_start(out=s4[:, b_ * 64:(b_ + 1) * 64],
                             in_=ag_s[b_].rearrange("(p f) -> p f", p=128)).then_inc(sem, 16)
            sy.dma_start(out=bsl[:], in_=io["bsel"]).then_inc(sem, 16)
            sy.wait_ge(sem, 80)

        @blk.vector
        def _(v):
            v.wait_ge(sem, 80)
            v.tensor_scalar(ssel[:], s4[:, 0:64], bsl[:, 0:1], None, ALU.mult).then_inc(sem, 1)
            v.wait_ge(sem, 81)
            for b_ in range(1, 4):
                v.scalar_tensor_tensor(ssel[:], s4[:, b_ * 64:(b_ + 1) * 64], bsl[:, b_:b_ + 1],
                                       ssel[:], ALU.mult, ALU.add).then_inc(sem, 1)
                v.wait_ge(sem, 81 + b_)

        @blk.gpsimd
        def _(g):
            g.wait_ge(sem, 84)
            g.dma_start(out=sfull_dr.rearrange("(p f) -> p f", p=128), in_=ssel[:]).then_inc(sem, 16)
            g.wait_ge(sem, 100)

    io2 = dict(io)
    io2["x1i"] = x1_dr
    io2["sfull"] = sfull_dr
    io2["x2o"] = x2_dr
    if "2" in _FP:
        emit_p2(nc, io2)

    io3 = dict(io)
    io3["x2i"] = x2_dr
    io3["qf_dr"], io3["kf_dr"], io3["kt_dr"], io3["vt_dr"], io3["state_o"] = qf3, kf3, kt3, vt3, st3
    if "3" in _FP:
        emit_p3(nc, io3)

    with contextlib.ExitStack() as stG2:
        sall = stG2.enter_context(nc.sbuf_tensor("sallg", [64, 8 * 1040], F32))
        wsl = stG2.enter_context(nc.sbuf_tensor("wslg", [64, 8], F32))
        sout = stG2.enter_context(nc.sbuf_tensor("soutg", [64, 1040], F32))
        blk = stG2.enter_context(nc.Block())
        sem = stG2.enter_context(nc.semaphore("ag2_sem"))
        csem = stG2.enter_context(nc.semaphore("ag2c_sem"))
        if "ag2" not in _FP:
            blk = _NullBlk()

        @blk.gpsimd
        def _(g):
            g.collective_compute("AllGather", ALU.bypass, [[0, 1, 2, 3, 4, 5, 6, 7]],
                                 [st3.rearrange("a b -> a b")],
                                 [ag_st.rearrange("r a b -> (r a) b")]).then_inc(csem, 1)

        @blk.sync
        def _(sy):
            sy.wait_ge(csem, 1)
            sy.dma_start(out=sall[:].rearrange("p (r t) -> p r t", r=8),
                         in_=ag_st.rearrange("r a b -> a r b")).then_inc(sem, 16)
            sy.dma_start(out=wsl[:], in_=io["wsel"]).then_inc(sem, 16)
            sy.wait_ge(sem, 32)

        @blk.vector
        def _(v):
            v.wait_ge(sem, 32)
            v.memset(sout[:], 0.0).then_inc(sem, 1)
            v.wait_ge(sem, 33)
            for r in range(8):
                v.scalar_tensor_tensor(sout[:], sall[:, r * 1040:(r + 1) * 1040], wsl[:, r:r + 1],
                                       sout[:], ALU.mult, ALU.add).then_inc(sem, 1)
                v.wait_ge(sem, 34 + r)

        @blk.gpsimd
        def _(g):
            g.wait_ge(sem, 41)
            g.dma_start(out=stprev_dr, in_=sout[:]).then_inc(sem, 16)
            g.wait_ge(sem, 57)

    io4 = dict(io)
    io4["x2i"] = x2_dr
    io4["qf_dr"], io4["kf_dr"], io4["kt_dr"], io4["vt_dr"] = qf3, kf3, kt3, vt3
    io4["state_prev"] = stprev_dr
    io4["x3o"] = x3_dr
    if "4" in _FP:
        emit_p4(nc, io4)

    io5 = dict(io)
    io5["x3i"] = x3_dr
    if "5" in _FP:
        emit_p5(nc, io5)
    return nc


def _full_inmaps(inputs):
    u64, l128, ones = _tri_consts()
    ident = _ident()
    sel16 = _sel16()
    tril = np.triu(np.ones((128, 128), np.float32))
    x = np.asarray(inputs["L0_attn_x"])
    maps = []
    for core in range(8):
        b, h = core // 2, core % 2
        ohv = np.zeros((1, 2), np.float32); ohv[0, h] = 1.0
        bsel = np.zeros((128, 4), np.float32); bsel[:, b] = 1.0
        wsel = np.zeros((64, 8), np.float32)
        if h == 1:
            wsel[:, 2 * b] = 1.0
        maps.append({
            "xo": np.ascontiguousarray(x[b, h * TOK:(h + 1) * TOK, :]),
            "xp": np.ascontiguousarray(x[b, (1 - h) * TOK:(2 - h) * TOK, :]),
            "wqT": np.ascontiguousarray(np.asarray(inputs["L0_attn_w_q"]).T),
            "wkT": np.ascontiguousarray(np.asarray(inputs["L0_attn_w_k"]).T),
            "wvT": np.ascontiguousarray(np.asarray(inputs["L0_attn_w_v"]).T),
            "woT": np.ascontiguousarray(np.asarray(inputs["L0_attn_w_o"]).T),
            "lng": np.asarray(inputs["L0_attn_ln_gamma"]),
            "lnb": np.asarray(inputs["L0_attn_ln_beta"]),
            "qg": np.asarray(inputs["L0_attn_q_gamma"]),
            "kg": np.asarray(inputs["L0_attn_k_gamma"]),
            "bo": np.asarray(inputs["L0_attn_b_o"]),
            "rtok": np.asarray(inputs["L0_ffn_routing_token"]),
            "identf": ident, "sel2c": _sel2c(), "b2c": _b2c(),
            "bsel": bsel, "wsel": wsel, "sel16c": sel16, "trilc": tril, "oh": ohv,
            "w1l": np.asarray(inputs["L0_ffn_w1_light"]), "b1l": np.asarray(inputs["L0_ffn_b1_light"]),
            "w2l": np.asarray(inputs["L0_ffn_w2_light"]), "b2l": np.asarray(inputs["L0_ffn_b2_light"]),
            "gl": np.asarray(inputs["L0_ffn_gamma_light"]),
            "w1h": np.asarray(inputs["L0_ffn_w1_heavy"]), "b1h": np.asarray(inputs["L0_ffn_b1_heavy"]),
            "w2h": np.asarray(inputs["L0_ffn_w2_heavy"]), "b2h": np.asarray(inputs["L0_ffn_b2_heavy"]),
            "gh": np.asarray(inputs["L0_ffn_gamma_heavy"]),
            "u64c": u64, "l128c": l128, "onesc": ones,
            "WqT": np.ascontiguousarray(np.asarray(inputs["L1_attn_Wq"]).T),
            "WkT": np.ascontiguousarray(np.asarray(inputs["L1_attn_Wk"]).T),
            "WvT": np.ascontiguousarray(np.asarray(inputs["L1_attn_Wv"]).T),
            "WoT": np.ascontiguousarray(np.asarray(inputs["L1_attn_Wo"]).T),
            "bq": np.asarray(inputs["L1_attn_bq"]), "bk": np.asarray(inputs["L1_attn_bk"]),
            "bv": np.asarray(inputs["L1_attn_bv"]), "bo4": np.asarray(inputs["L1_attn_bo"]),
            "Wr": np.asarray(inputs["L1_ffn_Wr"]),
            "Wg": np.asarray(inputs["L1_ffn_Wg"]), "Wu": np.asarray(inputs["L1_ffn_Wu"]),
            "Wd": np.asarray(inputs["L1_ffn_Wd"]),
            "Sg": np.asarray(inputs["L1_ffn_Sg"]), "Su": np.asarray(inputs["L1_ffn_Su"]),
            "Sd": np.asarray(inputs["L1_ffn_Sd"]),
        })
    return maps


def _tri_consts():
    u64 = np.triu(np.ones((64, 64), np.float32))
    l128 = np.triu(np.ones((128, 128), np.float32), 1)
    ones = np.ones((128, 128), np.float32)
    return u64, l128, ones


def _p2_inmaps(inputs, p1res):
    u64, l128, ones = _tri_consts()
    ident = _ident()
    maps = []
    for core in range(8):
        b, h = core // 2, core % 2
        sf = np.concatenate([np.asarray(p1res[2 * b]["so"]), np.asarray(p1res[2 * b + 1]["so"])])
        ohv = np.zeros((1, 2), np.float32)
        ohv[0, h] = 1.0
        maps.append({
            "x1i": np.asarray(p1res[core]["x1o"]),
            "sfull": sf,
            "oh": ohv,
            "w1l": np.asarray(inputs["L0_ffn_w1_light"]),
            "b1l": np.asarray(inputs["L0_ffn_b1_light"]),
            "w2l": np.asarray(inputs["L0_ffn_w2_light"]),
            "b2l": np.asarray(inputs["L0_ffn_b2_light"]),
            "gl": np.asarray(inputs["L0_ffn_gamma_light"]),
            "w1h": np.asarray(inputs["L0_ffn_w1_heavy"]),
            "b1h": np.asarray(inputs["L0_ffn_b1_heavy"]),
            "w2h": np.asarray(inputs["L0_ffn_w2_heavy"]),
            "b2h": np.asarray(inputs["L0_ffn_b2_heavy"]),
            "gh": np.asarray(inputs["L0_ffn_gamma_heavy"]),
            "identf": ident, "u64c": u64, "l128c": l128, "onesc": ones,
        })
    return maps


def run_p2(inputs, p1res):
    if "p2" not in _CACHE:
        _CACHE["p2"] = build_p2()
    res = run_bass_kernel_spmd(_CACHE["p2"], _p2_inmaps(inputs, p1res), core_ids=list(range(8)))
    return res.results


def _p3_inmaps(inputs, p2res):
    ident = _ident()
    maps = []
    for core in range(8):
        maps.append({
            "x2i": np.asarray(p2res[core]["x2o"]),
            "WqT": np.ascontiguousarray(np.asarray(inputs["L1_attn_Wq"]).T),
            "WkT": np.ascontiguousarray(np.asarray(inputs["L1_attn_Wk"]).T),
            "WvT": np.ascontiguousarray(np.asarray(inputs["L1_attn_Wv"]).T),
            "bq": np.asarray(inputs["L1_attn_bq"]),
            "bk": np.asarray(inputs["L1_attn_bk"]),
            "bv": np.asarray(inputs["L1_attn_bv"]),
            "identf": ident,
        })
    return maps


def _p4_inmaps(inputs, p2res, p3res):
    ident = _ident()
    tril = np.triu(np.ones((128, 128), np.float32))
    maps = []
    for core in range(8):
        b, h = core // 2, core % 2
        state_prev = (np.asarray(p3res[2 * b]["state_o"]) if h == 1
                      else np.zeros((64, 1040), np.float32))
        maps.append({
            "x2i": np.asarray(p2res[core]["x2o"]),
            "qf_dr": np.asarray(p3res[core]["qf_dr"]),
            "kf_dr": np.asarray(p3res[core]["kf_dr"]),
            "kt_dr": np.asarray(p3res[core]["kt_dr"]),
            "vt_dr": np.asarray(p3res[core]["vt_dr"]),
            "state_prev": state_prev,
            "WoT": np.ascontiguousarray(np.asarray(inputs["L1_attn_Wo"]).T),
            "bo4": np.asarray(inputs["L1_attn_bo"]),
            "identf": ident, "trilc": tril,
        })
    return maps


def _sel16():
    m = np.zeros((16, 2048), np.float32)
    for e in range(16):
        m[e, e * 128:(e + 1) * 128] = 1.0
    return m


def _p5_inmaps(inputs, p4res):
    ident = _ident()
    sel = _sel16()
    maps = []
    for core in range(8):
        maps.append({
            "x3i": np.asarray(p4res[core]["x3o"]),
            "Wr": np.asarray(inputs["L1_ffn_Wr"]),
            "Wg": np.asarray(inputs["L1_ffn_Wg"]),
            "Wu": np.asarray(inputs["L1_ffn_Wu"]),
            "Wd": np.asarray(inputs["L1_ffn_Wd"]),
            "Sg": np.asarray(inputs["L1_ffn_Sg"]),
            "Su": np.asarray(inputs["L1_ffn_Su"]),
            "Sd": np.asarray(inputs["L1_ffn_Sd"]),
            "identf": ident, "sel16c": sel,
        })
    return maps


def run_p3(inputs, p2res):
    if "p3" not in _CACHE:
        _CACHE["p3"] = build_p3()
    return run_bass_kernel_spmd(_CACHE["p3"], _p3_inmaps(inputs, p2res), core_ids=list(range(8))).results


def run_p4(inputs, p2res, p3res):
    if "p4" not in _CACHE:
        _CACHE["p4"] = build_p4()
    return run_bass_kernel_spmd(_CACHE["p4"], _p4_inmaps(inputs, p2res, p3res), core_ids=list(range(8))).results


def run_p5(inputs, p4res):
    if "p5" not in _CACHE:
        _CACHE["p5"] = build_p5()
    return run_bass_kernel_spmd(_CACHE["p5"], _p5_inmaps(inputs, p4res), core_ids=list(range(8))).results


def kernel(**inputs):
    if "full" not in _CACHE:
        _CACHE["full"] = build_full()
    res = run_bass_kernel_spmd(_CACHE["full"], _full_inmaps(inputs), core_ids=list(range(8)))
    out = np.empty((B, S, D), np.float32)
    for core in range(8):
        b, h = core // 2, core % 2
        out[b, h * TOK:(h + 1) * TOK, :] = np.asarray(res.results[core]["outo"])
    return out


def kernel_unfused(**inputs):
    p1 = run_p1(inputs)
    p2 = run_p2(inputs, p1)
    p3 = run_p3(inputs, p2)
    p4 = run_p4(inputs, p2, p3)
    p5 = run_p5(inputs, p4)
    out = np.empty((B, S, D), np.float32)
    for core in range(8):
        b, h = core // 2, core % 2
        out[b, h * TOK:(h + 1) * TOK, :] = np.asarray(p5[core]["outo"])
    return out


if __name__ == "__main__":
    nc = build_p1()
    n = sum(1 for _ in nc.all_instructions())
    print("P1 built ok; instructions:", n)
